# version 09
# baseline (speedup 1.0000x reference)
import sys
if "/opt/trn_rl_repo" not in sys.path:
    sys.path.insert(0, "/opt/trn_rl_repo")
from collections import defaultdict
from contextlib import ExitStack

import numpy as np
import ml_dtypes
import concourse.bass as bass
import concourse.mybir as mybir
from concourse.bass_utils import run_bass_kernel_spmd

F32 = mybir.dt.float32
BF16 = mybir.dt.bfloat16
AF = mybir.ActivationFunctionType
ALU = mybir.AluOpType
AX = mybir.AxisListType

ENGINES = ("sync", "gpsimd", "tensor", "vector", "scalar")

S, D, E, FF = 2048, 1024, 64, 512
TT = S // 128
DC = D // 128
FC = FF // 128
NH = 2
TH = S // NH
TTH = TH // 128
NEG_BIG = -1e30
NEG_THR = -1e29


class Prog:
    def __init__(self):
        self.ops = {e: [] for e in ENGINES}
        self.sem_names = []
        self.counts = defaultdict(int)

    def sem(self, name):
        if name not in self.sem_names:
            self.sem_names.append(name)
        return name

    PARANOID = False

    def emit(self, engine, fn, deps=(), inc=None, amt=1):
        tok = None
        deps = [d for d in deps if d is not None]
        if Prog.PARANOID:
            deps = [(s, v) for s, v in self.counts.items()]
        if inc is not None:
            self.sem(inc)
            if amt == 16 and self.counts[inc] > 0:
                deps.append((inc, self.counts[inc]))
            self.counts[inc] += amt
            tok = (inc, self.counts[inc])
        self.ops[engine].append((fn, deps, inc, amt))
        return tok

    def final_deps(self):
        return [(s, self.counts[s]) for s in self.sem_names]

    def build(self, block, sems):
        for e in ENGINES:
            if not self.ops[e]:
                continue

            def make(e):
                def f(eng):
                    hw = defaultdict(int)
                    for fn, deps, inc, amt in self.ops[e]:
                        for (s, v) in deps:
                            if v > hw[s]:
                                eng.wait_ge(sems[s], v)
                                hw[s] = v
                        inst = fn(eng)
                        if inc is not None:
                            if inst is None:
                                inst = eng.nop()
                            inst.then_inc(sems[inc], amt)
                return f

            getattr(block, e)(make(e))


def build_nc(phase_limit=99, taps=()):
    nc = bass.Bass("TRN2", target_bir_lowering=False)
    x_h = nc.dram_tensor("x", [S, D], F32, kind="ExternalInput").ap()
    wk_h = nc.dram_tensor("wk", [D, D], F32, kind="ExternalInput").ap()
    wq_h = nc.dram_tensor("wq", [D, D], F32, kind="ExternalInput").ap()
    wv_h = nc.dram_tensor("wv", [D, D], F32, kind="ExternalInput").ap()
    wa_h = nc.dram_tensor("wa", [D, D], F32, kind="ExternalInput").ap()
    wb_h = nc.dram_tensor("wb", [D, D], F32, kind="ExternalInput").ap()
    cent_h = nc.dram_tensor("cent", [E, D], F32, kind="ExternalInput").ap()
    ebias_h = nc.dram_tensor("ebias", [1, E], F32, kind="ExternalInput").ap()
    wsup_h = nc.dram_tensor("wsup", [D, FF], F32, kind="ExternalInput").ap()
    wsdn_h = nc.dram_tensor("wsdn", [FF, D], F32, kind="ExternalInput").ap()
    wup_h = nc.dram_tensor("wup", [E, D, FF], F32, kind="ExternalInput").ap()
    wdn_h = nc.dram_tensor("wdn", [E, FF, D], F32, kind="ExternalInput").ap()
    y_h = nc.dram_tensor("y", [S, D], F32, kind="ExternalOutput").ap()

    ident_hbm = nc.inline_tensor(np.eye(128, dtype=ml_dtypes.bfloat16), name="identc").ap()
    ones_hbm = nc.inline_tensor(np.ones((1, 128), dtype=np.float32), name="onesc").ap()

    ARENA_BYTES = 196608
    arena = nc.alloc_sbuf_tensor("ARENA", [128, ARENA_BYTES], mybir.dt.uint8)
    SBUF_BASE = nc.lookup_mloc(arena).addr

    def at(name, off, shape, dt):
        assert 0 <= off and off < ARENA_BYTES
        return nc.alloc_sbuf_tensor_at(name, shape, dt, offset=SBUF_BASE + off).ap()

    xnT = at("xnT", 0, [128, DC, S], BF16)
    cw = at("cw", 32768, [128, TT * E], F32)
    co = 36864
    ident = at("ident", co + 0, [128, 128], BF16); co += 256
    centT = at("centT", co, [128, DC, E], BF16); co += 1024
    bias_sb = at("bias_sb", co, [1, E], F32); co += 256
    bias_bc = at("bias_bc", co, [128, E], F32); co += 256
    ones_sb = at("ones_sb", co, [1, 128], F32); co += 512
    aff_sb = at("aff_sb", co, [128, E], F32); co += 256
    biased = at("biased", co, [128, E], F32); co += 256
    max8 = at("max8", co, [128, 8], F32); co += 32
    zap = at("zap", co, [128, E], F32); co += 256
    mask = at("mask", co, [128, E], F32); co += 256
    sig = at("sig", co, [128, E], F32); co += 256
    wm = at("wm", co, [128, E], F32); co += 256
    st6 = at("st6", co, [128, 12], F32); co += 64
    mv2 = at("mv2", co, [128, 2 * TT], F32); co += 128
    nmp = at("nmp", co, [128, 4], F32); co += 32
    epsv = at("epsv", co, [128, 1], F32); co += 32
    rcp16 = at("rcp16", co, [128, TT], F32); co += 64
    rstd16 = at("rstd16", co, [128, TT], F32); co += 64
    negm16 = at("negm16", co, [128, TT], F32); co += 64
    rowsum16 = at("rowsum16", co, [128, TT], F32); co += 64
    recip16 = at("recip16", co, [128, TT], F32); co += 64
    den1 = at("den1", co, [128, 1], F32); co += 64
    rd1 = at("rd1", co, [128, 1], F32); co += 64
    assert co <= 45056, co
    u = at("u", 45056, [128, TT, D], BF16)
    Wu_b = [at(f"Wu{i}", 45056 + i * 8192, [128, DC, FF], BF16) for i in range(2)]
    Wd_b = [at(f"Wd{i}", 45056 + 16384 + i * 8192, [128, FC, D], BF16) for i in range(2)]
    vT = at("vT", 77824, [128, DC, S], BF16)
    qT = at("qT", 77824, [128, DC, S], BF16)
    kT = at("kT", 77824, [128, DC, S], BF16)
    acc = at("acc", 77824, [128, TTH, D], F32)
    WT = [at(f"WT{i}", 110592 + i * 16384, [128, DC, D], BF16) for i in range(2)]
    s_sb = at("s_sb", 110592, [128, TT, D], BF16)
    pT = at("pT", 143360, [128, DC, S], BF16)
    Hbuf = at("Hbuf", 143360, [128, FC, TH], BF16)
    Wsup = at("Wsup", 143360 + 8192, [128, DC, FF], BF16)
    Wsdn = at("Wsdn", 143360 + 16384, [128, FC, D], BF16)
    gbuf = at("gbuf", 143360 + 24576, [128, 4, D], BF16)
    xstage = [at(f"xstage{i}", 176128 + i * 4096, [128, D], F32) for i in range(2)]
    wstage = [at(f"wstage{i}", 192512 + i * 2048, [128, D], BF16) for i in range(2)]
    alpha_sb = at("alpha_sb", 176128, [128, S], BF16)
    alphaT_sb = at("alphaT_sb", 176128 + 4096, [128, S], BF16)
    xn_t2 = [at(f"xn_t{i}", 184320 + i * 2048, [128, D], BF16) for i in range(2)]
    sqf = at("sqf", 184320 + 4096, [128, D], F32)
    ostage = [at(f"ostage{i}", 184320 + i * 4096, [128, D], F32) for i in range(3)]

    PE = nc.alloc_psum_tensor("PE", [128, 2048], F32).ap()
    PS = nc.alloc_psum_tensor("PS", [128, 1024], F32).ap()
    PT = [nc.alloc_psum_tensor(f"PT{i}", [128, 1024], BF16).ap() for i in range(2)]

    P = Prog()
    Em = P.emit

    LN_ONLY = (phase_limit == -1)
    if not LN_ONLY:
        t_id = Em("sync", lambda s: s.dma_start(out=ident[:], in_=ident_hbm), inc="dx", amt=16)
        t_ones = Em("sync", lambda s: s.dma_start(out=ones_sb[:], in_=ones_hbm), inc="dx", amt=16)
        t_bias = Em("sync", lambda s: s.dma_start(out=bias_sb[:], in_=ebias_h), inc="dx", amt=16)
    else:
        t_id = t_ones = t_bias = None

    t_eps = Em("vector", lambda v: v.memset(epsv[:], 1e-5), inc="v")
    xs_free = [None, None]
    pt_free = [None, None]
    xn_used = [None, None]
    wst_free = [None, None]
    pe_free = [None] * 4
    sq_done = None
    xnT_tok = None
    xnT_tile_toks = [None] * TT
    wv_tok = [None] * DC
    vT_ev = [[None] * 4 for _ in range(DC)]
    vT_emitted = set()
    vg = [0]

    def emit_vT_group(oc, tb):
        bank = vg[0] % 4
        vg[0] += 1
        t_mm = None
        for dc in range(DC):
            t_mm = Em("tensor", lambda t, oc=oc, tb=tb, dc=dc, bank=bank: t.matmul(
                PE[:, bank * 512:(bank + 1) * 512],
                WT[0][:, dc, oc * 128:(oc + 1) * 128],
                xnT[:, dc, tb * 512:(tb + 1) * 512],
                start=(dc == 0), stop=(dc == DC - 1)),
                deps=[wv_tok[oc], xnT_tile_toks[4 * tb + 3], pe_free[bank]],
                inc=("t" if dc == DC - 1 else None))
        t_ev = Em("scalar", lambda sc, oc=oc, tb=tb, bank=bank: sc.copy(
            vT[:, oc, tb * 512:(tb + 1) * 512], PE[:, bank * 512:(bank + 1) * 512]),
            deps=[t_mm], inc="s")
        pe_free[bank] = t_ev
        vT_ev[oc][tb] = t_ev
        vT_emitted.add((oc, tb))

    def emit_ready_vT(tt):
        for tb in range(4):
            if 4 * tb + 3 > tt:
                break
            for oc in range(DC):
                if 2 * oc + 1 <= tt and (oc, tb) not in vT_emitted:
                    emit_vT_group(oc, tb)

    xp = [at(f"xp{i}", 45056 + i * 8192, [128, 2, D], F32) for i in range(2)]
    for tt in range(TT):
        b = (tt // 2) % 2
        if tt % 2 == 0:
            q = tt // 2
            t_dx = Em("sync", lambda s, q=q, b=b: s.dma_start(
                out=xp[b][:], in_=x_h[q * 256:(q + 1) * 256, :].rearrange(
                    "(c p) d -> p c d", p=128)),
                deps=[xs_free[b]], inc="dx", amt=16)
        xsrc = xp[b][:, tt % 2, :]
        t_b1 = Em("vector", lambda v, xsrc=xsrc: v.bn_stats(
            st6[:, 0:6], xsrc[:, 0:512]), deps=[t_dx, sq_done], inc="v")
        t_b2 = Em("vector", lambda v, xsrc=xsrc: v.bn_stats(
            st6[:, 6:12], xsrc[:, 512:1024]), deps=[t_b1], inc="v")
        t_ag = Em("vector", lambda v, tt=tt: v.bn_aggr(
            mv2[:, 2 * tt:2 * tt + 2], st6[:]), deps=[t_b2], inc="v")
        sq_done = t_ag
        t_e3 = Em("scalar", lambda sc, tt=tt: sc.activation(
            rcp16[:, tt:tt + 1], mv2[:, 2 * tt + 1:2 * tt + 2], AF.Ln, bias=epsv[:]),
            deps=[t_ag, t_eps], inc="s")
        t_e4 = Em("scalar", lambda sc, tt=tt: sc.activation(
            rstd16[:, tt:tt + 1], rcp16[:, tt:tt + 1], AF.Exp, scale=-0.5),
            deps=[t_e3], inc="s")
        if LN_ONLY:
            xs_free[b] = t_ag
            continue
        xn_t = xn_t2[tt % 2]
        t_f = Em("vector", lambda v, tt=tt, xsrc=xsrc, xn_t=xn_t: v.tensor_scalar(
            xn_t[:], xsrc[:], mv2[:, 2 * tt:2 * tt + 1], rstd16[:, tt:tt + 1],
            ALU.subtract, ALU.mult), deps=[t_e4, t_ag, xn_used[tt % 2]], inc="v")
        xs_free[b] = t_f
        t_tr = None
        for dc in range(DC):
            t_tr = Em("tensor", lambda t, dc=dc, xn_t=xn_t: t.transpose(
                PT[0][:, dc * 128:(dc + 1) * 128], xn_t[:, dc * 128:(dc + 1) * 128],
                ident[:]), deps=[t_f, t_id, pt_free[0]],
                inc=("t" if dc == DC - 1 else None))
        xn_used[tt % 2] = t_tr
        for dc in range(DC):
            xnT_tok = Em("scalar", lambda sc, tt=tt, dc=dc: sc.copy(
                xnT[:, dc, tt * 128:(tt + 1) * 128],
                PT[0][:, dc * 128:(dc + 1) * 128]), deps=[t_tr], inc="s")
        pt_free[0] = xnT_tok
        xnT_tile_toks[tt] = xnT_tok
        if tt % 2 == 1:
            rc = (tt - 1) // 2
            wb_ = rc % 2
            t_w = Em("gpsimd", lambda g, rc=rc, wb_=wb_: g.dma_start(
                out=wstage[wb_][:], in_=wv_h[rc * 128:(rc + 1) * 128, :]),
                deps=[wst_free[wb_]], inc="dw", amt=16)
            t_wtr = None
            for dc in range(DC):
                t_wtr = Em("tensor", lambda t, dc=dc, wb_=wb_: t.transpose(
                    PT[1][:, dc * 128:(dc + 1) * 128],
                    wstage[wb_][:, dc * 128:(dc + 1) * 128], ident[:]),
                    deps=[t_w, t_id, pt_free[1]], inc=("t" if dc == DC - 1 else None))
            wst_free[wb_] = t_wtr
            wv_tok[rc] = Em("scalar", lambda sc, rc=rc: sc.copy(
                WT[0][:, :, rc * 128:(rc + 1) * 128],
                PT[1].rearrange("p (c q) -> p c q", c=DC)), deps=[t_wtr], inc="s")
            pt_free[1] = wv_tok[rc]
        emit_ready_vT(tt)
    emit_ready_vT(99)
    xs_last = max(t for t in xs_free if t is not None)
    t_wvT = wv_tok[-1]
    vT_toks = [vT_ev[oc][tb] for oc in range(DC) for tb in range(4)]
    if phase_limit <= 0:
        return _finish(nc, P, taps, locals())

    def prep_wT(w_hbm, slot, deps_extra):
        nonlocal wst_free
        done = None
        for rc in range(DC):
            b = rc % 2
            t_w = Em("gpsimd", lambda g, rc=rc, b=b, w_hbm=w_hbm: g.dma_start(
                out=wstage[b][:], in_=w_hbm[rc * 128:(rc + 1) * 128, :]),
                deps=[wst_free[b]] + deps_extra, inc="dw", amt=16)
            pb = rc % 2
            t_tr = None
            for dc in range(DC):
                t_tr = Em("tensor", lambda t, rc=rc, dc=dc, b=b, pb=pb: t.transpose(
                    PT[pb][:, dc * 128:(dc + 1) * 128],
                    wstage[b][:, dc * 128:(dc + 1) * 128], ident[:]),
                    deps=[t_w, pt_free[pb]], inc=("t" if dc == DC - 1 else None))
            wst_free[b] = t_tr
            done = Em("vector", lambda v, rc=rc, slot=slot, pb=pb: v.tensor_copy(
                WT[slot][:, :, rc * 128:(rc + 1) * 128],
                PT[pb].rearrange("p (c q) -> p c q", c=DC)), deps=[t_tr], inc="v")
            pt_free[pb] = done
        return done

    def load_nat(w_hbm, slot, deps_extra):
        return Em("gpsimd", lambda g, w_hbm=w_hbm, slot=slot: g.dma_start(
            out=WT[slot][:], in_=w_hbm.rearrange("(c p) k -> p c k", p=128)),
            deps=deps_extra, inc="dw", amt=16)

    def proj(dst, lhs_of, rhs_of, dep_lhs, dep_rhs, extra_mm_deps=()):
        toks = []
        g = 0
        for oc in range(DC):
            for tb in range(4):
                bank = g % 4
                t_mm = None
                for dc in range(DC):
                    t_mm = Em("tensor", lambda t, oc=oc, tb=tb, dc=dc, bank=bank: t.matmul(
                        PE[:, bank * 512:(bank + 1) * 512], lhs_of(dc, oc), rhs_of(dc, tb),
                        start=(dc == 0), stop=(dc == DC - 1)),
                        deps=[dep_lhs, dep_rhs, pe_free[bank]] + list(extra_mm_deps),
                        inc=("t" if dc == DC - 1 else None))
                t_ev = Em("vector", lambda v, oc=oc, tb=tb, bank=bank, dst=dst: v.tensor_copy(
                    dst[:, oc, tb * 512:(tb + 1) * 512], PE[:, bank * 512:(bank + 1) * 512]),
                    deps=[t_mm], inc="v")
                pe_free[bank] = t_ev
                toks.append(t_ev)
                g += 1
        return toks

    vT_last = max((t for row in vT_ev for t in row), key=lambda x: x[1])
    t_wbT = prep_wT(wb_h, 1, [])
    u_toks = []
    g = 0
    for tt in range(TT):
        for wh in range(2):
            bank = g % 4
            t_mm = None
            for vc in range(DC):
                t_mm = Em("tensor", lambda t, tt=tt, wh=wh, vc=vc, bank=bank: t.matmul(
                    PE[:, bank * 512:(bank + 1) * 512],
                    vT[:, vc, tt * 128:(tt + 1) * 128],
                    WT[1][:, vc, wh * 512:(wh + 1) * 512],
                    start=(vc == 0), stop=(vc == DC - 1)),
                    deps=[t_wbT, pe_free[bank]] + [vT_ev[oc][tt // 4] for oc in range(DC)],
                    inc=("t" if vc == DC - 1 else None))
            t_ev = Em("vector", lambda v, tt=tt, wh=wh, bank=bank: v.tensor_copy(
                u[:, tt, wh * 512:(wh + 1) * 512], PE[:, bank * 512:(bank + 1) * 512]),
                deps=[t_mm, xs_last], inc="v")
            pe_free[bank] = t_ev
            u_toks.append(t_ev)
            g += 1
    if phase_limit <= 2:
        return _finish(nc, P, taps, locals())

    t_wqT = prep_wT(wq_h, 0, [vT_last])
    qT_toks = proj(qT, lambda dc, oc: WT[0][:, dc, oc * 128:(oc + 1) * 128],
                   lambda dc, tb: xnT[:, dc, tb * 512:(tb + 1) * 512],
                   t_wqT, xnT_tok, extra_mm_deps=[u_toks[-1]])
    t_wa = load_nat(wa_h, 1, [u_toks[-1]])
    pT_toks = proj(pT, lambda qc, kb: WT[1][:, qc, kb * 128:(kb + 1) * 128],
                   lambda qc, tb: qT[:, qc, tb * 512:(tb + 1) * 512],
                   t_wa, qT_toks[-1])
    t_wkT = prep_wT(wk_h, 0, [qT_toks[-1]])
    kT_toks = proj(kT, lambda dc, oc: WT[0][:, dc, oc * 128:(oc + 1) * 128],
                   lambda dc, tb: xnT[:, dc, tb * 512:(tb + 1) * 512],
                   t_wkT, xnT_tok, extra_mm_deps=[pT_toks[-1]])
    if phase_limit <= 4:
        return _finish(nc, P, taps, locals())

    t_cn = Em("gpsimd", lambda g: g.dma_start(out=wstage[0][0:64, :], in_=cent_h),
              deps=[wst_free[0]], inc="dw", amt=16)
    t_ct = None
    for dc in range(DC):
        t_ct = Em("tensor", lambda t, dc=dc: t.transpose(
            PT[0][:, dc * 64:(dc + 1) * 64], wstage[0][0:64, dc * 128:(dc + 1) * 128],
            ident[0:64, 0:64]), deps=[t_cn, pt_free[0]], inc=("t" if dc == DC - 1 else None))
    t_centT = Em("vector", lambda v: v.tensor_copy(
        centT[:], PT[0][:, 0:512].rearrange("p (c e) -> p c e", c=DC)),
        deps=[t_ct], inc="v")
    pt_free[0] = t_centT
    wst_free[0] = t_ct
    t_bb = Em("tensor", lambda t: t.matmul(
        PS[:, 0:E], ones_sb[:], bias_sb[:], start=True, stop=True),
        deps=[t_ones, t_bias], inc="t")
    t_bbc = Em("vector", lambda v: v.tensor_copy(bias_bc[:], PS[:, 0:E]),
               deps=[t_bb], inc="v")
    ps_free = [t_bbc, None]
    prev_chain = None
    for tt in range(TT):
        pb = tt % 2
        t_mm = None
        for dc in range(DC):
            t_mm = Em("tensor", lambda t, tt=tt, dc=dc, pb=pb: t.matmul(
                PS[:, pb * 512:pb * 512 + E], xnT[:, dc, tt * 128:(tt + 1) * 128],
                centT[:, dc, :], start=(dc == 0), stop=(dc == DC - 1)),
                deps=[xnT_tile_toks[tt], t_centT, ps_free[pb]],
                inc=("t" if dc == DC - 1 else None))
        c1 = Em("vector", lambda v, pb=pb: v.tensor_copy(aff_sb[:], PS[:, pb * 512:pb * 512 + E]),
                deps=[t_mm, prev_chain], inc="v")
        ps_free[pb] = c1
        c2 = Em("vector", lambda v: v.tensor_add(biased[:], aff_sb[:], bias_bc[:]),
                deps=[c1, t_bbc], inc="v")
        c3 = Em("vector", lambda v: v.max(out=max8[:], in_=biased[:]), deps=[c2], inc="v")
        c4 = Em("vector", lambda v: v.match_replace(
            out=zap[:], in_to_replace=max8[:], in_values=biased[:], imm_value=NEG_BIG),
            deps=[c3], inc="v")
        c5 = Em("vector", lambda v: v.tensor_scalar(mask[:], zap[:], NEG_THR, None, ALU.is_le),
                deps=[c4], inc="v")
        c6 = Em("scalar", lambda sc: sc.activation(sig[:], aff_sb[:], AF.Sigmoid),
                deps=[c1, c5], inc="s")
        c7 = Em("vector", lambda v: v.tensor_mul(wm[:], sig[:], mask[:]),
                deps=[c6, c5], inc="v")
        c8 = Em("vector", lambda v: v.tensor_reduce(den1[:], wm[:], AX.X, ALU.add),
                deps=[c7], inc="v")
        c9 = Em("vector", lambda v: v.tensor_scalar(den1[:], den1[:], 1e-8, None, ALU.add),
                deps=[c8], inc="v")
        c10 = Em("vector", lambda v: v.reciprocal(rd1[:], den1[:]), deps=[c9], inc="v")
        c11 = Em("vector", lambda v, tt=tt: v.tensor_scalar(
            cw[:, tt * E:(tt + 1) * E], wm[:], rd1[:], None, ALU.mult),
            deps=[c10], inc="v")
        prev_chain = c11
    cw_done = prev_chain

    s_toks = []
    e_free = None
    ps_av_free = max(t for t in ps_free if t is not None)
    at_free = [pt_free[0], pt_free[1]]
    alpha_used = None
    nmp_free = None
    for it in range(TT):
        pm_toks = []
        for jb in range(4):
            t_mm = None
            for kb in range(DC):
                t_mm = Em("tensor", lambda t, it=it, jb=jb, kb=kb: t.matmul(
                    PE[:, jb * 512:(jb + 1) * 512],
                    pT[:, kb, it * 128:(it + 1) * 128],
                    kT[:, kb, jb * 512:(jb + 1) * 512],
                    start=(kb == 0), stop=(kb == DC - 1)),
                    deps=[pT_toks[-1], kT_toks[-1], e_free],
                    inc=("t" if kb == DC - 1 else None))
            pm_toks.append(Em("vector", lambda v, jb=jb: v.reduce_max(
                nmp[:, jb:jb + 1], PE[:, jb * 512:(jb + 1) * 512], AX.X),
                deps=[t_mm, nmp_free], inc="v"))
        t_nm = Em("vector", lambda v, it=it: v.tensor_reduce(
            negm16[:, it:it + 1], nmp[:], AX.X, ALU.max, negate=True),
            deps=pm_toks, inc="v")
        nmp_free = t_nm
        exp_toks = []
        for c in range(4):
            exp_toks.append(Em("scalar", lambda sc, it=it, c=c: sc.activation(
                alpha_sb[:, c * 512:(c + 1) * 512], PE[:, c * 512:(c + 1) * 512],
                AF.Exp, bias=negm16[:, it:it + 1]),
                deps=[t_nm, alpha_used if c == 0 else None], inc="s"))
        e_free = exp_toks[-1]
        t_rs = Em("vector", lambda v, it=it: v.reduce_sum(
            rowsum16[:, it:it + 1], alpha_sb[:], AX.X), deps=[exp_toks[-1]], inc="v")
        t_rc = Em("vector", lambda v, it=it: v.reciprocal(
            recip16[:, it:it + 1], rowsum16[:, it:it + 1]), deps=[t_rs], inc="v")
        at_evac = []
        for gb in range(2):
            t_tr = None
            for k in range(8):
                jb8 = gb * 8 + k
                t_tr = Em("tensor", lambda t, jb8=jb8, k=k, gb=gb: t.transpose(
                    PT[gb][:, k * 128:(k + 1) * 128],
                    alpha_sb[:, jb8 * 128:(jb8 + 1) * 128], ident[:]),
                    deps=[exp_toks[jb8 // 4], at_free[gb]], inc=("t" if k == 7 else None))
            t_ev = Em("vector", lambda v, gb=gb: v.tensor_copy(
                alphaT_sb[:, gb * 1024:(gb + 1) * 1024], PT[gb][:]),
                deps=[t_tr], inc="v")
            at_free[gb] = t_ev
            at_evac.append(t_ev)
        alpha_used = at_evac[-1]
        av_groups = []
        for wh in range(2):
            t_mm = None
            for j in range(16):
                t_mm = Em("tensor", lambda t, wh=wh, j=j: t.matmul(
                    PS[:, wh * 512:(wh + 1) * 512],
                    alphaT_sb[:, j * 128:(j + 1) * 128],
                    u[:, j, wh * 512:(wh + 1) * 512],
                    start=(j == 0), stop=(j == 15)),
                    deps=[at_evac[j // 8], u_toks[-1], ps_av_free],
                    inc=("t" if j == 15 else None))
            av_groups.append(t_mm)
        t_s = Em("vector", lambda v, it=it: v.tensor_scalar(
            s_sb[:, it, :], PS[:, 0:1024], recip16[:, it:it + 1], None, ALU.mult),
            deps=av_groups + [t_rc], inc="v")
        ps_av_free = t_s
        s_toks.append(t_s)
    if phase_limit <= 5:
        return _finish(nc, P, taps, locals())

    t_wsu = Em("gpsimd", lambda g: g.dma_start(
        out=Wsup[:], in_=wsup_h.rearrange("(c p) f -> p c f", p=128)),
        deps=[s_toks[-1]], inc="dw", amt=16)
    t_wsd = Em("gpsimd", lambda g: g.dma_start(
        out=Wsdn[:], in_=wsdn_h.rearrange("(c p) f -> p c f", p=128)),
        deps=[s_toks[-1]], inc="dw", amt=16)

    up_bank_free = [None] * 4
    dn_bank_free = [None, None]
    wu_free = [s_toks[-1], s_toks[-1]]
    wd_free = [s_toks[-1], s_toks[-1]]
    h_readers = []
    out_dma_toks = []
    xs6_free = [s_toks[-1], s_toks[-1]]
    os_free = [None, None, None]
    g_read = [None] * 4

    def up_pass(lhs_buf, dep_w, half, dep_hfree):
        toks = [[], []]
        g = 0
        for tb in range(TH // 512):
            for fc in range(FC):
                bank = g % 4
                t_mm = None
                for dc in range(DC):
                    t_mm = Em("tensor", lambda t, fc=fc, tb=tb, dc=dc, bank=bank,
                              lhs_buf=lhs_buf, half=half: t.matmul(
                        PE[:, bank * 512:(bank + 1) * 512],
                        lhs_buf[:, dc, fc * 128:(fc + 1) * 128],
                        xnT[:, dc, half * TH + tb * 512:half * TH + (tb + 1) * 512],
                        start=(dc == 0), stop=(dc == DC - 1)),
                        deps=[dep_w, xnT_tok, up_bank_free[bank]] + list(dep_hfree),
                        inc=("t" if dc == DC - 1 else None))
                t_h = Em("scalar", lambda sc, fc=fc, tb=tb, bank=bank: sc.activation(
                    Hbuf[:, fc, tb * 512:(tb + 1) * 512],
                    PE[:, bank * 512:(bank + 1) * 512], AF.Silu),
                    deps=[t_mm], inc="s")
                up_bank_free[bank] = t_h
                toks[tb].append(t_h)
                g += 1
        return toks

    def down_pass(rhs_buf, dep_w, half, h_toks, weighted, e=None, acc_init_toks=None,
                  acc_guards=None):
        toks = []
        last_mm = None
        g = 0
        for tl in range(TTH):
            for dh in range(2):
                bank = g % 2
                t_mm = None
                for fc in range(FC):
                    t_mm = Em("tensor", lambda t, tl=tl, dh=dh, fc=fc, bank=bank,
                              rhs_buf=rhs_buf: t.matmul(
                        PS[:, bank * 512:(bank + 1) * 512],
                        Hbuf[:, fc, tl * 128:(tl + 1) * 128],
                        rhs_buf[:, fc, dh * 512:(dh + 1) * 512],
                        start=(fc == 0), stop=(fc == FC - 1)),
                        deps=[dep_w, dn_bank_free[bank]] + h_toks[tl // 4],
                        inc=("t" if fc == FC - 1 else None))
                last_mm = t_mm
                if weighted:
                    tg = half * TTH + tl
                    t_ev = Em("vector", lambda v, tl=tl, dh=dh, bank=bank, tg=tg, e=e: v.scalar_tensor_tensor(
                        acc[:, tl, dh * 512:(dh + 1) * 512],
                        PS[:, bank * 512:(bank + 1) * 512],
                        cw[:, tg * E + e:tg * E + e + 1],
                        acc[:, tl, dh * 512:(dh + 1) * 512], ALU.mult, ALU.add),
                        deps=[t_mm, cw_done, acc_init_toks[tl]], inc="v")
                else:
                    t_ev = Em("vector", lambda v, tl=tl, dh=dh, bank=bank: v.tensor_copy(
                        acc[:, tl, dh * 512:(dh + 1) * 512],
                        PS[:, bank * 512:(bank + 1) * 512]),
                        deps=[t_mm, s_toks[-1],
                              acc_guards[tl] if acc_guards else None], inc="v")
                dn_bank_free[bank] = t_ev
                toks.append(t_ev)
                g += 1
        return toks, last_mm

    acc_guards = None
    for half in range(NH):
        sh_h = up_pass(Wsup, t_wsu, half, h_readers)
        sh_acc, last_sh_mm = down_pass(Wsdn, t_wsd, half, sh_h, weighted=False,
                                       acc_guards=acc_guards)
        h_readers = [last_sh_mm]
        acc_init = [None] * TTH
        for i, tok in enumerate(sh_acc):
            acc_init[i // 2] = tok
        for e in range(E):
            par = e % 2
            t_wu = Em("gpsimd", lambda g, e=e, par=par: g.dma_start(
                out=Wu_b[par][:], in_=wup_h[e].rearrange("(c p) f -> p c f", p=128)),
                deps=[wu_free[par]], inc="dw", amt=16)
            t_wd = Em("gpsimd", lambda g, e=e, par=par: g.dma_start(
                out=Wd_b[par][:], in_=wdn_h[e].rearrange("(c p) f -> p c f", p=128)),
                deps=[wd_free[par]], inc="dw", amt=16)
            h_toks = up_pass(Wu_b[par], t_wu, half, h_readers)
            wu_free[par] = h_toks[1][-1]
            ev_toks, last_mm = down_pass(Wd_b[par], t_wd, half, h_toks,
                                         weighted=True, e=e, acc_init_toks=acc_init)
            wd_free[par] = last_mm
            h_readers = [last_mm]
        last_ev_toks = ev_toks
        g_used = None
        t_m1 = None
        acc_guards = [None] * TTH
        for tl in range(TTH):
            tg = half * TTH + tl
            b = tl % 2
            t_dx = Em("sync", lambda s, tg=tg, b=b: s.dma_start(
                out=xstage[b][:], in_=x_h[tg * 128:(tg + 1) * 128, :]),
                deps=[xs6_free[b]], inc="dx", amt=16)
            t_g = Em("scalar", lambda sc, tl=tl: sc.activation(
                gbuf[:, tl % 4, :], acc[:, tl, :], AF.Silu),
                deps=[last_ev_toks[2 * tl + 1], g_read[tl % 4]], inc="s")
            g_used = t_g
            acc_guards[tl] = t_g
            t_m1 = Em("vector", lambda v, tl=tl, tg=tg, b=b: v.tensor_mul(
                ostage[b][:], s_sb[:, tg, :], gbuf[:, tl % 4, :]),
                deps=[t_g, s_toks[tg], os_free[b]], inc="v")
            g_read[tl % 4] = t_m1
            t_m2 = Em("vector", lambda v, b=b: v.tensor_add(
                ostage[b][:], ostage[b][:], xstage[b][:]),
                deps=[t_m1, t_dx], inc="v")
            xs6_free[b] = t_m2
            t_o = Em("sync", lambda s, tg=tg, b=b: s.dma_start(
                out=y_h[tg * 128:(tg + 1) * 128, :], in_=ostage[b][:]),
                deps=[t_m2], inc="dx", amt=16)
            os_free[b] = t_o
            out_dma_toks.append(t_o)

    Em("sync", lambda s: None, deps=[out_dma_toks[-1]])
    return _finish(nc, P, taps, locals())


def _finish(nc, P, taps, env):
    tap_outs = []
    for name, key in taps:
        t = env[key]
        shape = list(t.shape)
        dt = t.tensor.dtype if hasattr(t, "tensor") else t.dtype
        o = nc.dram_tensor(name, shape, dt, kind="ExternalOutput").ap()
        P.emit("sync", lambda s, o=o, t=t: s.dma_start(out=o, in_=t[:]),
               deps=P.final_deps(), inc="dx", amt=16)
        tap_outs.append(name)
    if tap_outs:
        P.emit("sync", lambda s: None, deps=P.final_deps())
    with ExitStack() as ctx:
        sems = {n: ctx.enter_context(nc.semaphore(n)) for n in P.sem_names}
        block = ctx.enter_context(nc.Block())
        P.build(block, sems)
    return nc


_NC_CACHE = {}


def _get_nc(phase_limit=99, taps=()):
    key = (phase_limit, tuple(taps))
    if key not in _NC_CACHE:
        _NC_CACHE[key] = build_nc(phase_limit, taps)
    return _NC_CACHE[key]


def kernel(L0_attn_x, L0_attn_Wk, L0_attn_Wq, L0_attn_Wv, L0_attn_Wa, L0_attn_Wb,
           L0_ffn_centroids, L0_ffn_expert_biases, L0_ffn_Ws_up, L0_ffn_Ws_down,
           L0_ffn_W_up, L0_ffn_W_down, _phase_limit=99, _taps=()):
    nc = _get_nc(_phase_limit, _taps)
    common = dict(
        wk=np.ascontiguousarray(L0_attn_Wk), wq=np.ascontiguousarray(L0_attn_Wq),
        wv=np.ascontiguousarray(L0_attn_Wv), wa=np.ascontiguousarray(L0_attn_Wa),
        wb=np.ascontiguousarray(L0_attn_Wb),
        cent=np.ascontiguousarray(L0_ffn_centroids),
        ebias=np.ascontiguousarray(np.reshape(L0_ffn_expert_biases, (1, E))),
        wsup=np.ascontiguousarray(L0_ffn_Ws_up), wsdn=np.ascontiguousarray(L0_ffn_Ws_down),
        wup=np.ascontiguousarray(L0_ffn_W_up), wdn=np.ascontiguousarray(L0_ffn_W_down),
    )
    x = np.ascontiguousarray(L0_attn_x)
    in_maps = [dict(common, x=np.ascontiguousarray(x[c])) for c in range(8)]
    res = run_bass_kernel_spmd(nc, in_maps, core_ids=list(range(8)))
    if _taps or _phase_limit < 99:
        return res
    return np.stack([r["y"] for r in res.results], axis=0)


# version 10
# speedup vs baseline: 1.0467x; 1.0467x over previous
import sys
if "/opt/trn_rl_repo" not in sys.path:
    sys.path.insert(0, "/opt/trn_rl_repo")
from collections import defaultdict
from contextlib import ExitStack

import numpy as np
import ml_dtypes
import concourse.bass as bass
import concourse.mybir as mybir
from concourse.bass_utils import run_bass_kernel_spmd

F32 = mybir.dt.float32
BF16 = mybir.dt.bfloat16
AF = mybir.ActivationFunctionType
ALU = mybir.AluOpType
AX = mybir.AxisListType

ENGINES = ("sync", "gpsimd", "tensor", "vector", "scalar")

S, D, E, FF = 2048, 1024, 64, 512
TT = S // 128
DC = D // 128
FC = FF // 128
NH = 2
TH = S // NH
TTH = TH // 128
NEG_BIG = -1e30
NEG_THR = -1e29


class Prog:
    def __init__(self):
        self.ops = {e: [] for e in ENGINES}
        self.sem_names = []
        self.counts = defaultdict(int)

    def sem(self, name):
        if name not in self.sem_names:
            self.sem_names.append(name)
        return name

    PARANOID = False

    def emit(self, engine, fn, deps=(), inc=None, amt=1):
        tok = None
        deps = [d for d in deps if d is not None]
        if Prog.PARANOID:
            deps = [(s, v) for s, v in self.counts.items()]
        if inc is not None:
            self.sem(inc)
            if amt == 16 and self.counts[inc] > 0:
                deps.append((inc, self.counts[inc]))
            self.counts[inc] += amt
            tok = (inc, self.counts[inc])
        self.ops[engine].append((fn, deps, inc, amt))
        return tok

    def final_deps(self):
        return [(s, self.counts[s]) for s in self.sem_names]

    def build(self, block, sems):
        for e in ENGINES:
            if not self.ops[e]:
                continue

            def make(e):
                def f(eng):
                    hw = defaultdict(int)
                    for fn, deps, inc, amt in self.ops[e]:
                        for (s, v) in deps:
                            if v > hw[s]:
                                eng.wait_ge(sems[s], v)
                                hw[s] = v
                        inst = fn(eng)
                        if inc is not None:
                            if inst is None:
                                inst = eng.nop()
                            inst.then_inc(sems[inc], amt)
                return f

            getattr(block, e)(make(e))


def build_nc(phase_limit=99, taps=()):
    nc = bass.Bass("TRN2", target_bir_lowering=False)
    x_h = nc.dram_tensor("x", [S, D], F32, kind="ExternalInput").ap()
    wk_h = nc.dram_tensor("wk", [D, D], F32, kind="ExternalInput").ap()
    wq_h = nc.dram_tensor("wq", [D, D], F32, kind="ExternalInput").ap()
    wv_h = nc.dram_tensor("wv", [D, D], F32, kind="ExternalInput").ap()
    wa_h = nc.dram_tensor("wa", [D, D], F32, kind="ExternalInput").ap()
    wb_h = nc.dram_tensor("wb", [D, D], F32, kind="ExternalInput").ap()
    cent_h = nc.dram_tensor("cent", [E, D], F32, kind="ExternalInput").ap()
    ebias_h = nc.dram_tensor("ebias", [1, E], F32, kind="ExternalInput").ap()
    wsup_h = nc.dram_tensor("wsup", [D, FF], F32, kind="ExternalInput").ap()
    wsdn_h = nc.dram_tensor("wsdn", [FF, D], F32, kind="ExternalInput").ap()
    wup_h = nc.dram_tensor("wup", [E, D, FF], F32, kind="ExternalInput").ap()
    wdn_h = nc.dram_tensor("wdn", [E, FF, D], F32, kind="ExternalInput").ap()
    y_h = nc.dram_tensor("y", [S, D], F32, kind="ExternalOutput").ap()

    ident_hbm = nc.inline_tensor(np.eye(128, dtype=ml_dtypes.bfloat16), name="identc").ap()
    ones_hbm = nc.inline_tensor(np.ones((1, 128), dtype=np.float32), name="onesc").ap()

    ARENA_BYTES = 196608
    arena = nc.alloc_sbuf_tensor("ARENA", [128, ARENA_BYTES], mybir.dt.uint8)
    SBUF_BASE = nc.lookup_mloc(arena).addr

    def at(name, off, shape, dt):
        assert 0 <= off and off < ARENA_BYTES
        return nc.alloc_sbuf_tensor_at(name, shape, dt, offset=SBUF_BASE + off).ap()

    xnT = at("xnT", 0, [128, DC, S], BF16)
    cw = at("cw", 32768, [128, TT * E], F32)
    co = 36864
    ident = at("ident", co + 0, [128, 128], BF16); co += 256
    centT = at("centT", co, [128, DC, E], BF16); co += 1024
    bias_sb = at("bias_sb", co, [1, E], F32); co += 256
    bias_bc = at("bias_bc", co, [128, E], F32); co += 256
    ones_sb = at("ones_sb", co, [1, 128], F32); co += 512
    aff_sb = at("aff_sb", co, [128, E], F32); co += 256
    biased = at("biased", co, [128, E], F32); co += 256
    max8 = at("max8", co, [128, 8], F32); co += 32
    zap = at("zap", co, [128, E], F32); co += 256
    mask = at("mask", co, [128, E], F32); co += 256
    sig = at("sig", co, [128, E], F32); co += 256
    wm = at("wm", co, [128, E], F32); co += 256
    st6 = at("st6", co, [128, 12], F32); co += 64
    mv2 = at("mv2", co, [128, 2 * TT], F32); co += 128
    nmp = at("nmp", co, [128, 4], F32); co += 32
    epsv = at("epsv", co, [128, 1], F32); co += 32
    rcp16 = at("rcp16", co, [128, TT], F32); co += 64
    rstd16 = at("rstd16", co, [128, TT], F32); co += 64
    negm16 = at("negm16", co, [128, TT], F32); co += 64
    rowsum16 = at("rowsum16", co, [128, TT], F32); co += 64
    recip16 = at("recip16", co, [128, TT], F32); co += 64
    den1 = at("den1", co, [128, 1], F32); co += 64
    rd1 = at("rd1", co, [128, 1], F32); co += 64
    assert co <= 45056, co
    u = at("u", 45056, [128, TT, D], BF16)
    Wu_b = [at(f"Wu{i}", 45056 + i * 8192, [128, DC, FF], BF16) for i in range(2)]
    Wd_b = [at(f"Wd{i}", 45056 + 16384 + i * 8192, [128, FC, D], BF16) for i in range(2)]
    vT = at("vT", 77824, [128, DC, S], BF16)
    qT = at("qT", 77824, [128, DC, S], BF16)
    kT = at("kT", 77824, [128, DC, S], BF16)
    acc = at("acc", 77824, [128, TTH, D], F32)
    WT = [at(f"WT{i}", 110592 + i * 16384, [128, DC, D], BF16) for i in range(2)]
    s_sb = at("s_sb", 110592, [128, TT, D], BF16)
    pT = at("pT", 143360, [128, DC, S], BF16)
    Hbuf = at("Hbuf", 143360, [128, FC, TH], BF16)
    Wsup = at("Wsup", 143360 + 8192, [128, DC, FF], BF16)
    Wsdn = at("Wsdn", 143360 + 16384, [128, FC, D], BF16)
    gbuf = at("gbuf", 143360 + 24576, [128, 4, D], BF16)
    xstage = [at(f"xstage{i}", 176128 + i * 4096, [128, D], F32) for i in range(2)]
    wstage = [at(f"wstage{i}", 192512 + i * 2048, [128, D], BF16) for i in range(2)]
    alpha_sb = at("alpha_sb", 176128, [128, S], BF16)
    alphaT_sb = at("alphaT_sb", 176128 + 4096, [128, S], BF16)
    xn_t2 = [at(f"xn_t{i}", 184320 + i * 2048, [128, D], BF16) for i in range(2)]
    sqf = at("sqf", 184320 + 4096, [128, D], F32)
    ostage = [at(f"ostage{i}", 184320 + i * 4096, [128, D], F32) for i in range(3)]

    PE = nc.alloc_psum_tensor("PE", [128, 2048], F32).ap()
    PS = nc.alloc_psum_tensor("PS", [128, 1024], F32).ap()
    PT = [nc.alloc_psum_tensor(f"PT{i}", [128, 1024], BF16).ap() for i in range(2)]

    P = Prog()
    Em = P.emit

    LN_ONLY = (phase_limit == -1)
    if not LN_ONLY:
        t_id = Em("sync", lambda s: s.dma_start(out=ident[:], in_=ident_hbm), inc="dx", amt=16)
        t_ones = Em("sync", lambda s: s.dma_start(out=ones_sb[:], in_=ones_hbm), inc="dx", amt=16)
        t_bias = Em("sync", lambda s: s.dma_start(out=bias_sb[:], in_=ebias_h), inc="dx", amt=16)
    else:
        t_id = t_ones = t_bias = None

    t_eps = Em("vector", lambda v: v.memset(epsv[:], 1e-5), inc="v")
    xs_free = [None, None]
    pt_free = [None, None]
    xn_used = [None, None]
    wst_free = [None, None]
    pe_free = [None] * 4
    sq_done = None
    xnT_tok = None
    xnT_tile_toks = [None] * TT
    wv_tok = [None] * DC
    vT_ev = [[None] * 4 for _ in range(DC)]
    vT_emitted = set()
    vg = [0]

    def emit_vT_group(oc, tb):
        bank = vg[0] % 4
        vg[0] += 1
        t_mm = None
        for dc in range(DC):
            t_mm = Em("tensor", lambda t, oc=oc, tb=tb, dc=dc, bank=bank: t.matmul(
                PE[:, bank * 512:(bank + 1) * 512],
                WT[0][:, dc, oc * 128:(oc + 1) * 128],
                xnT[:, dc, tb * 512:(tb + 1) * 512],
                start=(dc == 0), stop=(dc == DC - 1)),
                deps=[wv_tok[oc], xnT_tile_toks[4 * tb + 3], pe_free[bank]],
                inc=("t" if dc == DC - 1 else None))
        t_ev = Em("scalar", lambda sc, oc=oc, tb=tb, bank=bank: sc.copy(
            vT[:, oc, tb * 512:(tb + 1) * 512], PE[:, bank * 512:(bank + 1) * 512]),
            deps=[t_mm], inc="s")
        pe_free[bank] = t_ev
        vT_ev[oc][tb] = t_ev
        vT_emitted.add((oc, tb))

    def emit_ready_vT(tt):
        for tb in range(4):
            if 4 * tb + 3 > tt:
                break
            for oc in range(DC):
                if 2 * oc + 1 <= tt and (oc, tb) not in vT_emitted:
                    emit_vT_group(oc, tb)

    xp = [at(f"xp{i}", 45056 + i * 8192, [128, 2, D], F32) for i in range(2)]
    e4s = [None] * TT
    ags = [None] * TT
    xsrcs = [None] * TT

    def p0_apply(tt):
        nonlocal xnT_tok
        b = (tt // 2) % 2
        xn_t = xn_t2[tt % 2]
        xsrc = xsrcs[tt]
        t_f = Em("vector", lambda v, tt=tt, xsrc=xsrc, xn_t=xn_t: v.tensor_scalar(
            xn_t[:], xsrc[:], mv2[:, 2 * tt:2 * tt + 1], rstd16[:, tt:tt + 1],
            ALU.subtract, ALU.mult), deps=[e4s[tt], ags[tt], xn_used[tt % 2]], inc="v")
        xs_free[b] = t_f
        t_tr = None
        for dc in range(DC):
            t_tr = Em("tensor", lambda t, dc=dc, xn_t=xn_t: t.transpose(
                PT[0][:, dc * 128:(dc + 1) * 128], xn_t[:, dc * 128:(dc + 1) * 128],
                ident[:]), deps=[t_f, t_id, pt_free[0]],
                inc=("t" if dc == DC - 1 else None))
        xn_used[tt % 2] = t_tr
        for dc in range(DC):
            xnT_tok = Em("scalar", lambda sc, tt=tt, dc=dc: sc.copy(
                xnT[:, dc, tt * 128:(tt + 1) * 128],
                PT[0][:, dc * 128:(dc + 1) * 128]), deps=[t_tr], inc="s")
        pt_free[0] = xnT_tok
        xnT_tile_toks[tt] = xnT_tok
        if tt % 2 == 1:
            rc = (tt - 1) // 2
            wb_ = rc % 2
            t_w = Em("gpsimd", lambda g, rc=rc, wb_=wb_: g.dma_start(
                out=wstage[wb_][:], in_=wv_h[rc * 128:(rc + 1) * 128, :]),
                deps=[wst_free[wb_]], inc="dw", amt=16)
            t_wtr = None
            for dc in range(DC):
                t_wtr = Em("tensor", lambda t, dc=dc, wb_=wb_: t.transpose(
                    PT[1][:, dc * 128:(dc + 1) * 128],
                    wstage[wb_][:, dc * 128:(dc + 1) * 128], ident[:]),
                    deps=[t_w, t_id, pt_free[1]], inc=("t" if dc == DC - 1 else None))
            wst_free[wb_] = t_wtr
            wv_tok[rc] = Em("scalar", lambda sc, rc=rc: sc.copy(
                WT[0][:, :, rc * 128:(rc + 1) * 128],
                PT[1].rearrange("p (c q) -> p c q", c=DC)), deps=[t_wtr], inc="s")
            pt_free[1] = wv_tok[rc]
        emit_ready_vT(tt)

    t_dx = None
    for tt in range(TT + 1):
        if tt < TT:
            b = (tt // 2) % 2
            if tt % 2 == 0:
                q = tt // 2
                t_dx = Em("sync", lambda s, q=q, b=b: s.dma_start(
                    out=xp[b][:], in_=x_h[q * 256:(q + 1) * 256, :].rearrange(
                        "(c p) d -> p c d", p=128)),
                    deps=[xs_free[b]], inc="dx", amt=16)
            xsrcs[tt] = xp[b][:, tt % 2, :]
            t_b1 = Em("vector", lambda v, tt=tt: v.bn_stats(
                st6[:, 0:6], xsrcs[tt][:, 0:512]), deps=[t_dx, sq_done], inc="v")
            t_b2 = Em("vector", lambda v, tt=tt: v.bn_stats(
                st6[:, 6:12], xsrcs[tt][:, 512:1024]), deps=[t_b1], inc="v")
            t_ag = Em("vector", lambda v, tt=tt: v.bn_aggr(
                mv2[:, 2 * tt:2 * tt + 2], st6[:]), deps=[t_b2], inc="v")
            sq_done = t_ag
            ags[tt] = t_ag
            t_e3 = Em("scalar", lambda sc, tt=tt: sc.activation(
                rcp16[:, tt:tt + 1], mv2[:, 2 * tt + 1:2 * tt + 2], AF.Ln, bias=epsv[:]),
                deps=[t_ag, t_eps], inc="s")
            e4s[tt] = Em("scalar", lambda sc, tt=tt: sc.activation(
                rstd16[:, tt:tt + 1], rcp16[:, tt:tt + 1], AF.Exp, scale=-0.5),
                deps=[t_e3], inc="s")
        if not LN_ONLY and tt >= 1:
            p0_apply(tt - 1)
    if LN_ONLY:
        return _finish(nc, P, taps, locals())
    emit_ready_vT(99)
    xs_last = max(t for t in xs_free if t is not None)
    t_wvT = wv_tok[-1]
    vT_toks = [vT_ev[oc][tb] for oc in range(DC) for tb in range(4)]
    if phase_limit <= 0:
        return _finish(nc, P, taps, locals())

    def prep_wT(w_hbm, slot, deps_extra):
        nonlocal wst_free
        done = None
        for rc in range(DC):
            b = rc % 2
            t_w = Em("gpsimd", lambda g, rc=rc, b=b, w_hbm=w_hbm: g.dma_start(
                out=wstage[b][:], in_=w_hbm[rc * 128:(rc + 1) * 128, :]),
                deps=[wst_free[b]] + deps_extra, inc="dw", amt=16)
            pb = rc % 2
            t_tr = None
            for dc in range(DC):
                t_tr = Em("tensor", lambda t, rc=rc, dc=dc, b=b, pb=pb: t.transpose(
                    PT[pb][:, dc * 128:(dc + 1) * 128],
                    wstage[b][:, dc * 128:(dc + 1) * 128], ident[:]),
                    deps=[t_w, pt_free[pb]], inc=("t" if dc == DC - 1 else None))
            wst_free[b] = t_tr
            done = Em("vector", lambda v, rc=rc, slot=slot, pb=pb: v.tensor_copy(
                WT[slot][:, :, rc * 128:(rc + 1) * 128],
                PT[pb].rearrange("p (c q) -> p c q", c=DC)), deps=[t_tr], inc="v")
            pt_free[pb] = done
        return done

    def load_nat(w_hbm, slot, deps_extra):
        return Em("gpsimd", lambda g, w_hbm=w_hbm, slot=slot: g.dma_start(
            out=WT[slot][:], in_=w_hbm.rearrange("(c p) k -> p c k", p=128)),
            deps=deps_extra, inc="dw", amt=16)

    def proj(dst, lhs_of, rhs_of, dep_lhs, dep_rhs, extra_mm_deps=()):
        toks = []
        g = 0
        for oc in range(DC):
            for tb in range(4):
                bank = g % 4
                t_mm = None
                for dc in range(DC):
                    t_mm = Em("tensor", lambda t, oc=oc, tb=tb, dc=dc, bank=bank: t.matmul(
                        PE[:, bank * 512:(bank + 1) * 512], lhs_of(dc, oc), rhs_of(dc, tb),
                        start=(dc == 0), stop=(dc == DC - 1)),
                        deps=[dep_lhs, dep_rhs, pe_free[bank]] + list(extra_mm_deps),
                        inc=("t" if dc == DC - 1 else None))
                t_ev = Em("vector", lambda v, oc=oc, tb=tb, bank=bank, dst=dst: v.tensor_copy(
                    dst[:, oc, tb * 512:(tb + 1) * 512], PE[:, bank * 512:(bank + 1) * 512]),
                    deps=[t_mm], inc="v")
                pe_free[bank] = t_ev
                toks.append(t_ev)
                g += 1
        return toks

    vT_last = max((t for row in vT_ev for t in row), key=lambda x: x[1])
    t_wbT = prep_wT(wb_h, 1, [])
    u_toks = []
    g = 0
    for tt in range(TT):
        for wh in range(2):
            bank = g % 4
            t_mm = None
            for vc in range(DC):
                t_mm = Em("tensor", lambda t, tt=tt, wh=wh, vc=vc, bank=bank: t.matmul(
                    PE[:, bank * 512:(bank + 1) * 512],
                    vT[:, vc, tt * 128:(tt + 1) * 128],
                    WT[1][:, vc, wh * 512:(wh + 1) * 512],
                    start=(vc == 0), stop=(vc == DC - 1)),
                    deps=[t_wbT, pe_free[bank]] + [vT_ev[oc][tt // 4] for oc in range(DC)],
                    inc=("t" if vc == DC - 1 else None))
            t_ev = Em("vector", lambda v, tt=tt, wh=wh, bank=bank: v.tensor_copy(
                u[:, tt, wh * 512:(wh + 1) * 512], PE[:, bank * 512:(bank + 1) * 512]),
                deps=[t_mm, xs_last], inc="v")
            pe_free[bank] = t_ev
            u_toks.append(t_ev)
            g += 1
    if phase_limit <= 2:
        return _finish(nc, P, taps, locals())

    t_wqT = prep_wT(wq_h, 0, [vT_last])
    qT_toks = proj(qT, lambda dc, oc: WT[0][:, dc, oc * 128:(oc + 1) * 128],
                   lambda dc, tb: xnT[:, dc, tb * 512:(tb + 1) * 512],
                   t_wqT, xnT_tok, extra_mm_deps=[u_toks[-1]])
    t_wa = load_nat(wa_h, 1, [u_toks[-1]])
    pT_toks = proj(pT, lambda qc, kb: WT[1][:, qc, kb * 128:(kb + 1) * 128],
                   lambda qc, tb: qT[:, qc, tb * 512:(tb + 1) * 512],
                   t_wa, qT_toks[-1])
    t_wkT = prep_wT(wk_h, 0, [qT_toks[-1]])
    kT_toks = proj(kT, lambda dc, oc: WT[0][:, dc, oc * 128:(oc + 1) * 128],
                   lambda dc, tb: xnT[:, dc, tb * 512:(tb + 1) * 512],
                   t_wkT, xnT_tok, extra_mm_deps=[pT_toks[-1]])
    if phase_limit <= 4:
        return _finish(nc, P, taps, locals())

    t_cn = Em("gpsimd", lambda g: g.dma_start(out=wstage[0][0:64, :], in_=cent_h),
              deps=[wst_free[0]], inc="dw", amt=16)
    t_ct = None
    for dc in range(DC):
        t_ct = Em("tensor", lambda t, dc=dc: t.transpose(
            PT[0][:, dc * 64:(dc + 1) * 64], wstage[0][0:64, dc * 128:(dc + 1) * 128],
            ident[0:64, 0:64]), deps=[t_cn, pt_free[0]], inc=("t" if dc == DC - 1 else None))
    t_centT = Em("vector", lambda v: v.tensor_copy(
        centT[:], PT[0][:, 0:512].rearrange("p (c e) -> p c e", c=DC)),
        deps=[t_ct], inc="v")
    pt_free[0] = t_centT
    wst_free[0] = t_ct
    t_bb = Em("tensor", lambda t: t.matmul(
        PS[:, 0:E], ones_sb[:], bias_sb[:], start=True, stop=True),
        deps=[t_ones, t_bias], inc="t")
    t_bbc = Em("vector", lambda v: v.tensor_copy(bias_bc[:], PS[:, 0:E]),
               deps=[t_bb], inc="v")
    ps_free = [t_bbc, None]
    prev_chain = None
    for tt in range(TT):
        pb = tt % 2
        t_mm = None
        for dc in range(DC):
            t_mm = Em("tensor", lambda t, tt=tt, dc=dc, pb=pb: t.matmul(
                PS[:, pb * 512:pb * 512 + E], xnT[:, dc, tt * 128:(tt + 1) * 128],
                centT[:, dc, :], start=(dc == 0), stop=(dc == DC - 1)),
                deps=[xnT_tile_toks[tt], t_centT, ps_free[pb]],
                inc=("t" if dc == DC - 1 else None))
        c1 = Em("vector", lambda v, pb=pb: v.tensor_copy(aff_sb[:], PS[:, pb * 512:pb * 512 + E]),
                deps=[t_mm, prev_chain], inc="v")
        ps_free[pb] = c1
        c2 = Em("vector", lambda v: v.tensor_add(biased[:], aff_sb[:], bias_bc[:]),
                deps=[c1, t_bbc], inc="v")
        c3 = Em("vector", lambda v: v.max(out=max8[:], in_=biased[:]), deps=[c2], inc="v")
        c4 = Em("vector", lambda v: v.match_replace(
            out=zap[:], in_to_replace=max8[:], in_values=biased[:], imm_value=NEG_BIG),
            deps=[c3], inc="v")
        c5 = Em("vector", lambda v: v.tensor_scalar(mask[:], zap[:], NEG_THR, None, ALU.is_le),
                deps=[c4], inc="v")
        c6 = Em("scalar", lambda sc: sc.activation(sig[:], aff_sb[:], AF.Sigmoid),
                deps=[c1, c5], inc="s")
        c7 = Em("vector", lambda v: v.tensor_mul(wm[:], sig[:], mask[:]),
                deps=[c6, c5], inc="v")
        c8 = Em("vector", lambda v: v.tensor_reduce(den1[:], wm[:], AX.X, ALU.add),
                deps=[c7], inc="v")
        c9 = Em("vector", lambda v: v.tensor_scalar(den1[:], den1[:], 1e-8, None, ALU.add),
                deps=[c8], inc="v")
        c10 = Em("vector", lambda v: v.reciprocal(rd1[:], den1[:]), deps=[c9], inc="v")
        c11 = Em("vector", lambda v, tt=tt: v.tensor_scalar(
            cw[:, tt * E:(tt + 1) * E], wm[:], rd1[:], None, ALU.mult),
            deps=[c10], inc="v")
        prev_chain = c11
    cw_done = prev_chain

    s_toks = []
    e_free = None
    ps_av_free = max(t for t in ps_free if t is not None)
    at_free = [pt_free[0], pt_free[1]]
    alpha_used = None
    nmp_free = None
    for it in range(TT):
        pm_toks = []
        for jb in range(4):
            t_mm = None
            for kb in range(DC):
                t_mm = Em("tensor", lambda t, it=it, jb=jb, kb=kb: t.matmul(
                    PE[:, jb * 512:(jb + 1) * 512],
                    pT[:, kb, it * 128:(it + 1) * 128],
                    kT[:, kb, jb * 512:(jb + 1) * 512],
                    start=(kb == 0), stop=(kb == DC - 1)),
                    deps=[pT_toks[-1], kT_toks[-1], e_free],
                    inc=("t" if kb == DC - 1 else None))
            pm_toks.append(Em("vector", lambda v, jb=jb: v.reduce_max(
                nmp[:, jb:jb + 1], PE[:, jb * 512:(jb + 1) * 512], AX.X),
                deps=[t_mm, nmp_free], inc="v"))
        t_nm = Em("vector", lambda v, it=it: v.tensor_reduce(
            negm16[:, it:it + 1], nmp[:], AX.X, ALU.max, negate=True),
            deps=pm_toks, inc="v")
        nmp_free = t_nm
        exp_toks = []
        for c in range(4):
            exp_toks.append(Em("scalar", lambda sc, it=it, c=c: sc.activation(
                alpha_sb[:, c * 512:(c + 1) * 512], PE[:, c * 512:(c + 1) * 512],
                AF.Exp, bias=negm16[:, it:it + 1]),
                deps=[t_nm, alpha_used if c == 0 else None], inc="s"))
        e_free = exp_toks[-1]
        at_evac = []
        for gb in range(2):
            t_tr = None
            for k in range(8):
                jb8 = gb * 8 + k
                t_tr = Em("tensor", lambda t, jb8=jb8, k=k, gb=gb: t.transpose(
                    PT[gb][:, k * 128:(k + 1) * 128],
                    alpha_sb[:, jb8 * 128:(jb8 + 1) * 128], ident[:]),
                    deps=[exp_toks[jb8 // 4], at_free[gb]], inc=("t" if k == 7 else None))
            t_ev = Em("vector", lambda v, gb=gb: v.tensor_copy(
                alphaT_sb[:, gb * 1024:(gb + 1) * 1024], PT[gb][:]),
                deps=[t_tr], inc="v")
            at_free[gb] = t_ev
            at_evac.append(t_ev)
        t_rs = Em("vector", lambda v, it=it: v.reduce_sum(
            rowsum16[:, it:it + 1], alpha_sb[:], AX.X), deps=[exp_toks[-1]], inc="v")
        t_rc = Em("vector", lambda v, it=it: v.reciprocal(
            recip16[:, it:it + 1], rowsum16[:, it:it + 1]), deps=[t_rs], inc="v")
        alpha_used = t_rs
        av_groups = []
        for wh in range(2):
            t_mm = None
            for j in range(16):
                t_mm = Em("tensor", lambda t, wh=wh, j=j: t.matmul(
                    PS[:, wh * 512:(wh + 1) * 512],
                    alphaT_sb[:, j * 128:(j + 1) * 128],
                    u[:, j, wh * 512:(wh + 1) * 512],
                    start=(j == 0), stop=(j == 15)),
                    deps=[at_evac[j // 8], u_toks[-1], ps_av_free],
                    inc=("t" if j == 15 else None))
            av_groups.append(t_mm)
        t_s = Em("vector", lambda v, it=it: v.tensor_scalar(
            s_sb[:, it, :], PS[:, 0:1024], recip16[:, it:it + 1], None, ALU.mult),
            deps=av_groups + [t_rc], inc="v")
        ps_av_free = t_s
        s_toks.append(t_s)
    if phase_limit <= 5:
        return _finish(nc, P, taps, locals())

    t_wsu = Em("gpsimd", lambda g: g.dma_start(
        out=Wsup[:], in_=wsup_h.rearrange("(c p) f -> p c f", p=128)),
        deps=[s_toks[-1]], inc="dw", amt=16)
    t_wsd = Em("gpsimd", lambda g: g.dma_start(
        out=Wsdn[:], in_=wsdn_h.rearrange("(c p) f -> p c f", p=128)),
        deps=[s_toks[-1]], inc="dw", amt=16)

    up_bank_free = [None] * 4
    dn_bank_free = [None, None]
    wu_free = [s_toks[-1], s_toks[-1]]
    wd_free = [s_toks[-1], s_toks[-1]]
    h_readers = []
    out_dma_toks = []
    xs6_free = [s_toks[-1], s_toks[-1]]
    os_free = [None, None, None]
    g_read = [None] * 4

    def up_pass(lhs_buf, dep_w, half, dep_hfree):
        toks = [[], []]
        g = 0
        for tb in range(TH // 512):
            for fc in range(FC):
                bank = g % 4
                t_mm = None
                for dc in range(DC):
                    t_mm = Em("tensor", lambda t, fc=fc, tb=tb, dc=dc, bank=bank,
                              lhs_buf=lhs_buf, half=half: t.matmul(
                        PE[:, bank * 512:(bank + 1) * 512],
                        lhs_buf[:, dc, fc * 128:(fc + 1) * 128],
                        xnT[:, dc, half * TH + tb * 512:half * TH + (tb + 1) * 512],
                        start=(dc == 0), stop=(dc == DC - 1)),
                        deps=[dep_w, xnT_tok, up_bank_free[bank]] + list(dep_hfree),
                        inc=("t" if dc == DC - 1 else None))
                t_h = Em("scalar", lambda sc, fc=fc, tb=tb, bank=bank: sc.activation(
                    Hbuf[:, fc, tb * 512:(tb + 1) * 512],
                    PE[:, bank * 512:(bank + 1) * 512], AF.Silu),
                    deps=[t_mm], inc="s")
                up_bank_free[bank] = t_h
                toks[tb].append(t_h)
                g += 1
        return toks

    def down_pass(rhs_buf, dep_w, half, h_toks, weighted, e=None, acc_init_toks=None,
                  acc_guards=None):
        toks = []
        last_mm = None
        g = 0
        for tl in range(TTH):
            for dh in range(2):
                bank = g % 2
                t_mm = None
                for fc in range(FC):
                    t_mm = Em("tensor", lambda t, tl=tl, dh=dh, fc=fc, bank=bank,
                              rhs_buf=rhs_buf: t.matmul(
                        PS[:, bank * 512:(bank + 1) * 512],
                        Hbuf[:, fc, tl * 128:(tl + 1) * 128],
                        rhs_buf[:, fc, dh * 512:(dh + 1) * 512],
                        start=(fc == 0), stop=(fc == FC - 1)),
                        deps=[dep_w, dn_bank_free[bank]] + h_toks[tl // 4],
                        inc=("t" if fc == FC - 1 else None))
                last_mm = t_mm
                if weighted:
                    tg = half * TTH + tl
                    t_ev = Em("vector", lambda v, tl=tl, dh=dh, bank=bank, tg=tg, e=e: v.scalar_tensor_tensor(
                        acc[:, tl, dh * 512:(dh + 1) * 512],
                        PS[:, bank * 512:(bank + 1) * 512],
                        cw[:, tg * E + e:tg * E + e + 1],
                        acc[:, tl, dh * 512:(dh + 1) * 512], ALU.mult, ALU.add),
                        deps=[t_mm, cw_done, acc_init_toks[tl]], inc="v")
                else:
                    t_ev = Em("vector", lambda v, tl=tl, dh=dh, bank=bank: v.tensor_copy(
                        acc[:, tl, dh * 512:(dh + 1) * 512],
                        PS[:, bank * 512:(bank + 1) * 512]),
                        deps=[t_mm, s_toks[-1],
                              acc_guards[tl] if acc_guards else None], inc="v")
                dn_bank_free[bank] = t_ev
                toks.append(t_ev)
                g += 1
        return toks, last_mm

    acc_guards = None
    for half in range(NH):
        sh_h = up_pass(Wsup, t_wsu, half, h_readers)
        sh_acc, last_sh_mm = down_pass(Wsdn, t_wsd, half, sh_h, weighted=False,
                                       acc_guards=acc_guards)
        h_readers = [last_sh_mm]
        acc_init = [None] * TTH
        for i, tok in enumerate(sh_acc):
            acc_init[i // 2] = tok
        for e in range(E):
            par = e % 2
            t_wu = Em("gpsimd", lambda g, e=e, par=par: g.dma_start(
                out=Wu_b[par][:], in_=wup_h[e].rearrange("(c p) f -> p c f", p=128)),
                deps=[wu_free[par]], inc="dw", amt=16)
            t_wd = Em("gpsimd", lambda g, e=e, par=par: g.dma_start(
                out=Wd_b[par][:], in_=wdn_h[e].rearrange("(c p) f -> p c f", p=128)),
                deps=[wd_free[par]], inc="dw", amt=16)
            h_toks = up_pass(Wu_b[par], t_wu, half, h_readers)
            wu_free[par] = h_toks[1][-1]
            ev_toks, last_mm = down_pass(Wd_b[par], t_wd, half, h_toks,
                                         weighted=True, e=e, acc_init_toks=acc_init)
            wd_free[par] = last_mm
            h_readers = [last_mm]
        last_ev_toks = ev_toks
        g_used = None
        t_m1 = None
        acc_guards = [None] * TTH
        for tl in range(TTH):
            tg = half * TTH + tl
            b = tl % 2
            t_dx = Em("sync", lambda s, tg=tg, b=b: s.dma_start(
                out=xstage[b][:], in_=x_h[tg * 128:(tg + 1) * 128, :]),
                deps=[xs6_free[b]], inc="dx", amt=16)
            t_g = Em("scalar", lambda sc, tl=tl: sc.activation(
                gbuf[:, tl % 4, :], acc[:, tl, :], AF.Silu),
                deps=[last_ev_toks[2 * tl + 1], g_read[tl % 4]], inc="s")
            g_used = t_g
            acc_guards[tl] = t_g
            t_m1 = Em("vector", lambda v, tl=tl, tg=tg, b=b: v.tensor_mul(
                ostage[b][:], s_sb[:, tg, :], gbuf[:, tl % 4, :]),
                deps=[t_g, s_toks[tg], os_free[b]], inc="v")
            g_read[tl % 4] = t_m1
            t_m2 = Em("vector", lambda v, b=b: v.tensor_add(
                ostage[b][:], ostage[b][:], xstage[b][:]),
                deps=[t_m1, t_dx], inc="v")
            xs6_free[b] = t_m2
            t_o = Em("sync", lambda s, tg=tg, b=b: s.dma_start(
                out=y_h[tg * 128:(tg + 1) * 128, :], in_=ostage[b][:]),
                deps=[t_m2], inc="dx", amt=16)
            os_free[b] = t_o
            out_dma_toks.append(t_o)

    Em("sync", lambda s: None, deps=[out_dma_toks[-1]])
    return _finish(nc, P, taps, locals())


def _finish(nc, P, taps, env):
    tap_outs = []
    for name, key in taps:
        t = env[key]
        shape = list(t.shape)
        dt = t.tensor.dtype if hasattr(t, "tensor") else t.dtype
        o = nc.dram_tensor(name, shape, dt, kind="ExternalOutput").ap()
        P.emit("sync", lambda s, o=o, t=t: s.dma_start(out=o, in_=t[:]),
               deps=P.final_deps(), inc="dx", amt=16)
        tap_outs.append(name)
    if tap_outs:
        P.emit("sync", lambda s: None, deps=P.final_deps())
    with ExitStack() as ctx:
        sems = {n: ctx.enter_context(nc.semaphore(n)) for n in P.sem_names}
        block = ctx.enter_context(nc.Block())
        P.build(block, sems)
    return nc


_NC_CACHE = {}


def _get_nc(phase_limit=99, taps=()):
    key = (phase_limit, tuple(taps))
    if key not in _NC_CACHE:
        _NC_CACHE[key] = build_nc(phase_limit, taps)
    return _NC_CACHE[key]


def kernel(L0_attn_x, L0_attn_Wk, L0_attn_Wq, L0_attn_Wv, L0_attn_Wa, L0_attn_Wb,
           L0_ffn_centroids, L0_ffn_expert_biases, L0_ffn_Ws_up, L0_ffn_Ws_down,
           L0_ffn_W_up, L0_ffn_W_down, _phase_limit=99, _taps=()):
    nc = _get_nc(_phase_limit, _taps)
    common = dict(
        wk=np.ascontiguousarray(L0_attn_Wk), wq=np.ascontiguousarray(L0_attn_Wq),
        wv=np.ascontiguousarray(L0_attn_Wv), wa=np.ascontiguousarray(L0_attn_Wa),
        wb=np.ascontiguousarray(L0_attn_Wb),
        cent=np.ascontiguousarray(L0_ffn_centroids),
        ebias=np.ascontiguousarray(np.reshape(L0_ffn_expert_biases, (1, E))),
        wsup=np.ascontiguousarray(L0_ffn_Ws_up), wsdn=np.ascontiguousarray(L0_ffn_Ws_down),
        wup=np.ascontiguousarray(L0_ffn_W_up), wdn=np.ascontiguousarray(L0_ffn_W_down),
    )
    x = np.ascontiguousarray(L0_attn_x)
    in_maps = [dict(common, x=np.ascontiguousarray(x[c])) for c in range(8)]
    res = run_bass_kernel_spmd(nc, in_maps, core_ids=list(range(8)))
    if _taps or _phase_limit < 99:
        return res
    return np.stack([r["y"] for r in res.results], axis=0)


# version 13
# speedup vs baseline: 1.2090x; 1.1550x over previous
import sys
if "/opt/trn_rl_repo" not in sys.path:
    sys.path.insert(0, "/opt/trn_rl_repo")
from collections import defaultdict
from contextlib import ExitStack

import numpy as np
import ml_dtypes
import concourse.bass as bass
import concourse.mybir as mybir
from concourse.bass_utils import run_bass_kernel_spmd
from concourse.bass_core_rust import IndirectOffsetOnAxis

F32 = mybir.dt.float32
BF16 = mybir.dt.bfloat16
AF = mybir.ActivationFunctionType
ALU = mybir.AluOpType
AX = mybir.AxisListType

ENGINES = ("sync", "gpsimd", "tensor", "vector", "scalar")

S, D, E, FF = 2048, 1024, 64, 512
TT = S // 128
DC = D // 128
FC = FF // 128
NH = 2
TH = S // NH
TTH = TH // 128
NEG_BIG = -1e30
NEG_THR = -1e29
CAP = 512
CAPH = CAP // 2
BIGK = 65536.0


class Prog:
    def __init__(self):
        self.ops = {e: [] for e in ENGINES}
        self.sem_names = []
        self.counts = defaultdict(int)

    def sem(self, name):
        if name not in self.sem_names:
            self.sem_names.append(name)
        return name

    PARANOID = False

    def emit(self, engine, fn, deps=(), inc=None, amt=1):
        tok = None
        deps = [d for d in deps if d is not None]
        if Prog.PARANOID:
            deps = [(s, v) for s, v in self.counts.items()]
        if inc is not None:
            self.sem(inc)
            if amt == 16 and self.counts[inc] > 0:
                deps.append((inc, self.counts[inc]))
            self.counts[inc] += amt
            tok = (inc, self.counts[inc])
        self.ops[engine].append((fn, deps, inc, amt))
        return tok

    def final_deps(self):
        return [(s, self.counts[s]) for s in self.sem_names]

    def build(self, block, sems):
        for e in ENGINES:
            if not self.ops[e]:
                continue

            def make(e):
                def f(eng):
                    hw = defaultdict(int)
                    for fn, deps, inc, amt in self.ops[e]:
                        for (s, v) in deps:
                            if v > hw[s]:
                                eng.wait_ge(sems[s], v)
                                hw[s] = v
                        inst = fn(eng)
                        if inc is not None:
                            if inst is None:
                                inst = eng.nop()
                            inst.then_inc(sems[inc], amt)
                return f

            getattr(block, e)(make(e))


def build_nc(phase_limit=99, taps=()):
    nc = bass.Bass("TRN2", target_bir_lowering=False)
    x_h = nc.dram_tensor("x", [S, D], F32, kind="ExternalInput").ap()
    wk_h = nc.dram_tensor("wk", [D, D], F32, kind="ExternalInput").ap()
    wq_h = nc.dram_tensor("wq", [D, D], F32, kind="ExternalInput").ap()
    wv_h = nc.dram_tensor("wv", [D, D], F32, kind="ExternalInput").ap()
    wa_h = nc.dram_tensor("wa", [D, D], F32, kind="ExternalInput").ap()
    wb_h = nc.dram_tensor("wb", [D, D], F32, kind="ExternalInput").ap()
    cent_h = nc.dram_tensor("cent", [E, D], F32, kind="ExternalInput").ap()
    ebias_h = nc.dram_tensor("ebias", [1, E], F32, kind="ExternalInput").ap()
    wsup_h = nc.dram_tensor("wsup", [D, FF], F32, kind="ExternalInput").ap()
    wsdn_h = nc.dram_tensor("wsdn", [FF, D], F32, kind="ExternalInput").ap()
    wup_h = nc.dram_tensor("wup", [E, D, FF], F32, kind="ExternalInput").ap()
    wdn_h = nc.dram_tensor("wdn", [E, FF, D], F32, kind="ExternalInput").ap()
    y_h = nc.dram_tensor("y", [S, D], F32, kind="ExternalOutput").ap()
    xnd_h = nc.dram_tensor("xnd", [S, D], BF16, kind="Internal").ap()
    sdr_h = nc.dram_tensor("sdr", [S, D], BF16, kind="Internal").ap()
    ygd_h = nc.dram_tensor("ygd", [E * CAP, D], BF16, kind="Internal").ap()

    ident_hbm = nc.inline_tensor(np.eye(128, dtype=ml_dtypes.bfloat16), name="identc").ap()
    ones_hbm = nc.inline_tensor(np.ones((1, 128), dtype=np.float32), name="onesc").ap()
    iota64_hbm = nc.inline_tensor(np.arange(E, dtype=np.float32).reshape(1, E), name="iotac").ap()
    identF_hbm = nc.inline_tensor(np.eye(128, dtype=np.float32), name="identfc").ap()

    ARENA_BYTES = 204800
    arena = nc.alloc_sbuf_tensor("ARENA", [128, ARENA_BYTES], mybir.dt.uint8)
    SBUF_BASE = nc.lookup_mloc(arena).addr

    def at(name, off, shape, dt):
        assert 0 <= off and off < ARENA_BYTES
        return nc.alloc_sbuf_tensor_at(name, shape, dt, offset=SBUF_BASE + off).ap()

    xnT = at("xnT", 0, [128, DC, S], BF16)
    cw = at("cw", 32768, [128, TT * E], F32)
    co = 36864
    ident = at("ident", co + 0, [128, 128], BF16); co += 256
    centT = at("centT", co, [128, DC, E], BF16); co += 1024
    bias_sb = at("bias_sb", co, [1, E], F32); co += 256
    bias_bc = at("bias_bc", co, [128, E], F32); co += 256
    ones_sb = at("ones_sb", co, [1, 128], F32); co += 512
    aff_sb = at("aff_sb", co, [128, E], F32); co += 256
    biased = at("biased", co, [128, E], F32); co += 256
    max8 = at("max8", co, [128, 8], F32); co += 32
    zap = at("zap", co, [128, E], F32); co += 256
    mask = at("mask", co, [128, E], F32); co += 256
    sig = at("sig", co, [128, E], F32); co += 256
    wm = at("wm", co, [128, E], F32); co += 256
    st6 = at("st6", co, [128, 12], F32); co += 64
    mv2 = at("mv2", co, [128, 2 * TT], F32); co += 128
    nmp = at("nmp", co, [128, 4], F32); co += 32
    epsv = at("epsv", co, [128, 1], F32); co += 32
    rcp16 = at("rcp16", co, [128, TT], F32); co += 64
    rstd16 = at("rstd16", co, [128, TT], F32); co += 64
    negm16 = at("negm16", co, [128, TT], F32); co += 64
    rowsum16 = at("rowsum16", co, [128, TT], F32); co += 64
    recip16 = at("recip16", co, [128, TT], F32); co += 64
    den1 = at("den1", co, [128, 1], F32); co += 64
    rd1 = at("rd1", co, [128, 1], F32); co += 64
    iota_sb = at("iota_sb", co, [1, E], F32); co += 256
    erow = at("erow", co, [128, E], F32); co += 256
    identF = at("identF", co, [128, 128], F32); co += 512
    assert co <= 45056, co
    MKb = at("MKb", 196608, [128, TT * E], BF16)
    FLU = at("FLU", 198656, [128, TT * 8], mybir.dt.uint32)
    WJ = at("WJ", 199168, [128, TT * 8], F32)
    IXT = at("IXT", 199680, [128, 4 * E], mybir.dt.uint32)
    MT = at("MT", 110592, [E, S], F32)
    CUM = at("CUM", 110592 + 8192, [E, S], F32)
    VALS = at("VALS", 110592 + 16384, [E, TH], F32)
    IDXL = at("IDXL", 110592 + 20480, [E, CAP], F32)
    RAMP = at("RAMP", 110592 + 22528, [E, TH], F32)
    EXV = at("EXV", 110592 + 26624, [E, 8], F32)
    ZROW = at("ZROW", 110592 + 26656, [E, TH], F32)
    PTM = at("PTM", 176128, [128, TT * E], F32)
    XG = [at(f"XG{i}", 0 + i * 8192, [128, 4, D], BF16) for i in range(2)]
    XGT = [at(f"XGT{i}", 16384 + i * 8192, [128, DC, CAP], BF16) for i in range(2)]
    Hg = at("Hg", 176128, [128, FC, CAP], BF16)
    gacc = [at(f"gacc{i}", 180224 + i * 2048, [128, D], BF16) for i in range(2)]
    gsil = [at(f"gsil{i}", 176128 + i * 2048, [128, D], BF16) for i in range(2)]
    Ygst = [at(f"Ygst{i}", 184320 + i * 4096, [128, 2, D], BF16) for i in range(2)]
    accG = at("accG", 143360, [128, TTH, D], F32)
    Hsh = at("Hsh", 176128, [128, FC, TH], BF16)
    u = at("u", 45056, [128, TT, D], BF16)
    Wu_b = [at(f"Wu{i}", 45056 + i * 8192, [128, DC, FF], BF16) for i in range(2)]
    Wd_b = [at(f"Wd{i}", 45056 + 16384 + i * 8192, [128, FC, D], BF16) for i in range(2)]
    vT = at("vT", 77824, [128, DC, S], BF16)
    qT = at("qT", 77824, [128, DC, S], BF16)
    kT = at("kT", 77824, [128, DC, S], BF16)
    acc = at("acc", 77824, [128, TTH, D], F32)
    WT = [at(f"WT{i}", 110592 + i * 16384, [128, DC, D], BF16) for i in range(2)]
    sstg = [at(f"sstg{i}", 200704 + i * 2048, [128, D], BF16) for i in range(2)]
    pT = at("pT", 143360, [128, DC, S], BF16)
    wstage = [at(f"wstage{i}", 192512 + i * 2048, [128, D], BF16) for i in range(2)]
    alpha_sb = at("alpha_sb", 176128, [128, S], BF16)
    alphaT_sb = at("alphaT_sb", 176128 + 4096, [128, S], BF16)
    xn_t2 = [at(f"xn_t{i}", 184320 + i * 2048, [128, D], BF16) for i in range(2)]
    sqf = at("sqf", 184320 + 4096, [128, D], F32)
    ostage = [at(f"ostage{i}", 184320 + i * 4096, [128, D], F32) for i in range(3)]

    PE = nc.alloc_psum_tensor("PE", [128, 2048], F32).ap()
    PS = nc.alloc_psum_tensor("PS", [128, 1024], F32).ap()
    PT = [nc.alloc_psum_tensor(f"PT{i}", [128, 1024], BF16).ap() for i in range(2)]

    P = Prog()
    Em = P.emit

    LN_ONLY = (phase_limit == -1)
    if not LN_ONLY:
        t_id = Em("sync", lambda s: s.dma_start(out=ident[:], in_=ident_hbm), inc="dx", amt=16)
        t_ones = Em("sync", lambda s: s.dma_start(out=ones_sb[:], in_=ones_hbm), inc="dx", amt=16)
        t_bias = Em("sync", lambda s: s.dma_start(out=bias_sb[:], in_=ebias_h), inc="dx", amt=16)
    else:
        t_id = t_ones = t_bias = None

    t_eps = Em("vector", lambda v: v.memset(epsv[:], 1e-5), inc="v")
    xs_free = [None, None]
    pt_free = [None, None]
    xn_used = [[], []]
    xnd_toks = []
    wst_free = [None, None]
    pe_free = [None] * 4
    sq_done = None
    xnT_tok = None
    xnT_tile_toks = [None] * TT
    wv_tok = [None] * DC
    vT_ev = [[None] * 4 for _ in range(DC)]
    vT_emitted = set()
    vg = [0]

    def emit_vT_group(oc, tb):
        bank = vg[0] % 4
        vg[0] += 1
        t_mm = None
        for dc in range(DC):
            t_mm = Em("tensor", lambda t, oc=oc, tb=tb, dc=dc, bank=bank: t.matmul(
                PE[:, bank * 512:(bank + 1) * 512],
                WT[0][:, dc, oc * 128:(oc + 1) * 128],
                xnT[:, dc, tb * 512:(tb + 1) * 512],
                start=(dc == 0), stop=(dc == DC - 1)),
                deps=[wv_tok[oc], xnT_tile_toks[4 * tb + 3], pe_free[bank]],
                inc=("t" if dc == DC - 1 else None))
        t_ev = Em("scalar", lambda sc, oc=oc, tb=tb, bank=bank: sc.copy(
            vT[:, oc, tb * 512:(tb + 1) * 512], PE[:, bank * 512:(bank + 1) * 512]),
            deps=[t_mm], inc="s")
        pe_free[bank] = t_ev
        vT_ev[oc][tb] = t_ev
        vT_emitted.add((oc, tb))

    def emit_ready_vT(tt):
        for tb in range(4):
            if 4 * tb + 3 > tt:
                break
            for oc in range(DC):
                if 2 * oc + 1 <= tt and (oc, tb) not in vT_emitted:
                    emit_vT_group(oc, tb)

    xp = [at(f"xp{i}", 45056 + i * 8192, [128, 2, D], F32) for i in range(2)]
    e4s = [None] * TT
    ags = [None] * TT
    xsrcs = [None] * TT

    def p0_apply(tt):
        nonlocal xnT_tok
        b = (tt // 2) % 2
        xn_t = xn_t2[tt % 2]
        xsrc = xsrcs[tt]
        t_f = Em("vector", lambda v, tt=tt, xsrc=xsrc, xn_t=xn_t: v.tensor_scalar(
            xn_t[:], xsrc[:], mv2[:, 2 * tt:2 * tt + 1], rstd16[:, tt:tt + 1],
            ALU.subtract, ALU.mult), deps=[e4s[tt], ags[tt]] + list(xn_used[tt % 2]), inc="v")
        xs_free[b] = t_f
        t_tr = None
        for dc in range(DC):
            t_tr = Em("tensor", lambda t, dc=dc, xn_t=xn_t: t.transpose(
                PT[0][:, dc * 128:(dc + 1) * 128], xn_t[:, dc * 128:(dc + 1) * 128],
                ident[:]), deps=[t_f, t_id, pt_free[0]],
                inc=("t" if dc == DC - 1 else None))
        t_xnd = Em("sync", lambda s, tt=tt, xn_t=xn_t: s.dma_start(
            out=xnd_h[tt * 128:(tt + 1) * 128, :], in_=xn_t[:]),
            deps=[t_f], inc="dn", amt=16)
        xnd_toks.append(t_xnd)
        xn_used[tt % 2] = [t_tr, t_xnd]
        for dc in range(DC):
            xnT_tok = Em("scalar", lambda sc, tt=tt, dc=dc: sc.copy(
                xnT[:, dc, tt * 128:(tt + 1) * 128],
                PT[0][:, dc * 128:(dc + 1) * 128]), deps=[t_tr], inc="s")
        pt_free[0] = xnT_tok
        xnT_tile_toks[tt] = xnT_tok
        if tt % 2 == 1:
            rc = (tt - 1) // 2
            wb_ = rc % 2
            t_w = Em("gpsimd", lambda g, rc=rc, wb_=wb_: g.dma_start(
                out=wstage[wb_][:], in_=wv_h[rc * 128:(rc + 1) * 128, :]),
                deps=[wst_free[wb_]], inc="dw", amt=16)
            t_wtr = None
            for dc in range(DC):
                t_wtr = Em("tensor", lambda t, dc=dc, wb_=wb_: t.transpose(
                    PT[1][:, dc * 128:(dc + 1) * 128],
                    wstage[wb_][:, dc * 128:(dc + 1) * 128], ident[:]),
                    deps=[t_w, t_id, pt_free[1]], inc=("t" if dc == DC - 1 else None))
            wst_free[wb_] = t_wtr
            wv_tok[rc] = Em("scalar", lambda sc, rc=rc: sc.copy(
                WT[0][:, :, rc * 128:(rc + 1) * 128],
                PT[1].rearrange("p (c q) -> p c q", c=DC)), deps=[t_wtr], inc="s")
            pt_free[1] = wv_tok[rc]
        emit_ready_vT(tt)

    t_dx = None
    for tt in range(TT + 1):
        if tt < TT:
            b = (tt // 2) % 2
            if tt % 2 == 0:
                q = tt // 2
                t_dx = Em("sync", lambda s, q=q, b=b: s.dma_start(
                    out=xp[b][:], in_=x_h[q * 256:(q + 1) * 256, :].rearrange(
                        "(c p) d -> p c d", p=128)),
                    deps=[xs_free[b]], inc="dx", amt=16)
            xsrcs[tt] = xp[b][:, tt % 2, :]
            t_b1 = Em("vector", lambda v, tt=tt: v.bn_stats(
                st6[:, 0:6], xsrcs[tt][:, 0:512]), deps=[t_dx, sq_done], inc="v")
            t_b2 = Em("vector", lambda v, tt=tt: v.bn_stats(
                st6[:, 6:12], xsrcs[tt][:, 512:1024]), deps=[t_b1], inc="v")
            t_ag = Em("vector", lambda v, tt=tt: v.bn_aggr(
                mv2[:, 2 * tt:2 * tt + 2], st6[:]), deps=[t_b2], inc="v")
            sq_done = t_ag
            ags[tt] = t_ag
            t_e3 = Em("scalar", lambda sc, tt=tt: sc.activation(
                rcp16[:, tt:tt + 1], mv2[:, 2 * tt + 1:2 * tt + 2], AF.Ln, bias=epsv[:]),
                deps=[t_ag, t_eps], inc="s")
            e4s[tt] = Em("scalar", lambda sc, tt=tt: sc.activation(
                rstd16[:, tt:tt + 1], rcp16[:, tt:tt + 1], AF.Exp, scale=-0.5),
                deps=[t_e3], inc="s")
        if not LN_ONLY and tt >= 1:
            p0_apply(tt - 1)
    if LN_ONLY:
        return _finish(nc, P, taps, locals())
    emit_ready_vT(99)
    xs_last = max(t for t in xs_free if t is not None)
    t_wvT = wv_tok[-1]
    vT_toks = [vT_ev[oc][tb] for oc in range(DC) for tb in range(4)]
    if phase_limit <= 0:
        return _finish(nc, P, taps, locals())

    def prep_wT(w_hbm, slot, deps_extra):
        nonlocal wst_free
        done = None
        for rc in range(DC):
            b = rc % 2
            t_w = Em("gpsimd", lambda g, rc=rc, b=b, w_hbm=w_hbm: g.dma_start(
                out=wstage[b][:], in_=w_hbm[rc * 128:(rc + 1) * 128, :]),
                deps=[wst_free[b]] + deps_extra, inc="dw", amt=16)
            pb = rc % 2
            t_tr = None
            for dc in range(DC):
                t_tr = Em("tensor", lambda t, rc=rc, dc=dc, b=b, pb=pb: t.transpose(
                    PT[pb][:, dc * 128:(dc + 1) * 128],
                    wstage[b][:, dc * 128:(dc + 1) * 128], ident[:]),
                    deps=[t_w, pt_free[pb]], inc=("t" if dc == DC - 1 else None))
            wst_free[b] = t_tr
            done = Em("vector", lambda v, rc=rc, slot=slot, pb=pb: v.tensor_copy(
                WT[slot][:, :, rc * 128:(rc + 1) * 128],
                PT[pb].rearrange("p (c q) -> p c q", c=DC)), deps=[t_tr], inc="v")
            pt_free[pb] = done
        return done

    def load_nat(w_hbm, slot, deps_extra):
        return Em("gpsimd", lambda g, w_hbm=w_hbm, slot=slot: g.dma_start(
            out=WT[slot][:], in_=w_hbm.rearrange("(c p) k -> p c k", p=128)),
            deps=deps_extra, inc="dw", amt=16)

    def proj(dst, lhs_of, rhs_of, dep_lhs, dep_rhs, extra_mm_deps=()):
        toks = []
        g = 0
        for oc in range(DC):
            for tb in range(4):
                bank = g % 4
                t_mm = None
                for dc in range(DC):
                    t_mm = Em("tensor", lambda t, oc=oc, tb=tb, dc=dc, bank=bank: t.matmul(
                        PE[:, bank * 512:(bank + 1) * 512], lhs_of(dc, oc), rhs_of(dc, tb),
                        start=(dc == 0), stop=(dc == DC - 1)),
                        deps=[dep_lhs, dep_rhs, pe_free[bank]] + list(extra_mm_deps),
                        inc=("t" if dc == DC - 1 else None))
                t_ev = Em("vector", lambda v, oc=oc, tb=tb, bank=bank, dst=dst: v.tensor_copy(
                    dst[:, oc, tb * 512:(tb + 1) * 512], PE[:, bank * 512:(bank + 1) * 512]),
                    deps=[t_mm], inc="v")
                pe_free[bank] = t_ev
                toks.append(t_ev)
                g += 1
        return toks

    vT_last = max((t for row in vT_ev for t in row), key=lambda x: x[1])
    t_wbT = prep_wT(wb_h, 1, [])
    u_toks = []
    g = 0
    for tt in range(TT):
        for wh in range(2):
            bank = g % 4
            t_mm = None
            for vc in range(DC):
                t_mm = Em("tensor", lambda t, tt=tt, wh=wh, vc=vc, bank=bank: t.matmul(
                    PE[:, bank * 512:(bank + 1) * 512],
                    vT[:, vc, tt * 128:(tt + 1) * 128],
                    WT[1][:, vc, wh * 512:(wh + 1) * 512],
                    start=(vc == 0), stop=(vc == DC - 1)),
                    deps=[t_wbT, pe_free[bank]] + [vT_ev[oc][tt // 4] for oc in range(DC)],
                    inc=("t" if vc == DC - 1 else None))
            t_ev = Em("vector", lambda v, tt=tt, wh=wh, bank=bank: v.tensor_copy(
                u[:, tt, wh * 512:(wh + 1) * 512], PE[:, bank * 512:(bank + 1) * 512]),
                deps=[t_mm, xs_last], inc="v")
            pe_free[bank] = t_ev
            u_toks.append(t_ev)
            g += 1
    if phase_limit <= 2:
        return _finish(nc, P, taps, locals())

    t_wqT = prep_wT(wq_h, 0, [vT_last])
    qT_toks = proj(qT, lambda dc, oc: WT[0][:, dc, oc * 128:(oc + 1) * 128],
                   lambda dc, tb: xnT[:, dc, tb * 512:(tb + 1) * 512],
                   t_wqT, xnT_tok, extra_mm_deps=[u_toks[-1]])
    t_wa = load_nat(wa_h, 1, [u_toks[-1]])
    pT_toks = proj(pT, lambda qc, kb: WT[1][:, qc, kb * 128:(kb + 1) * 128],
                   lambda qc, tb: qT[:, qc, tb * 512:(tb + 1) * 512],
                   t_wa, qT_toks[-1])
    t_wkT = prep_wT(wk_h, 0, [qT_toks[-1]])
    kT_toks = proj(kT, lambda dc, oc: WT[0][:, dc, oc * 128:(oc + 1) * 128],
                   lambda dc, tb: xnT[:, dc, tb * 512:(tb + 1) * 512],
                   t_wkT, xnT_tok, extra_mm_deps=[pT_toks[-1]])
    if phase_limit <= 4:
        return _finish(nc, P, taps, locals())

    t_cn = Em("gpsimd", lambda g: g.dma_start(out=wstage[0][0:64, :], in_=cent_h),
              deps=[wst_free[0]], inc="dw", amt=16)
    t_ct = None
    for dc in range(DC):
        t_ct = Em("tensor", lambda t, dc=dc: t.transpose(
            PT[0][:, dc * 64:(dc + 1) * 64], wstage[0][0:64, dc * 128:(dc + 1) * 128],
            ident[0:64, 0:64]), deps=[t_cn, pt_free[0]], inc=("t" if dc == DC - 1 else None))
    t_centT = Em("vector", lambda v: v.tensor_copy(
        centT[:], PT[0][:, 0:512].rearrange("p (c e) -> p c e", c=DC)),
        deps=[t_ct], inc="v")
    pt_free[0] = t_centT
    wst_free[0] = t_ct
    t_bb = Em("tensor", lambda t: t.matmul(
        PS[:, 0:E], ones_sb[:], bias_sb[:], start=True, stop=True),
        deps=[t_ones, t_bias], inc="t")
    t_bbc = Em("vector", lambda v: v.tensor_copy(bias_bc[:], PS[:, 0:E]),
               deps=[t_bb], inc="v")
    t_io = Em("sync", lambda s: s.dma_start(out=iota_sb[:], in_=iota64_hbm), inc="dx", amt=16)
    t_if = Em("sync", lambda s: s.dma_start(out=identF[:], in_=identF_hbm), inc="dx", amt=16)
    t_be = Em("tensor", lambda t: t.matmul(
        PS[:, 512:512 + E], ones_sb[:], iota_sb[:], start=True, stop=True),
        deps=[t_ones, t_io], inc="t")
    t_erow = Em("vector", lambda v: v.tensor_copy(erow[:], PS[:, 512:512 + E]),
                deps=[t_be], inc="v")
    ps_free = [t_bbc, t_erow]
    prev_chain = None
    for tt in range(TT):
        pb = tt % 2
        t_mm = None
        for dc in range(DC):
            t_mm = Em("tensor", lambda t, tt=tt, dc=dc, pb=pb: t.matmul(
                PS[:, pb * 512:pb * 512 + E], xnT[:, dc, tt * 128:(tt + 1) * 128],
                centT[:, dc, :], start=(dc == 0), stop=(dc == DC - 1)),
                deps=[xnT_tile_toks[tt], t_centT, ps_free[pb]],
                inc=("t" if dc == DC - 1 else None))
        c1 = Em("vector", lambda v, pb=pb: v.tensor_copy(aff_sb[:], PS[:, pb * 512:pb * 512 + E]),
                deps=[t_mm, prev_chain], inc="v")
        ps_free[pb] = c1
        c2 = Em("vector", lambda v: v.tensor_add(biased[:], aff_sb[:], bias_bc[:]),
                deps=[c1, t_bbc], inc="v")
        c3 = Em("vector", lambda v: v.max(out=max8[:], in_=biased[:]), deps=[c2], inc="v")
        c4 = Em("vector", lambda v: v.match_replace(
            out=zap[:], in_to_replace=max8[:], in_values=biased[:], imm_value=NEG_BIG),
            deps=[c3], inc="v")
        c5 = Em("vector", lambda v: v.tensor_scalar(mask[:], zap[:], NEG_THR, None, ALU.is_le),
                deps=[c4], inc="v")
        c6 = Em("scalar", lambda sc: sc.activation(sig[:], aff_sb[:], AF.Sigmoid),
                deps=[c1, c5], inc="s")
        c7 = Em("vector", lambda v: v.tensor_mul(wm[:], sig[:], mask[:]),
                deps=[c6, c5], inc="v")
        c8 = Em("vector", lambda v: v.tensor_reduce(den1[:], wm[:], AX.X, ALU.add),
                deps=[c7], inc="v")
        c9 = Em("vector", lambda v: v.tensor_scalar(den1[:], den1[:], 1e-8, None, ALU.add),
                deps=[c8], inc="v")
        c10 = Em("vector", lambda v: v.reciprocal(rd1[:], den1[:]), deps=[c9], inc="v")
        c11 = Em("vector", lambda v, tt=tt: v.tensor_scalar(
            cw[:, tt * E:(tt + 1) * E], wm[:], rd1[:], None, ALU.mult),
            deps=[c10], inc="v")
        c12 = Em("vector", lambda v, tt=tt: v.tensor_copy(
            MKb[:, tt * E:(tt + 1) * E], mask[:]), deps=[c5], inc="v")
        pbm = tt % 2
        t_mt = Em("tensor", lambda t, tt=tt, pbm=pbm: t.transpose(
            PT[pbm][0:64, 0:128], MKb[:, tt * E:(tt + 1) * E], ident[:]),
            deps=[c12, pt_free[pbm]], inc="t")
        t_mte = Em("vector", lambda v, tt=tt, pbm=pbm: v.tensor_copy(
            MT[:, tt * 128:(tt + 1) * 128], PT[pbm][0:64, 0:128]),
            deps=[t_mt], inc="v")
        pt_free[pbm] = t_mte
        prev_chain = t_mte
    cw_done = prev_chain

    t_zr = Em("vector", lambda v: v.memset(ZROW[:], 0.0), deps=[cw_done], inc="v")
    cum_toks = []
    for h in range(NH):
        cum_toks.append(Em("vector", lambda v, h=h: v.tensor_tensor_scan(
            CUM[:, h * TH:(h + 1) * TH], MT[:, h * TH:(h + 1) * TH], ZROW[:],
            0.0, ALU.add, ALU.add), deps=[t_zr, cw_done], inc="v"))
    t_r1 = Em("vector", lambda v: v.memset(VALS[:], 1.0), deps=[cum_toks[-1]], inc="v")
    t_r2 = Em("vector", lambda v: v.tensor_tensor_scan(
        RAMP[:], VALS[:], ZROW[:], 0.0, ALU.add, ALU.add), deps=[t_r1], inc="v")
    cumsum_done = t_r2

    ext_state = {"h": -1, "r": 32, "src": 0, "prep": None, "last": cumsum_done}
    ext_bufs = None

    def emit_extract_some(n):
        for _ in range(n):
            st = ext_state
            if st["r"] >= CAPH // 8:
                if st["h"] >= NH - 1:
                    return
                st["h"] += 1
                h = st["h"]
                t_v1 = Em("vector", lambda v: v.tensor_scalar(
                    VALS[:], RAMP[:], -1.0, float(TH + 1), ALU.mult, ALU.add),
                    deps=[st["last"], cumsum_done], inc="v")
                t_v2 = Em("vector", lambda v, h=h: v.tensor_mul(
                    VALS[:], VALS[:], MT[:, h * TH:(h + 1) * TH]),
                    deps=[t_v1], inc="v")
                st["last"] = t_v2
                st["r"] = 0
                st["src"] = 0
            h, r = st["h"], st["r"]
            srcb = VALS if st["src"] == 0 else MT[:, h * TH:(h + 1) * TH]
            dstb = MT[:, h * TH:(h + 1) * TH] if st["src"] == 0 else VALS
            t_m = Em("vector", lambda v, srcb=srcb: v.max(out=EXV[:], in_=srcb),
                     deps=[st["last"]], inc="v")
            t_z = Em("vector", lambda v, srcb=srcb, dstb=dstb: v.match_replace(
                out=dstb, in_to_replace=EXV[:], in_values=srcb, imm_value=0.0),
                deps=[t_m], inc="v")
            t_i = Em("vector", lambda v, h=h, r=r: v.tensor_scalar(
                IDXL[:, h * CAPH + 8 * r:h * CAPH + 8 * r + 8], EXV[:],
                -1.0, float(TH + h * TH), ALU.mult, ALU.add),
                deps=[t_m], inc="v")
            t_i2 = Em("vector", lambda v, h=h, r=r: v.tensor_scalar_min(
                IDXL[:, h * CAPH + 8 * r:h * CAPH + 8 * r + 8],
                IDXL[:, h * CAPH + 8 * r:h * CAPH + 8 * r + 8], float(S - 1)),
                deps=[t_i], inc="v")
            st["last"] = t_z
            st["r"] += 1
            st["src"] ^= 1

    s_toks = []
    s_wr_toks = []
    sstg_free = [None, None]
    e_free = None
    ps_av_free = max(t for t in ps_free if t is not None)
    at_free = [pt_free[0], pt_free[1]]
    alpha_used = None
    nmp_free = None
    for it in range(TT):
        pm_toks = []
        for jb in range(4):
            t_mm = None
            for kb in range(DC):
                t_mm = Em("tensor", lambda t, it=it, jb=jb, kb=kb: t.matmul(
                    PE[:, jb * 512:(jb + 1) * 512],
                    pT[:, kb, it * 128:(it + 1) * 128],
                    kT[:, kb, jb * 512:(jb + 1) * 512],
                    start=(kb == 0), stop=(kb == DC - 1)),
                    deps=[pT_toks[-1], kT_toks[-1], e_free],
                    inc=("t" if kb == DC - 1 else None))
            pm_toks.append(Em("vector", lambda v, jb=jb: v.reduce_max(
                nmp[:, jb:jb + 1], PE[:, jb * 512:(jb + 1) * 512], AX.X),
                deps=[t_mm, nmp_free], inc="v"))
        t_nm = Em("vector", lambda v, it=it: v.tensor_reduce(
            negm16[:, it:it + 1], nmp[:], AX.X, ALU.max, negate=True),
            deps=pm_toks, inc="v")
        nmp_free = t_nm
        exp_toks = []
        for c in range(4):
            exp_toks.append(Em("scalar", lambda sc, it=it, c=c: sc.activation(
                alpha_sb[:, c * 512:(c + 1) * 512], PE[:, c * 512:(c + 1) * 512],
                AF.Exp, bias=negm16[:, it:it + 1]),
                deps=[t_nm, alpha_used if c == 0 else None], inc="s"))
        e_free = exp_toks[-1]
        at_evac = []
        for gb in range(2):
            t_tr = None
            for k in range(8):
                jb8 = gb * 8 + k
                t_tr = Em("tensor", lambda t, jb8=jb8, k=k, gb=gb: t.transpose(
                    PT[gb][:, k * 128:(k + 1) * 128],
                    alpha_sb[:, jb8 * 128:(jb8 + 1) * 128], ident[:]),
                    deps=[exp_toks[jb8 // 4], at_free[gb]], inc=("t" if k == 7 else None))
            t_ev = Em("vector", lambda v, gb=gb: v.tensor_copy(
                alphaT_sb[:, gb * 1024:(gb + 1) * 1024], PT[gb][:]),
                deps=[t_tr], inc="v")
            at_free[gb] = t_ev
            at_evac.append(t_ev)
        t_rs = Em("vector", lambda v, it=it: v.reduce_sum(
            rowsum16[:, it:it + 1], alpha_sb[:], AX.X), deps=[exp_toks[-1]], inc="v")
        t_rc = Em("vector", lambda v, it=it: v.reciprocal(
            recip16[:, it:it + 1], rowsum16[:, it:it + 1]), deps=[t_rs], inc="v")
        alpha_used = t_rs
        emit_extract_some(5)
        av_groups = []
        for wh in range(2):
            t_mm = None
            for j in range(16):
                t_mm = Em("tensor", lambda t, wh=wh, j=j: t.matmul(
                    PS[:, wh * 512:(wh + 1) * 512],
                    alphaT_sb[:, j * 128:(j + 1) * 128],
                    u[:, j, wh * 512:(wh + 1) * 512],
                    start=(j == 0), stop=(j == 15)),
                    deps=[at_evac[j // 8], u_toks[-1], ps_av_free],
                    inc=("t" if j == 15 else None))
            av_groups.append(t_mm)
        t_s = Em("vector", lambda v, it=it: v.tensor_scalar(
            sstg[it % 2][:], PS[:, 0:1024], recip16[:, it:it + 1], None, ALU.mult),
            deps=av_groups + [t_rc, sstg_free[it % 2]], inc="v")
        ps_av_free = t_s
        t_sw = Em("sync", lambda s, it=it: s.dma_start(
            out=sdr_h[it * 128:(it + 1) * 128, :], in_=sstg[it % 2][:]),
            deps=[t_s], inc="dy", amt=16)
        sstg_free[it % 2] = t_sw
        s_wr_toks.append(t_sw)
        s_toks.append(t_s)
    if phase_limit <= 5:
        return _finish(nc, P, taps, locals())

    emit_extract_some(200)
    ixr = ext_state["last"]
    ixt_toks = []
    ps6_free = [s_toks[-1], s_toks[-1]]
    for c in range(4):
        pb6 = c % 2
        t_tp = Em("tensor", lambda t, c=c, pb6=pb6: t.transpose(
            PS[0:128, pb6 * 512:pb6 * 512 + 64], IDXL[:, c * 128:(c + 1) * 128],
            identF[0:64, 0:64]), deps=[ixr, ps6_free[pb6], t_if], inc="t")
        t_ix = Em("vector", lambda v, c=c, pb6=pb6: v.tensor_copy(
            IXT[:, c * E:(c + 1) * E], PS[:, pb6 * 512:pb6 * 512 + 64]),
            deps=[t_tp], inc="v")
        ps6_free[pb6] = t_ix
        ixt_toks.append(t_ix)

    def acc_ap(g):
        return acc[:, g, :] if g < TTH else accG[:, g - TTH, :]

    t_wsu = Em("gpsimd", lambda g: g.dma_start(
        out=Wu_b[0][:], in_=wsup_h.rearrange("(c p) f -> p c f", p=128)),
        deps=[s_toks[-1]], inc="dw", amt=16)
    t_wsd = Em("gpsimd", lambda g: g.dma_start(
        out=Wd_b[0][:], in_=wsdn_h.rearrange("(c p) f -> p c f", p=128)),
        deps=[s_toks[-1]], inc="dw", amt=16)
    up_bank_free = [None] * 3
    dn_bank_free = [ps6_free[0], ps6_free[1]]
    ubr = [0]
    hsh_free = [alpha_used]
    shup_last = None
    sh_acc_toks = [None] * TT
    for h in range(NH):
        sil_toks = []
        for tb in range(2):
            for fc in range(FC):
                bank = ubr[0] % 3
                ubr[0] += 1
                t_mm = None
                for dc in range(DC):
                    t_mm = Em("tensor", lambda t, h=h, tb=tb, fc=fc, dc=dc, bank=bank: t.matmul(
                        PE[:, bank * 512:(bank + 1) * 512],
                        Wu_b[0][:, dc, fc * 128:(fc + 1) * 128],
                        xnT[:, dc, h * TH + tb * 512:h * TH + (tb + 1) * 512],
                        start=(dc == 0), stop=(dc == DC - 1)),
                        deps=[t_wsu, xnT_tok, up_bank_free[bank]],
                        inc=("t" if dc == DC - 1 else None))
                shup_last = t_mm
                t_h = Em("scalar", lambda sc, tb=tb, fc=fc, bank=bank: sc.activation(
                    Hsh[:, fc, tb * 512:(tb + 1) * 512],
                    PE[:, bank * 512:(bank + 1) * 512], AF.Silu),
                    deps=[t_mm] + hsh_free, inc="s")
                up_bank_free[bank] = t_h
                sil_toks.append(t_h)
        last_dn = None
        for tl in range(TTH):
            for dh in range(2):
                bank = (tl * 2 + dh) % 2
                t_mm = None
                for fc in range(FC):
                    t_mm = Em("tensor", lambda t, tl=tl, dh=dh, fc=fc, bank=bank: t.matmul(
                        PS[:, bank * 512:(bank + 1) * 512],
                        Hsh[:, fc, tl * 128:(tl + 1) * 128],
                        Wd_b[0][:, fc, dh * 512:(dh + 1) * 512],
                        start=(fc == 0), stop=(fc == FC - 1)),
                        deps=[t_wsd, dn_bank_free[bank]] + sil_toks,
                        inc=("t" if fc == FC - 1 else None))
                last_dn = t_mm
                t_ev = Em("vector", lambda v, h=h, tl=tl, dh=dh, bank=bank: v.tensor_copy(
                    acc_ap(h * TTH + tl)[:, dh * 512:(dh + 1) * 512],
                    PS[:, bank * 512:(bank + 1) * 512]),
                    deps=[t_mm, s_toks[-1]], inc="v")
                dn_bank_free[bank] = t_ev
                sh_acc_toks[h * TTH + tl] = t_ev
        hsh_free = [last_dn]
    shdn_last = hsh_free[0]

    pe3_free = [s_toks[-1]]
    flu_toks = [None] * TT
    wj_toks = [None] * TT
    p2_next = [0]

    def emit_pass2_tile():
        tt = p2_next[0]
        if tt >= TT:
            return
        p2_next[0] += 1
        sec = float((CAPH if tt >= TTH else 0) - 1)
        t_tp = Em("tensor", lambda t, tt=tt: t.transpose(
            PE[0:128, 1536:1536 + 64], CUM[:, tt * 128:(tt + 1) * 128],
            identF[0:64, 0:64]), deps=[cumsum_done, pe3_free[0], t_if], inc="t")
        t_pe = Em("vector", lambda v: v.tensor_copy(
            aff_sb[:], PE[:, 1536:1536 + 64]), deps=[t_tp], inc="v")
        pe3_free[0] = t_pe
        s1 = Em("vector", lambda v: v.scalar_tensor_tensor(
            zap[:], erow[:], float(CAP), aff_sb[:], ALU.mult, ALU.add),
            deps=[t_pe, t_erow], inc="v")
        s2 = Em("vector", lambda v, sec=sec: v.tensor_scalar(
            zap[:], zap[:], sec, None, ALU.add), deps=[s1], inc="v")
        s3 = Em("vector", lambda v: v.tensor_scalar(
            mask[:], zap[:], -1.0, BIGK, ALU.mult, ALU.add), deps=[s2], inc="v")
        s4 = Em("vector", lambda v, tt=tt: v.tensor_mul(
            mask[:], mask[:], MKb[:, tt * E:(tt + 1) * E]), deps=[s3], inc="v")
        s5 = Em("vector", lambda v: v.max(out=max8[:], in_=mask[:]), deps=[s4], inc="v")
        s6 = Em("vector", lambda v: v.tensor_scalar(
            sig[:, 0:8], max8[:], -1.0, BIGK, ALU.mult, ALU.add), deps=[s5], inc="v")
        flu_toks[tt] = Em("vector", lambda v, tt=tt: v.tensor_copy(
            FLU[:, tt * 8:(tt + 1) * 8], sig[:, 0:8]), deps=[s6], inc="v")
        prev = flu_toks[tt]
        for j in range(8):
            o1 = Em("vector", lambda v, j=j: v.tensor_scalar(
                wm[:], mask[:], max8[:, j:j + 1], None, ALU.is_equal),
                deps=[prev], inc="v")
            o2 = Em("vector", lambda v, tt=tt: v.tensor_mul(
                wm[:], wm[:], cw[:, tt * E:(tt + 1) * E]), deps=[o1], inc="v")
            prev = Em("vector", lambda v, tt=tt, j=j: v.reduce_sum(
                WJ[:, tt * 8 + j:tt * 8 + j + 1], wm[:], AX.X), deps=[o2], inc="v")
        wj_toks[tt] = prev

    pt6_free = [pt_free[0], pt_free[1]]
    xg_free = [[shup_last], [shup_last]]
    xgt_free = [[shup_last], [shup_last]]
    hg_free = [shdn_last]
    ygst_free = [None, None]
    wu_free = [shup_last, s_toks[-1]]
    wd_free = [shdn_last, s_toks[-1]]
    yg_last = None
    for e in range(E):
        par = e % 2
        t_wu = Em("gpsimd", lambda g, e=e, par=par: g.dma_start(
            out=Wu_b[par][:], in_=wup_h[e].rearrange("(c p) f -> p c f", p=128)),
            deps=[wu_free[par]], inc="dw", amt=16)
        t_wd = Em("gpsimd", lambda g, e=e, par=par: g.dma_start(
            out=Wd_b[par][:], in_=wdn_h[e].rearrange("(c p) f -> p c f", p=128)),
            deps=[wd_free[par]], inc="dw", amt=16)
        gath = []
        for c in range(4):
            gath.append(Em("gpsimd", lambda g, e=e, c=c, par=par: g.indirect_dma_start(
                out=XG[par][:, c, :], out_offset=None, in_=xnd_h,
                in_offset=IndirectOffsetOnAxis(IXT[:, c * E + e:c * E + e + 1], 0)),
                deps=[ixt_toks[c], xnd_toks[-1]] + xg_free[par], inc="dw", amt=16))
        xg_free[par] = []
        xgt_wr = []
        for c in range(4):
            pb6 = c % 2
            t_tr = None
            for dj in range(DC):
                t_tr = Em("tensor", lambda t, c=c, dj=dj, par=par, pb6=pb6: t.transpose(
                    PT[pb6][:, dj * 128:(dj + 1) * 128],
                    XG[par][:, c, dj * 128:(dj + 1) * 128], ident[:]),
                    deps=[gath[c], pt6_free[pb6]] + xgt_free[par],
                    inc=("t" if dj == DC - 1 else None))
            xg_free[par].append(t_tr)
            t_xe = Em("vector", lambda v, c=c, par=par, pb6=pb6: v.tensor_copy(
                XGT[par][:, :, c * 128:(c + 1) * 128],
                PT[pb6].rearrange("p (dj q) -> p dj q", dj=DC)),
                deps=[t_tr], inc="v")
            pt6_free[pb6] = t_xe
            xgt_wr.append(t_xe)
        xgt_free[par] = []
        sil_toks = []
        for fb in range(FC):
            bank = ubr[0] % 3
            ubr[0] += 1
            t_mm = None
            for dc in range(DC):
                t_mm = Em("tensor", lambda t, fb=fb, dc=dc, par=par, bank=bank: t.matmul(
                    PE[:, bank * 512:(bank + 1) * 512],
                    Wu_b[par][:, dc, fb * 128:(fb + 1) * 128],
                    XGT[par][:, dc, :],
                    start=(dc == 0), stop=(dc == DC - 1)),
                    deps=[t_wu, up_bank_free[bank]] + xgt_wr,
                    inc=("t" if dc == DC - 1 else None))
            xgt_free[par].append(t_mm)
            t_h = Em("scalar", lambda sc, fb=fb, bank=bank: sc.activation(
                Hg[:, fb, :], PE[:, bank * 512:(bank + 1) * 512], AF.Silu),
                deps=[t_mm] + hg_free, inc="s")
            up_bank_free[bank] = t_h
            sil_toks.append(t_h)
        wu_free[par] = t_mm
        last_dn = None
        for sb in range(4):
            for dh in range(2):
                bank = (sb * 2 + dh) % 2
                t_mm = None
                for fb in range(FC):
                    t_mm = Em("tensor", lambda t, sb=sb, dh=dh, fb=fb, par=par, bank=bank: t.matmul(
                        PS[:, bank * 512:(bank + 1) * 512],
                        Hg[:, fb, sb * 128:(sb + 1) * 128],
                        Wd_b[par][:, fb, dh * 512:(dh + 1) * 512],
                        start=(fb == 0), stop=(fb == FC - 1)),
                        deps=[t_wd, dn_bank_free[bank]] + sil_toks,
                        inc=("t" if fb == FC - 1 else None))
                last_dn = t_mm
                buf = (sb // 2) % 2
                t_ev = Em("vector", lambda v, sb=sb, dh=dh, bank=bank, buf=buf: v.tensor_copy(
                    Ygst[buf][:, sb % 2, dh * 512:(dh + 1) * 512],
                    PS[:, bank * 512:(bank + 1) * 512]),
                    deps=[t_mm, ygst_free[buf]], inc="v")
                dn_bank_free[bank] = t_ev
                if sb % 2 == 1 and dh == 1:
                    pair = sb // 2
                    yg_last = Em("sync", lambda s, e=e, pair=pair, buf=buf: s.dma_start(
                        out=ygd_h[e * CAP + pair * 256:e * CAP + (pair + 1) * 256, :]
                            .rearrange("(c p) d -> p c d", p=128),
                        in_=Ygst[buf][:]),
                        deps=[dn_bank_free[0], dn_bank_free[1]], inc="dy", amt=16)
                    ygst_free[buf] = yg_last
        wd_free[par] = last_dn
        hg_free = [last_dn]
        if e % 4 == 0:
            emit_pass2_tile()
    while p2_next[0] < TT:
        emit_pass2_tile()

    if phase_limit <= 6:
        return _finish(nc, P, taps, locals())
    xstf = [at(f"xstf{i}", i * 4096, [128, D], F32) for i in range(2)]
    gacc_free = [[yg_last], [yg_last]]
    xs6_free = [list(xg_free[0]), list(xg_free[0])]
    os_free = [[yg_last], [yg_last], [yg_last]]
    gsil_free = [hg_free, hg_free]
    out_dma_toks = []
    for tt in range(TT):
        prev_stt = sh_acc_toks[tt]
        for j in range(8):
            gb6 = j % 2
            t_ga = Em("gpsimd", lambda g, tt=tt, j=j, gb6=gb6: g.indirect_dma_start(
                out=gacc[gb6][:], out_offset=None, in_=ygd_h,
                in_offset=IndirectOffsetOnAxis(FLU[:, tt * 8 + j:tt * 8 + j + 1], 0)),
                deps=[flu_toks[tt], yg_last] + gacc_free[gb6], inc="dw", amt=16)
            prev_stt = Em("vector", lambda v, tt=tt, j=j, gb6=gb6: v.scalar_tensor_tensor(
                acc_ap(tt)[:], gacc[gb6][:], WJ[:, tt * 8 + j:tt * 8 + j + 1],
                acc_ap(tt)[:], ALU.mult, ALU.add),
                deps=[t_ga, wj_toks[tt], prev_stt], inc="v")
            gacc_free[gb6] = [prev_stt]
        if phase_limit <= 7:
            continue
        t_g = Em("scalar", lambda sc, tt=tt: sc.activation(
            gsil[tt % 2][:], acc_ap(tt)[:], AF.Silu),
            deps=[prev_stt] + gsil_free[tt % 2], inc="s")
        b = tt % 2
        t_dx = Em("sync", lambda s, tt=tt, b=b: s.dma_start(
            out=xstf[b][:], in_=x_h[tt * 128:(tt + 1) * 128, :]),
            deps=xs6_free[b] if isinstance(xs6_free[b], list) else [xs6_free[b]],
            inc="dx", amt=16)
        t_sl = Em("sync", lambda s, tt=tt: s.dma_start(
            out=sstg[tt % 2][:], in_=sdr_h[tt * 128:(tt + 1) * 128, :]),
            deps=[s_wr_toks[tt]] + sstg_free[tt % 2] if isinstance(sstg_free[tt % 2], list)
                 else [s_wr_toks[tt], sstg_free[tt % 2]],
            inc="dy", amt=16)
        ob = tt % 3
        t_m1 = Em("vector", lambda v, tt=tt, ob=ob: v.tensor_mul(
            ostage[ob][:], sstg[tt % 2][:], gsil[tt % 2][:]),
            deps=[t_g, t_sl] + os_free[ob], inc="v")
        sstg_free[tt % 2] = [t_m1]
        gsil_free[tt % 2] = [t_m1]
        t_m2 = Em("vector", lambda v, b=b, ob=ob: v.tensor_add(
            ostage[ob][:], ostage[ob][:], xstf[b][:]),
            deps=[t_m1, t_dx], inc="v")
        xs6_free[b] = [t_m2]
        t_o = Em("sync", lambda s, tt=tt, ob=ob: s.dma_start(
            out=y_h[tt * 128:(tt + 1) * 128, :], in_=ostage[ob][:]),
            deps=[t_m2], inc="dx", amt=16)
        os_free[ob] = [t_o]
        out_dma_toks.append(t_o)

    if phase_limit <= 7:
        Em("sync", lambda s: None, deps=P.final_deps())
    else:
        Em("sync", lambda s: None, deps=[out_dma_toks[-1], yg_last])
    return _finish(nc, P, taps, locals())


def _finish(nc, P, taps, env):
    tap_outs = []
    for name, key in taps:
        t = env[key]
        shape = list(t.shape)
        dt = t.tensor.dtype if hasattr(t, "tensor") else t.dtype
        o = nc.dram_tensor(name, shape, dt, kind="ExternalOutput").ap()
        P.emit("sync", lambda s, o=o, t=t: s.dma_start(out=o, in_=t[:]),
               deps=P.final_deps(), inc="dx", amt=16)
        tap_outs.append(name)
    if tap_outs:
        P.emit("sync", lambda s: None, deps=P.final_deps())
    with ExitStack() as ctx:
        sems = {n: ctx.enter_context(nc.semaphore(n)) for n in P.sem_names}
        block = ctx.enter_context(nc.Block())
        P.build(block, sems)
    return nc


_NC_CACHE = {}


def _get_nc(phase_limit=99, taps=()):
    key = (phase_limit, tuple(taps))
    if key not in _NC_CACHE:
        _NC_CACHE[key] = build_nc(phase_limit, taps)
    return _NC_CACHE[key]


def kernel(L0_attn_x, L0_attn_Wk, L0_attn_Wq, L0_attn_Wv, L0_attn_Wa, L0_attn_Wb,
           L0_ffn_centroids, L0_ffn_expert_biases, L0_ffn_Ws_up, L0_ffn_Ws_down,
           L0_ffn_W_up, L0_ffn_W_down, _phase_limit=99, _taps=()):
    nc = _get_nc(_phase_limit, _taps)
    common = dict(
        wk=np.ascontiguousarray(L0_attn_Wk), wq=np.ascontiguousarray(L0_attn_Wq),
        wv=np.ascontiguousarray(L0_attn_Wv), wa=np.ascontiguousarray(L0_attn_Wa),
        wb=np.ascontiguousarray(L0_attn_Wb),
        cent=np.ascontiguousarray(L0_ffn_centroids),
        ebias=np.ascontiguousarray(np.reshape(L0_ffn_expert_biases, (1, E))),
        wsup=np.ascontiguousarray(L0_ffn_Ws_up), wsdn=np.ascontiguousarray(L0_ffn_Ws_down),
        wup=np.ascontiguousarray(L0_ffn_W_up), wdn=np.ascontiguousarray(L0_ffn_W_down),
    )
    x = np.ascontiguousarray(L0_attn_x)
    in_maps = [dict(common, x=np.ascontiguousarray(x[c])) for c in range(8)]
    res = run_bass_kernel_spmd(nc, in_maps, core_ids=list(range(8)))
    if _taps or _phase_limit < 99:
        return res
    return np.stack([r["y"] for r in res.results], axis=0)


# version 14
# speedup vs baseline: 1.7778x; 1.4705x over previous
import sys
if "/opt/trn_rl_repo" not in sys.path:
    sys.path.insert(0, "/opt/trn_rl_repo")
from collections import defaultdict
from contextlib import ExitStack

import numpy as np
import ml_dtypes
import concourse.bass as bass
import concourse.mybir as mybir
from concourse.bass_utils import run_bass_kernel_spmd
from concourse.bass_core_rust import IndirectOffsetOnAxis

F32 = mybir.dt.float32
BF16 = mybir.dt.bfloat16
AF = mybir.ActivationFunctionType
ALU = mybir.AluOpType
AX = mybir.AxisListType

ENGINES = ("sync", "gpsimd", "tensor", "vector", "scalar")

S, D, E, FF = 2048, 1024, 64, 512
TT = S // 128
DC = D // 128
FC = FF // 128
NH = 2
TH = S // NH
TTH = TH // 128
NEG_BIG = -1e30
NEG_THR = -1e29
CAP = 512
CAPH = CAP // 2
BIGK = 65536.0


class Prog:
    def __init__(self):
        self.ops = {e: [] for e in ENGINES}
        self.sem_names = []
        self.counts = defaultdict(int)

    def sem(self, name):
        if name not in self.sem_names:
            self.sem_names.append(name)
        return name

    PARANOID = False

    def __init_rr__(self):
        pass

    def emit(self, engine, fn, deps=(), inc=None, amt=1):
        tok = None
        deps = [d for d in deps if d is not None]
        if Prog.PARANOID:
            deps = [(s, v) for s, v in self.counts.items()]
        if inc == "dw":
            rr = getattr(self, "_dw_rr", 0)
            self._dw_rr = rr + 1
            inc = f"dw{rr % 4}"
        if inc is not None:
            self.sem(inc)
            if amt == 16 and self.counts[inc] > 0:
                deps.append((inc, self.counts[inc]))
            self.counts[inc] += amt
            tok = (inc, self.counts[inc])
        self.ops[engine].append((fn, deps, inc, amt))
        return tok

    def final_deps(self):
        return [(s, self.counts[s]) for s in self.sem_names]

    def build(self, block, sems):
        for e in ENGINES:
            if not self.ops[e]:
                continue

            def make(e):
                def f(eng):
                    hw = defaultdict(int)
                    for fn, deps, inc, amt in self.ops[e]:
                        for (s, v) in deps:
                            if v > hw[s]:
                                eng.wait_ge(sems[s], v)
                                hw[s] = v
                        inst = fn(eng)
                        if inc is not None:
                            if inst is None:
                                inst = eng.nop()
                            inst.then_inc(sems[inc], amt)
                return f

            getattr(block, e)(make(e))


def build_nc(phase_limit=99, taps=()):
    nc = bass.Bass("TRN2", target_bir_lowering=False)
    x_h = nc.dram_tensor("x", [S, D], F32, kind="ExternalInput").ap()
    wk_h = nc.dram_tensor("wk", [D, D], F32, kind="ExternalInput").ap()
    wq_h = nc.dram_tensor("wq", [D, D], F32, kind="ExternalInput").ap()
    wv_h = nc.dram_tensor("wv", [D, D], F32, kind="ExternalInput").ap()
    wa_h = nc.dram_tensor("wa", [D, D], F32, kind="ExternalInput").ap()
    wb_h = nc.dram_tensor("wb", [D, D], F32, kind="ExternalInput").ap()
    cent_h = nc.dram_tensor("cent", [E, D], F32, kind="ExternalInput").ap()
    ebias_h = nc.dram_tensor("ebias", [1, E], F32, kind="ExternalInput").ap()
    wsup_h = nc.dram_tensor("wsup", [D, FF], F32, kind="ExternalInput").ap()
    wsdn_h = nc.dram_tensor("wsdn", [FF, D], F32, kind="ExternalInput").ap()
    wup_h = nc.dram_tensor("wup", [E, D, FF], F32, kind="ExternalInput").ap()
    wdn_h = nc.dram_tensor("wdn", [E, FF, D], F32, kind="ExternalInput").ap()
    y_h = nc.dram_tensor("y", [S, D], F32, kind="ExternalOutput").ap()
    xnd_h = nc.dram_tensor("xnd", [S, D], BF16, kind="Internal").ap()
    sdr_h = nc.dram_tensor("sdr", [S, D], BF16, kind="Internal").ap()
    ygd_h = nc.dram_tensor("ygd", [E * CAP, D], BF16, kind="Internal").ap()

    ident_hbm = nc.inline_tensor(np.eye(128, dtype=ml_dtypes.bfloat16), name="identc").ap()
    ones_hbm = nc.inline_tensor(np.ones((1, 128), dtype=np.float32), name="onesc").ap()
    iota64_hbm = nc.inline_tensor(np.arange(E, dtype=np.float32).reshape(1, E), name="iotac").ap()
    identF_hbm = nc.inline_tensor(np.eye(128, dtype=np.float32), name="identfc").ap()

    ARENA_BYTES = 204800
    arena = nc.alloc_sbuf_tensor("ARENA", [128, ARENA_BYTES], mybir.dt.uint8)
    SBUF_BASE = nc.lookup_mloc(arena).addr

    def at(name, off, shape, dt):
        assert 0 <= off and off < ARENA_BYTES
        return nc.alloc_sbuf_tensor_at(name, shape, dt, offset=SBUF_BASE + off).ap()

    xnT = at("xnT", 0, [128, DC, S], BF16)
    cw = at("cw", 32768, [128, TT * E], F32)
    co = 36864
    ident = at("ident", co + 0, [128, 128], BF16); co += 256
    centT = at("centT", co, [128, DC, E], BF16); co += 1024
    bias_sb = at("bias_sb", co, [1, E], F32); co += 256
    bias_bc = at("bias_bc", co, [128, E], F32); co += 256
    ones_sb = at("ones_sb", co, [1, 128], F32); co += 512
    aff_sb = at("aff_sb", co, [128, E], F32); co += 256
    biased = at("biased", co, [128, E], F32); co += 256
    max8 = at("max8", co, [128, 8], F32); co += 32
    zap = at("zap", co, [128, E], F32); co += 256
    mask = at("mask", co, [128, E], F32); co += 256
    sig = at("sig", co, [128, E], F32); co += 256
    wm = at("wm", co, [128, E], F32); co += 256
    st6 = at("st6", co, [128, 12], F32); co += 64
    mv2 = at("mv2", co, [128, 2 * TT], F32); co += 128
    nmp = at("nmp", co, [128, 4], F32); co += 32
    epsv = at("epsv", co, [128, 1], F32); co += 32
    rcp16 = at("rcp16", co, [128, TT], F32); co += 64
    rstd16 = at("rstd16", co, [128, TT], F32); co += 64
    negm16 = at("negm16", co, [128, TT], F32); co += 64
    rowsum16 = at("rowsum16", co, [128, TT], F32); co += 64
    recip16 = at("recip16", co, [128, TT], F32); co += 64
    den1 = at("den1", co, [128, 1], F32); co += 64
    rd1 = at("rd1", co, [128, 1], F32); co += 64
    iota_sb = at("iota_sb", co, [1, E], F32); co += 256
    erow = at("erow", co, [128, E], F32); co += 256
    identF = at("identF", co, [128, 128], F32); co += 512
    assert co <= 45056, co
    MKb = at("MKb", 196608, [128, TT * E], BF16)
    FLU = at("FLU", 198656, [128, TT * 8], mybir.dt.uint32)
    WJ = at("WJ", 199168, [128, TT * 8], F32)
    IXT = at("IXT", 199680, [128, 4 * E], mybir.dt.uint32)
    MT = at("MT", 110592, [E, S], F32)
    CUM = at("CUM", 110592 + 8192, [E, S], F32)
    VALS = at("VALS", 110592 + 16384, [E, TH], F32)
    IDXL = at("IDXL", 110592 + 20480, [E, CAP], F32)
    RAMP = at("RAMP", 110592 + 22528, [E, TH], F32)
    EXV = at("EXV", 110592 + 26624, [E, 8], F32)
    ZROW = at("ZROW", 110592 + 26656, [E, TH], F32)
    PTM = at("PTM", 176128, [128, TT * E], F32)
    XG = [at(f"XG{i}", 0 + i * 8192, [128, 4, D], BF16) for i in range(2)]
    XGT = [at(f"XGT{i}", 16384 + i * 8192, [128, DC, CAP], BF16) for i in range(2)]
    Hg = at("Hg", 176128, [128, FC, CAP], BF16)
    gacc = [at(f"gacc{i}", 180224 + i * 2048, [128, D], BF16) for i in range(2)]
    gsil = [at(f"gsil{i}", 176128 + i * 2048, [128, D], BF16) for i in range(2)]
    Ygst = [at(f"Ygst{i}", 184320 + i * 4096, [128, 2, D], BF16) for i in range(2)]
    accG = at("accG", 143360, [128, TTH, D], F32)
    Hsh = at("Hsh", 176128, [128, FC, TH], BF16)
    u = at("u", 45056, [128, TT, D], BF16)
    Wu_b = [at(f"Wu{i}", 45056 + i * 8192, [128, DC, FF], BF16) for i in range(2)]
    Wd_b = [at(f"Wd{i}", 45056 + 16384 + i * 8192, [128, FC, D], BF16) for i in range(2)]
    vT = at("vT", 77824, [128, DC, S], BF16)
    qT = at("qT", 77824, [128, DC, S], BF16)
    kT = at("kT", 77824, [128, DC, S], BF16)
    acc = at("acc", 77824, [128, TTH, D], F32)
    WT = [at(f"WT{i}", 110592 + i * 16384, [128, DC, D], BF16) for i in range(2)]
    sstg = [at(f"sstg{i}", 200704 + i * 2048, [128, D], BF16) for i in range(2)]
    pT = at("pT", 143360, [128, DC, S], BF16)
    wstage = [at(f"wstage{i}", 192512 + i * 2048, [128, D], BF16) for i in range(2)]
    alpha_sb = at("alpha_sb", 176128, [128, S], BF16)
    alphaT_sb = at("alphaT_sb", 176128 + 4096, [128, S], BF16)
    xn_t2 = [at(f"xn_t{i}", 184320 + i * 2048, [128, D], BF16) for i in range(2)]
    sqf = at("sqf", 184320 + 4096, [128, D], F32)
    ostage = [at(f"ostage{i}", 184320 + i * 4096, [128, D], F32) for i in range(3)]

    PE = nc.alloc_psum_tensor("PE", [128, 2048], F32).ap()
    PS = nc.alloc_psum_tensor("PS", [128, 1024], F32).ap()
    PT = [nc.alloc_psum_tensor(f"PT{i}", [128, 1024], BF16).ap() for i in range(2)]

    P = Prog()
    Em = P.emit

    LN_ONLY = (phase_limit == -1)
    if not LN_ONLY:
        t_id = Em("sync", lambda s: s.dma_start(out=ident[:], in_=ident_hbm), inc="dx", amt=16)
        t_ones = Em("sync", lambda s: s.dma_start(out=ones_sb[:], in_=ones_hbm), inc="dx", amt=16)
        t_bias = Em("sync", lambda s: s.dma_start(out=bias_sb[:], in_=ebias_h), inc="dx", amt=16)
    else:
        t_id = t_ones = t_bias = None

    t_eps = Em("vector", lambda v: v.memset(epsv[:], 1e-5), inc="v")
    xs_free = [None, None]
    pt_free = [None, None]
    xn_used = [[], []]
    xnd_toks = []
    wst_free = [None, None]
    pe_free = [None] * 4
    sq_done = None
    xnT_tok = None
    xnT_tile_toks = [None] * TT
    wv_tok = [None] * DC
    vT_ev = [[None] * 4 for _ in range(DC)]
    vT_emitted = set()
    vg = [0]

    def emit_vT_group(oc, tb):
        bank = vg[0] % 4
        vg[0] += 1
        t_mm = None
        for dc in range(DC):
            t_mm = Em("tensor", lambda t, oc=oc, tb=tb, dc=dc, bank=bank: t.matmul(
                PE[:, bank * 512:(bank + 1) * 512],
                WT[0][:, dc, oc * 128:(oc + 1) * 128],
                xnT[:, dc, tb * 512:(tb + 1) * 512],
                start=(dc == 0), stop=(dc == DC - 1)),
                deps=[wv_tok[oc], xnT_tile_toks[4 * tb + 3], pe_free[bank]],
                inc=("t" if dc == DC - 1 else None))
        t_ev = Em("scalar", lambda sc, oc=oc, tb=tb, bank=bank: sc.copy(
            vT[:, oc, tb * 512:(tb + 1) * 512], PE[:, bank * 512:(bank + 1) * 512]),
            deps=[t_mm], inc="s")
        pe_free[bank] = t_ev
        vT_ev[oc][tb] = t_ev
        vT_emitted.add((oc, tb))

    def emit_ready_vT(tt):
        for tb in range(4):
            if 4 * tb + 3 > tt:
                break
            for oc in range(DC):
                if 2 * oc + 1 <= tt and (oc, tb) not in vT_emitted:
                    emit_vT_group(oc, tb)

    xp = [at(f"xp{i}", 45056 + i * 8192, [128, 2, D], F32) for i in range(2)]
    e4s = [None] * TT
    ags = [None] * TT
    xsrcs = [None] * TT

    def p0_apply(tt):
        nonlocal xnT_tok
        b = (tt // 2) % 2
        xn_t = xn_t2[tt % 2]
        xsrc = xsrcs[tt]
        t_f = Em("vector", lambda v, tt=tt, xsrc=xsrc, xn_t=xn_t: v.tensor_scalar(
            xn_t[:], xsrc[:], mv2[:, 2 * tt:2 * tt + 1], rstd16[:, tt:tt + 1],
            ALU.subtract, ALU.mult), deps=[e4s[tt], ags[tt]] + list(xn_used[tt % 2]), inc="v")
        xs_free[b] = t_f
        t_tr = None
        for dc in range(DC):
            t_tr = Em("tensor", lambda t, dc=dc, xn_t=xn_t: t.transpose(
                PT[0][:, dc * 128:(dc + 1) * 128], xn_t[:, dc * 128:(dc + 1) * 128],
                ident[:]), deps=[t_f, t_id, pt_free[0]],
                inc=("t" if dc == DC - 1 else None))
        t_xnd = Em("sync", lambda s, tt=tt, xn_t=xn_t: s.dma_start(
            out=xnd_h[tt * 128:(tt + 1) * 128, :], in_=xn_t[:]),
            deps=[t_f], inc="dn", amt=16)
        xnd_toks.append(t_xnd)
        xn_used[tt % 2] = [t_tr, t_xnd]
        for dc in range(DC):
            xnT_tok = Em("scalar", lambda sc, tt=tt, dc=dc: sc.copy(
                xnT[:, dc, tt * 128:(tt + 1) * 128],
                PT[0][:, dc * 128:(dc + 1) * 128]), deps=[t_tr], inc="s")
        pt_free[0] = xnT_tok
        xnT_tile_toks[tt] = xnT_tok
        if tt % 2 == 1:
            rc = (tt - 1) // 2
            wb_ = rc % 2
            t_w = Em("gpsimd", lambda g, rc=rc, wb_=wb_: g.dma_start(
                out=wstage[wb_][:], in_=wv_h[rc * 128:(rc + 1) * 128, :]),
                deps=[wst_free[wb_]], inc="dw", amt=16)
            t_wtr = None
            for dc in range(DC):
                t_wtr = Em("tensor", lambda t, dc=dc, wb_=wb_: t.transpose(
                    PT[1][:, dc * 128:(dc + 1) * 128],
                    wstage[wb_][:, dc * 128:(dc + 1) * 128], ident[:]),
                    deps=[t_w, t_id, pt_free[1]], inc=("t" if dc == DC - 1 else None))
            wst_free[wb_] = t_wtr
            wv_tok[rc] = Em("scalar", lambda sc, rc=rc: sc.copy(
                WT[0][:, :, rc * 128:(rc + 1) * 128],
                PT[1].rearrange("p (c q) -> p c q", c=DC)), deps=[t_wtr], inc="s")
            pt_free[1] = wv_tok[rc]
        emit_ready_vT(tt)

    t_dx = None
    for tt in range(TT + 1):
        if tt < TT:
            b = (tt // 2) % 2
            if tt % 2 == 0:
                q = tt // 2
                t_dx = Em("sync", lambda s, q=q, b=b: s.dma_start(
                    out=xp[b][:], in_=x_h[q * 256:(q + 1) * 256, :].rearrange(
                        "(c p) d -> p c d", p=128)),
                    deps=[xs_free[b]], inc="dx", amt=16)
            xsrcs[tt] = xp[b][:, tt % 2, :]
            t_b1 = Em("vector", lambda v, tt=tt: v.bn_stats(
                st6[:, 0:6], xsrcs[tt][:, 0:512]), deps=[t_dx, sq_done], inc="v")
            t_b2 = Em("vector", lambda v, tt=tt: v.bn_stats(
                st6[:, 6:12], xsrcs[tt][:, 512:1024]), deps=[t_b1], inc="v")
            t_ag = Em("vector", lambda v, tt=tt: v.bn_aggr(
                mv2[:, 2 * tt:2 * tt + 2], st6[:]), deps=[t_b2], inc="v")
            sq_done = t_ag
            ags[tt] = t_ag
            t_e3 = Em("scalar", lambda sc, tt=tt: sc.activation(
                rcp16[:, tt:tt + 1], mv2[:, 2 * tt + 1:2 * tt + 2], AF.Ln, bias=epsv[:]),
                deps=[t_ag, t_eps], inc="s")
            e4s[tt] = Em("scalar", lambda sc, tt=tt: sc.activation(
                rstd16[:, tt:tt + 1], rcp16[:, tt:tt + 1], AF.Exp, scale=-0.5),
                deps=[t_e3], inc="s")
        if not LN_ONLY and tt >= 1:
            p0_apply(tt - 1)
    if LN_ONLY:
        return _finish(nc, P, taps, locals())
    emit_ready_vT(99)
    xs_last = max(t for t in xs_free if t is not None)
    t_wvT = wv_tok[-1]
    vT_toks = [vT_ev[oc][tb] for oc in range(DC) for tb in range(4)]
    if phase_limit <= 0:
        return _finish(nc, P, taps, locals())

    def prep_wT(w_hbm, slot, deps_extra):
        nonlocal wst_free
        done = None
        for rc in range(DC):
            b = rc % 2
            t_w = Em("gpsimd", lambda g, rc=rc, b=b, w_hbm=w_hbm: g.dma_start(
                out=wstage[b][:], in_=w_hbm[rc * 128:(rc + 1) * 128, :]),
                deps=[wst_free[b]] + deps_extra, inc="dw", amt=16)
            pb = rc % 2
            t_tr = None
            for dc in range(DC):
                t_tr = Em("tensor", lambda t, rc=rc, dc=dc, b=b, pb=pb: t.transpose(
                    PT[pb][:, dc * 128:(dc + 1) * 128],
                    wstage[b][:, dc * 128:(dc + 1) * 128], ident[:]),
                    deps=[t_w, pt_free[pb]], inc=("t" if dc == DC - 1 else None))
            wst_free[b] = t_tr
            done = Em("vector", lambda v, rc=rc, slot=slot, pb=pb: v.tensor_copy(
                WT[slot][:, :, rc * 128:(rc + 1) * 128],
                PT[pb].rearrange("p (c q) -> p c q", c=DC)), deps=[t_tr], inc="v")
            pt_free[pb] = done
        return done

    def load_nat(w_hbm, slot, deps_extra):
        return Em("gpsimd", lambda g, w_hbm=w_hbm, slot=slot: g.dma_start(
            out=WT[slot][:], in_=w_hbm.rearrange("(c p) k -> p c k", p=128)),
            deps=deps_extra, inc="dw", amt=16)

    def proj(dst, lhs_of, rhs_of, dep_lhs, dep_rhs, extra_mm_deps=()):
        toks = []
        g = 0
        for oc in range(DC):
            for tb in range(4):
                bank = g % 4
                t_mm = None
                for dc in range(DC):
                    t_mm = Em("tensor", lambda t, oc=oc, tb=tb, dc=dc, bank=bank: t.matmul(
                        PE[:, bank * 512:(bank + 1) * 512], lhs_of(dc, oc), rhs_of(dc, tb),
                        start=(dc == 0), stop=(dc == DC - 1)),
                        deps=[dep_lhs, dep_rhs, pe_free[bank]] + list(extra_mm_deps),
                        inc=("t" if dc == DC - 1 else None))
                t_ev = Em("vector", lambda v, oc=oc, tb=tb, bank=bank, dst=dst: v.tensor_copy(
                    dst[:, oc, tb * 512:(tb + 1) * 512], PE[:, bank * 512:(bank + 1) * 512]),
                    deps=[t_mm], inc="v")
                pe_free[bank] = t_ev
                toks.append(t_ev)
                g += 1
        return toks

    vT_last = max((t for row in vT_ev for t in row), key=lambda x: x[1])
    t_wbT = prep_wT(wb_h, 1, [])
    u_toks = []
    g = 0
    for tt in range(TT):
        for wh in range(2):
            bank = g % 4
            t_mm = None
            for vc in range(DC):
                t_mm = Em("tensor", lambda t, tt=tt, wh=wh, vc=vc, bank=bank: t.matmul(
                    PE[:, bank * 512:(bank + 1) * 512],
                    vT[:, vc, tt * 128:(tt + 1) * 128],
                    WT[1][:, vc, wh * 512:(wh + 1) * 512],
                    start=(vc == 0), stop=(vc == DC - 1)),
                    deps=[t_wbT, pe_free[bank]] + [vT_ev[oc][tt // 4] for oc in range(DC)],
                    inc=("t" if vc == DC - 1 else None))
            t_ev = Em("vector", lambda v, tt=tt, wh=wh, bank=bank: v.tensor_copy(
                u[:, tt, wh * 512:(wh + 1) * 512], PE[:, bank * 512:(bank + 1) * 512]),
                deps=[t_mm, xs_last], inc="v")
            pe_free[bank] = t_ev
            u_toks.append(t_ev)
            g += 1
    if phase_limit <= 2:
        return _finish(nc, P, taps, locals())

    t_wqT = prep_wT(wq_h, 0, [vT_last])
    qT_toks = proj(qT, lambda dc, oc: WT[0][:, dc, oc * 128:(oc + 1) * 128],
                   lambda dc, tb: xnT[:, dc, tb * 512:(tb + 1) * 512],
                   t_wqT, xnT_tok, extra_mm_deps=[u_toks[-1]])
    t_wa = load_nat(wa_h, 1, [u_toks[-1]])
    pT_toks = proj(pT, lambda qc, kb: WT[1][:, qc, kb * 128:(kb + 1) * 128],
                   lambda qc, tb: qT[:, qc, tb * 512:(tb + 1) * 512],
                   t_wa, qT_toks[-1])
    t_wkT = prep_wT(wk_h, 0, [qT_toks[-1]])
    kT_toks = proj(kT, lambda dc, oc: WT[0][:, dc, oc * 128:(oc + 1) * 128],
                   lambda dc, tb: xnT[:, dc, tb * 512:(tb + 1) * 512],
                   t_wkT, xnT_tok, extra_mm_deps=[pT_toks[-1]])
    if phase_limit <= 4:
        return _finish(nc, P, taps, locals())

    t_cn = Em("gpsimd", lambda g: g.dma_start(out=wstage[0][0:64, :], in_=cent_h),
              deps=[wst_free[0]], inc="dw", amt=16)
    t_ct = None
    for dc in range(DC):
        t_ct = Em("tensor", lambda t, dc=dc: t.transpose(
            PT[0][:, dc * 64:(dc + 1) * 64], wstage[0][0:64, dc * 128:(dc + 1) * 128],
            ident[0:64, 0:64]), deps=[t_cn, pt_free[0]], inc=("t" if dc == DC - 1 else None))
    t_centT = Em("vector", lambda v: v.tensor_copy(
        centT[:], PT[0][:, 0:512].rearrange("p (c e) -> p c e", c=DC)),
        deps=[t_ct], inc="v")
    pt_free[0] = t_centT
    wst_free[0] = t_ct
    t_bb = Em("tensor", lambda t: t.matmul(
        PS[:, 0:E], ones_sb[:], bias_sb[:], start=True, stop=True),
        deps=[t_ones, t_bias], inc="t")
    t_bbc = Em("vector", lambda v: v.tensor_copy(bias_bc[:], PS[:, 0:E]),
               deps=[t_bb], inc="v")
    t_io = Em("sync", lambda s: s.dma_start(out=iota_sb[:], in_=iota64_hbm), inc="dx", amt=16)
    t_if = Em("sync", lambda s: s.dma_start(out=identF[:], in_=identF_hbm), inc="dx", amt=16)
    t_be = Em("tensor", lambda t: t.matmul(
        PS[:, 512:512 + E], ones_sb[:], iota_sb[:], start=True, stop=True),
        deps=[t_ones, t_io], inc="t")
    t_erow = Em("vector", lambda v: v.tensor_copy(erow[:], PS[:, 512:512 + E]),
                deps=[t_be], inc="v")
    ps_free = [t_bbc, t_erow]
    prev_chain = None
    for tt in range(TT):
        pb = tt % 2
        t_mm = None
        for dc in range(DC):
            t_mm = Em("tensor", lambda t, tt=tt, dc=dc, pb=pb: t.matmul(
                PS[:, pb * 512:pb * 512 + E], xnT[:, dc, tt * 128:(tt + 1) * 128],
                centT[:, dc, :], start=(dc == 0), stop=(dc == DC - 1)),
                deps=[xnT_tile_toks[tt], t_centT, ps_free[pb]],
                inc=("t" if dc == DC - 1 else None))
        c1 = Em("vector", lambda v, pb=pb: v.tensor_copy(aff_sb[:], PS[:, pb * 512:pb * 512 + E]),
                deps=[t_mm, prev_chain], inc="v")
        ps_free[pb] = c1
        c2 = Em("vector", lambda v: v.tensor_add(biased[:], aff_sb[:], bias_bc[:]),
                deps=[c1, t_bbc], inc="v")
        c3 = Em("vector", lambda v: v.max(out=max8[:], in_=biased[:]), deps=[c2], inc="v")
        c4 = Em("vector", lambda v: v.match_replace(
            out=zap[:], in_to_replace=max8[:], in_values=biased[:], imm_value=NEG_BIG),
            deps=[c3], inc="v")
        c5 = Em("vector", lambda v: v.tensor_scalar(mask[:], zap[:], NEG_THR, None, ALU.is_le),
                deps=[c4], inc="v")
        c6 = Em("scalar", lambda sc: sc.activation(sig[:], aff_sb[:], AF.Sigmoid),
                deps=[c1, c5], inc="s")
        c7 = Em("vector", lambda v: v.tensor_mul(wm[:], sig[:], mask[:]),
                deps=[c6, c5], inc="v")
        c8 = Em("vector", lambda v: v.tensor_reduce(den1[:], wm[:], AX.X, ALU.add),
                deps=[c7], inc="v")
        c9 = Em("vector", lambda v: v.tensor_scalar(den1[:], den1[:], 1e-8, None, ALU.add),
                deps=[c8], inc="v")
        c10 = Em("vector", lambda v: v.reciprocal(rd1[:], den1[:]), deps=[c9], inc="v")
        c11 = Em("vector", lambda v, tt=tt: v.tensor_scalar(
            cw[:, tt * E:(tt + 1) * E], wm[:], rd1[:], None, ALU.mult),
            deps=[c10], inc="v")
        c12 = Em("vector", lambda v, tt=tt: v.tensor_copy(
            MKb[:, tt * E:(tt + 1) * E], mask[:]), deps=[c5], inc="v")
        pbm = tt % 2
        t_mt = Em("tensor", lambda t, tt=tt, pbm=pbm: t.transpose(
            PT[pbm][0:64, 0:128], MKb[:, tt * E:(tt + 1) * E], ident[:]),
            deps=[c12, pt_free[pbm]], inc="t")
        t_mte = Em("vector", lambda v, tt=tt, pbm=pbm: v.tensor_copy(
            MT[:, tt * 128:(tt + 1) * 128], PT[pbm][0:64, 0:128]),
            deps=[t_mt], inc="v")
        pt_free[pbm] = t_mte
        prev_chain = t_mte
    cw_done = prev_chain

    t_zr = Em("vector", lambda v: v.memset(ZROW[:], 0.0), deps=[cw_done], inc="v")
    cum_toks = []
    for h in range(NH):
        cum_toks.append(Em("vector", lambda v, h=h: v.tensor_tensor_scan(
            CUM[:, h * TH:(h + 1) * TH], MT[:, h * TH:(h + 1) * TH], ZROW[:],
            0.0, ALU.add, ALU.add), deps=[t_zr, cw_done], inc="v"))
    t_r1 = Em("vector", lambda v: v.memset(VALS[:], 1.0), deps=[cum_toks[-1]], inc="v")
    t_r2 = Em("vector", lambda v: v.tensor_tensor_scan(
        RAMP[:], VALS[:], ZROW[:], 0.0, ALU.add, ALU.add), deps=[t_r1], inc="v")
    cumsum_done = t_r2

    ext_state = {"h": -1, "r": 32, "src": 0, "prep": None, "last": cumsum_done}
    ext_bufs = None

    def emit_extract_some(n):
        for _ in range(n):
            st = ext_state
            if st["r"] >= CAPH // 8:
                if st["h"] >= NH - 1:
                    return
                st["h"] += 1
                h = st["h"]
                t_v1 = Em("vector", lambda v: v.tensor_scalar(
                    VALS[:], RAMP[:], -1.0, float(TH + 1), ALU.mult, ALU.add),
                    deps=[st["last"], cumsum_done], inc="v")
                t_v2 = Em("vector", lambda v, h=h: v.tensor_mul(
                    VALS[:], VALS[:], MT[:, h * TH:(h + 1) * TH]),
                    deps=[t_v1], inc="v")
                st["last"] = t_v2
                st["r"] = 0
                st["src"] = 0
            h, r = st["h"], st["r"]
            srcb = VALS if st["src"] == 0 else MT[:, h * TH:(h + 1) * TH]
            dstb = MT[:, h * TH:(h + 1) * TH] if st["src"] == 0 else VALS
            t_m = Em("vector", lambda v, srcb=srcb: v.max(out=EXV[:], in_=srcb),
                     deps=[st["last"]], inc="v")
            t_z = Em("vector", lambda v, srcb=srcb, dstb=dstb: v.match_replace(
                out=dstb, in_to_replace=EXV[:], in_values=srcb, imm_value=0.0),
                deps=[t_m], inc="v")
            t_i = Em("vector", lambda v, h=h, r=r: v.tensor_scalar(
                IDXL[:, h * CAPH + 8 * r:h * CAPH + 8 * r + 8], EXV[:],
                -1.0, float(TH + h * TH), ALU.mult, ALU.add),
                deps=[t_m], inc="v")
            t_i2 = Em("vector", lambda v, h=h, r=r: v.tensor_scalar_min(
                IDXL[:, h * CAPH + 8 * r:h * CAPH + 8 * r + 8],
                IDXL[:, h * CAPH + 8 * r:h * CAPH + 8 * r + 8], float(S - 1)),
                deps=[t_i], inc="v")
            st["last"] = t_z
            st["r"] += 1
            st["src"] ^= 1

    s_toks = []
    s_wr_toks = []
    sstg_free = [None, None]
    e_free = None
    ps_av_free = max(t for t in ps_free if t is not None)
    at_free = [pt_free[0], pt_free[1]]
    alpha_used = None
    nmp_free = None
    for it in range(TT):
        pm_toks = []
        for jb in range(4):
            t_mm = None
            for kb in range(DC):
                t_mm = Em("tensor", lambda t, it=it, jb=jb, kb=kb: t.matmul(
                    PE[:, jb * 512:(jb + 1) * 512],
                    pT[:, kb, it * 128:(it + 1) * 128],
                    kT[:, kb, jb * 512:(jb + 1) * 512],
                    start=(kb == 0), stop=(kb == DC - 1)),
                    deps=[pT_toks[-1], kT_toks[-1], e_free],
                    inc=("t" if kb == DC - 1 else None))
            pm_toks.append(Em("vector", lambda v, jb=jb: v.reduce_max(
                nmp[:, jb:jb + 1], PE[:, jb * 512:(jb + 1) * 512], AX.X),
                deps=[t_mm, nmp_free], inc="v"))
        t_nm = Em("vector", lambda v, it=it: v.tensor_reduce(
            negm16[:, it:it + 1], nmp[:], AX.X, ALU.max, negate=True),
            deps=pm_toks, inc="v")
        nmp_free = t_nm
        exp_toks = []
        for c in range(4):
            exp_toks.append(Em("scalar", lambda sc, it=it, c=c: sc.activation(
                alpha_sb[:, c * 512:(c + 1) * 512], PE[:, c * 512:(c + 1) * 512],
                AF.Exp, bias=negm16[:, it:it + 1]),
                deps=[t_nm, alpha_used if c == 0 else None], inc="s"))
        e_free = exp_toks[-1]
        at_evac = []
        for gb in range(2):
            t_tr = None
            for k in range(8):
                jb8 = gb * 8 + k
                t_tr = Em("tensor", lambda t, jb8=jb8, k=k, gb=gb: t.transpose(
                    PT[gb][:, k * 128:(k + 1) * 128],
                    alpha_sb[:, jb8 * 128:(jb8 + 1) * 128], ident[:]),
                    deps=[exp_toks[jb8 // 4], at_free[gb]], inc=("t" if k == 7 else None))
            t_ev = Em("vector", lambda v, gb=gb: v.tensor_copy(
                alphaT_sb[:, gb * 1024:(gb + 1) * 1024], PT[gb][:]),
                deps=[t_tr], inc="v")
            at_free[gb] = t_ev
            at_evac.append(t_ev)
        t_rs = Em("vector", lambda v, it=it: v.reduce_sum(
            rowsum16[:, it:it + 1], alpha_sb[:], AX.X), deps=[exp_toks[-1]], inc="v")
        t_rc = Em("vector", lambda v, it=it: v.reciprocal(
            recip16[:, it:it + 1], rowsum16[:, it:it + 1]), deps=[t_rs], inc="v")
        alpha_used = t_rs
        emit_extract_some(5)
        av_groups = []
        for wh in range(2):
            t_mm = None
            for j in range(16):
                t_mm = Em("tensor", lambda t, wh=wh, j=j: t.matmul(
                    PS[:, wh * 512:(wh + 1) * 512],
                    alphaT_sb[:, j * 128:(j + 1) * 128],
                    u[:, j, wh * 512:(wh + 1) * 512],
                    start=(j == 0), stop=(j == 15)),
                    deps=[at_evac[j // 8], u_toks[-1], ps_av_free],
                    inc=("t" if j == 15 else None))
            av_groups.append(t_mm)
        t_s = Em("vector", lambda v, it=it: v.tensor_scalar(
            sstg[it % 2][:], PS[:, 0:1024], recip16[:, it:it + 1], None, ALU.mult),
            deps=av_groups + [t_rc, sstg_free[it % 2]], inc="v")
        ps_av_free = t_s
        t_sw = Em("sync", lambda s, it=it: s.dma_start(
            out=sdr_h[it * 128:(it + 1) * 128, :], in_=sstg[it % 2][:]),
            deps=[t_s], inc="dy", amt=16)
        sstg_free[it % 2] = t_sw
        s_wr_toks.append(t_sw)
        s_toks.append(t_s)
    if phase_limit <= 5:
        return _finish(nc, P, taps, locals())

    emit_extract_some(200)
    ixr = ext_state["last"]
    ixt_toks = []
    ps6_free = [s_toks[-1], s_toks[-1]]
    for c in range(4):
        pb6 = c % 2
        t_tp = Em("tensor", lambda t, c=c, pb6=pb6: t.transpose(
            PS[0:128, pb6 * 512:pb6 * 512 + 64], IDXL[:, c * 128:(c + 1) * 128],
            identF[0:64, 0:64]), deps=[ixr, ps6_free[pb6], t_if], inc="t")
        t_ix = Em("vector", lambda v, c=c, pb6=pb6: v.tensor_copy(
            IXT[:, c * E:(c + 1) * E], PS[:, pb6 * 512:pb6 * 512 + 64]),
            deps=[t_tp], inc="v")
        ps6_free[pb6] = t_ix
        ixt_toks.append(t_ix)

    def acc_ap(g):
        return acc[:, g, :] if g < TTH else accG[:, g - TTH, :]

    t_wsu = Em("gpsimd", lambda g: g.dma_start(
        out=Wu_b[0][:], in_=wsup_h.rearrange("(c p) f -> p c f", p=128)),
        deps=[s_toks[-1]], inc="dw", amt=16)
    t_wsd = Em("gpsimd", lambda g: g.dma_start(
        out=Wd_b[0][:], in_=wsdn_h.rearrange("(c p) f -> p c f", p=128)),
        deps=[s_toks[-1]], inc="dw", amt=16)
    up_bank_free = [None] * 3
    dn_bank_free = [ps6_free[0], ps6_free[1]]
    ubr = [0]
    hsh_free = [alpha_used]
    shup_last = None
    sh_acc_toks = [None] * TT
    for h in range(NH):
        sil_toks = []
        for tb in range(2):
            for fc in range(FC):
                bank = ubr[0] % 3
                ubr[0] += 1
                t_mm = None
                for dc in range(DC):
                    t_mm = Em("tensor", lambda t, h=h, tb=tb, fc=fc, dc=dc, bank=bank: t.matmul(
                        PE[:, bank * 512:(bank + 1) * 512],
                        Wu_b[0][:, dc, fc * 128:(fc + 1) * 128],
                        xnT[:, dc, h * TH + tb * 512:h * TH + (tb + 1) * 512],
                        start=(dc == 0), stop=(dc == DC - 1)),
                        deps=[t_wsu, xnT_tok, up_bank_free[bank]],
                        inc=("t" if dc == DC - 1 else None))
                shup_last = t_mm
                t_h = Em("scalar", lambda sc, tb=tb, fc=fc, bank=bank: sc.activation(
                    Hsh[:, fc, tb * 512:(tb + 1) * 512],
                    PE[:, bank * 512:(bank + 1) * 512], AF.Silu),
                    deps=[t_mm] + hsh_free, inc="s")
                up_bank_free[bank] = t_h
                sil_toks.append(t_h)
        last_dn = None
        for tl in range(TTH):
            for dh in range(2):
                bank = (tl * 2 + dh) % 2
                t_mm = None
                for fc in range(FC):
                    t_mm = Em("tensor", lambda t, tl=tl, dh=dh, fc=fc, bank=bank: t.matmul(
                        PS[:, bank * 512:(bank + 1) * 512],
                        Hsh[:, fc, tl * 128:(tl + 1) * 128],
                        Wd_b[0][:, fc, dh * 512:(dh + 1) * 512],
                        start=(fc == 0), stop=(fc == FC - 1)),
                        deps=[t_wsd, dn_bank_free[bank]] + sil_toks,
                        inc=("t" if fc == FC - 1 else None))
                last_dn = t_mm
                t_ev = Em("vector", lambda v, h=h, tl=tl, dh=dh, bank=bank: v.tensor_copy(
                    acc_ap(h * TTH + tl)[:, dh * 512:(dh + 1) * 512],
                    PS[:, bank * 512:(bank + 1) * 512]),
                    deps=[t_mm, s_toks[-1]], inc="v")
                dn_bank_free[bank] = t_ev
                sh_acc_toks[h * TTH + tl] = t_ev
        hsh_free = [last_dn]
    shdn_last = hsh_free[0]

    pe3_free = [s_toks[-1]]
    flu_toks = [None] * TT
    wj_toks = [None] * TT
    p2_next = [0]

    def emit_pass2_tile():
        tt = p2_next[0]
        if tt >= TT:
            return
        p2_next[0] += 1
        sec = float((CAPH if tt >= TTH else 0) - 1)
        t_tp = Em("tensor", lambda t, tt=tt: t.transpose(
            PE[0:128, 1536:1536 + 64], CUM[:, tt * 128:(tt + 1) * 128],
            identF[0:64, 0:64]), deps=[cumsum_done, pe3_free[0], t_if], inc="t")
        t_pe = Em("vector", lambda v: v.tensor_copy(
            aff_sb[:], PE[:, 1536:1536 + 64]), deps=[t_tp], inc="v")
        pe3_free[0] = t_pe
        s1 = Em("vector", lambda v: v.scalar_tensor_tensor(
            zap[:], erow[:], float(CAP), aff_sb[:], ALU.mult, ALU.add),
            deps=[t_pe, t_erow], inc="v")
        s2 = Em("vector", lambda v, sec=sec: v.tensor_scalar(
            zap[:], zap[:], sec, None, ALU.add), deps=[s1], inc="v")
        s3 = Em("vector", lambda v: v.tensor_scalar(
            mask[:], zap[:], -1.0, BIGK, ALU.mult, ALU.add), deps=[s2], inc="v")
        s4 = Em("vector", lambda v, tt=tt: v.tensor_mul(
            mask[:], mask[:], MKb[:, tt * E:(tt + 1) * E]), deps=[s3], inc="v")
        s5 = Em("vector", lambda v: v.max(out=max8[:], in_=mask[:]), deps=[s4], inc="v")
        s6 = Em("vector", lambda v: v.tensor_scalar(
            sig[:, 0:8], max8[:], -1.0, BIGK, ALU.mult, ALU.add), deps=[s5], inc="v")
        flu_toks[tt] = Em("vector", lambda v, tt=tt: v.tensor_copy(
            FLU[:, tt * 8:(tt + 1) * 8], sig[:, 0:8]), deps=[s6], inc="v")
        prev = flu_toks[tt]
        for j in range(8):
            o1 = Em("vector", lambda v, j=j: v.tensor_scalar(
                wm[:], mask[:], max8[:, j:j + 1], None, ALU.is_equal),
                deps=[prev], inc="v")
            o2 = Em("vector", lambda v, tt=tt: v.tensor_mul(
                wm[:], wm[:], cw[:, tt * E:(tt + 1) * E]), deps=[o1], inc="v")
            prev = Em("vector", lambda v, tt=tt, j=j: v.reduce_sum(
                WJ[:, tt * 8 + j:tt * 8 + j + 1], wm[:], AX.X), deps=[o2], inc="v")
        wj_toks[tt] = prev

    pt6_free = [pt_free[0], pt_free[1]]
    xg_free = [[shup_last], [shup_last]]
    xgt_free = [[shup_last], [shup_last]]
    hg_free = [shdn_last]
    ygst_free = [None, None]
    wu_free = [shup_last, s_toks[-1]]
    wd_free = [shdn_last, s_toks[-1]]
    yg_last = None
    for e in range(E):
        par = e % 2
        t_wu = Em("gpsimd", lambda g, e=e, par=par: g.dma_start(
            out=Wu_b[par][:], in_=wup_h[e].rearrange("(c p) f -> p c f", p=128)),
            deps=[wu_free[par]], inc="dw", amt=16)
        t_wd = Em("gpsimd", lambda g, e=e, par=par: g.dma_start(
            out=Wd_b[par][:], in_=wdn_h[e].rearrange("(c p) f -> p c f", p=128)),
            deps=[wd_free[par]], inc="dw", amt=16)
        gath = []
        for c in range(4):
            gath.append(Em("gpsimd", lambda g, e=e, c=c, par=par: g.indirect_dma_start(
                out=XG[par][:, c, :], out_offset=None, in_=xnd_h,
                in_offset=IndirectOffsetOnAxis(IXT[:, c * E + e:c * E + e + 1], 0)),
                deps=[ixt_toks[c], xnd_toks[-1]] + xg_free[par], inc="dw", amt=16))
        xg_free[par] = []
        xgt_wr = []
        for c in range(4):
            pb6 = c % 2
            t_tr = None
            for dj in range(DC):
                t_tr = Em("tensor", lambda t, c=c, dj=dj, par=par, pb6=pb6: t.transpose(
                    PT[pb6][:, dj * 128:(dj + 1) * 128],
                    XG[par][:, c, dj * 128:(dj + 1) * 128], ident[:]),
                    deps=[gath[c], pt6_free[pb6]] + xgt_free[par],
                    inc=("t" if dj == DC - 1 else None))
            xg_free[par].append(t_tr)
            t_xe = Em("vector", lambda v, c=c, par=par, pb6=pb6: v.tensor_copy(
                XGT[par][:, :, c * 128:(c + 1) * 128],
                PT[pb6].rearrange("p (dj q) -> p dj q", dj=DC)),
                deps=[t_tr], inc="v")
            pt6_free[pb6] = t_xe
            xgt_wr.append(t_xe)
        xgt_free[par] = []
        sil_toks = []
        for fb in range(FC):
            bank = ubr[0] % 3
            ubr[0] += 1
            t_mm = None
            for dc in range(DC):
                t_mm = Em("tensor", lambda t, fb=fb, dc=dc, par=par, bank=bank: t.matmul(
                    PE[:, bank * 512:(bank + 1) * 512],
                    Wu_b[par][:, dc, fb * 128:(fb + 1) * 128],
                    XGT[par][:, dc, :],
                    start=(dc == 0), stop=(dc == DC - 1)),
                    deps=[t_wu, up_bank_free[bank]] + xgt_wr,
                    inc=("t" if dc == DC - 1 else None))
            xgt_free[par].append(t_mm)
            t_h = Em("scalar", lambda sc, fb=fb, bank=bank: sc.activation(
                Hg[:, fb, :], PE[:, bank * 512:(bank + 1) * 512], AF.Silu),
                deps=[t_mm] + hg_free, inc="s")
            up_bank_free[bank] = t_h
            sil_toks.append(t_h)
        wu_free[par] = t_mm
        last_dn = None
        for sb in range(4):
            for dh in range(2):
                bank = (sb * 2 + dh) % 2
                t_mm = None
                for fb in range(FC):
                    t_mm = Em("tensor", lambda t, sb=sb, dh=dh, fb=fb, par=par, bank=bank: t.matmul(
                        PS[:, bank * 512:(bank + 1) * 512],
                        Hg[:, fb, sb * 128:(sb + 1) * 128],
                        Wd_b[par][:, fb, dh * 512:(dh + 1) * 512],
                        start=(fb == 0), stop=(fb == FC - 1)),
                        deps=[t_wd, dn_bank_free[bank]] + sil_toks,
                        inc=("t" if fb == FC - 1 else None))
                last_dn = t_mm
                buf = (sb // 2) % 2
                t_ev = Em("vector", lambda v, sb=sb, dh=dh, bank=bank, buf=buf: v.tensor_copy(
                    Ygst[buf][:, sb % 2, dh * 512:(dh + 1) * 512],
                    PS[:, bank * 512:(bank + 1) * 512]),
                    deps=[t_mm, ygst_free[buf]], inc="v")
                dn_bank_free[bank] = t_ev
                if sb % 2 == 1 and dh == 1:
                    pair = sb // 2
                    yg_last = Em("sync", lambda s, e=e, pair=pair, buf=buf: s.dma_start(
                        out=ygd_h[e * CAP + pair * 256:e * CAP + (pair + 1) * 256, :]
                            .rearrange("(c p) d -> p c d", p=128),
                        in_=Ygst[buf][:]),
                        deps=[dn_bank_free[0], dn_bank_free[1]], inc="dy", amt=16)
                    ygst_free[buf] = yg_last
        wd_free[par] = last_dn
        hg_free = [last_dn]
        if e % 4 == 0:
            emit_pass2_tile()
    while p2_next[0] < TT:
        emit_pass2_tile()

    if phase_limit <= 6:
        return _finish(nc, P, taps, locals())
    xstf = [at(f"xstf{i}", i * 4096, [128, D], F32) for i in range(2)]
    gacc_free = [[yg_last], [yg_last]]
    xs6_free = [list(xg_free[0]), list(xg_free[0])]
    os_free = [[yg_last], [yg_last], [yg_last]]
    gsil_free = [hg_free, hg_free]
    out_dma_toks = []
    for tt in range(TT):
        prev_stt = sh_acc_toks[tt]
        for j in range(8):
            gb6 = j % 2
            t_ga = Em("gpsimd", lambda g, tt=tt, j=j, gb6=gb6: g.indirect_dma_start(
                out=gacc[gb6][:], out_offset=None, in_=ygd_h,
                in_offset=IndirectOffsetOnAxis(FLU[:, tt * 8 + j:tt * 8 + j + 1], 0)),
                deps=[flu_toks[tt], yg_last] + gacc_free[gb6], inc="dw", amt=16)
            prev_stt = Em("vector", lambda v, tt=tt, j=j, gb6=gb6: v.scalar_tensor_tensor(
                acc_ap(tt)[:], gacc[gb6][:], WJ[:, tt * 8 + j:tt * 8 + j + 1],
                acc_ap(tt)[:], ALU.mult, ALU.add),
                deps=[t_ga, wj_toks[tt], prev_stt], inc="v")
            gacc_free[gb6] = [prev_stt]
        if phase_limit <= 7:
            continue
        t_g = Em("scalar", lambda sc, tt=tt: sc.activation(
            gsil[tt % 2][:], acc_ap(tt)[:], AF.Silu),
            deps=[prev_stt] + gsil_free[tt % 2], inc="s")
        b = tt % 2
        t_dx = Em("sync", lambda s, tt=tt, b=b: s.dma_start(
            out=xstf[b][:], in_=x_h[tt * 128:(tt + 1) * 128, :]),
            deps=xs6_free[b] if isinstance(xs6_free[b], list) else [xs6_free[b]],
            inc="dx", amt=16)
        t_sl = Em("sync", lambda s, tt=tt: s.dma_start(
            out=sstg[tt % 2][:], in_=sdr_h[tt * 128:(tt + 1) * 128, :]),
            deps=[s_wr_toks[tt]] + sstg_free[tt % 2] if isinstance(sstg_free[tt % 2], list)
                 else [s_wr_toks[tt], sstg_free[tt % 2]],
            inc="dy", amt=16)
        ob = tt % 3
        t_m1 = Em("vector", lambda v, tt=tt, ob=ob: v.tensor_mul(
            ostage[ob][:], sstg[tt % 2][:], gsil[tt % 2][:]),
            deps=[t_g, t_sl] + os_free[ob], inc="v")
        sstg_free[tt % 2] = [t_m1]
        gsil_free[tt % 2] = [t_m1]
        t_m2 = Em("vector", lambda v, b=b, ob=ob: v.tensor_add(
            ostage[ob][:], ostage[ob][:], xstf[b][:]),
            deps=[t_m1, t_dx], inc="v")
        xs6_free[b] = [t_m2]
        t_o = Em("sync", lambda s, tt=tt, ob=ob: s.dma_start(
            out=y_h[tt * 128:(tt + 1) * 128, :], in_=ostage[ob][:]),
            deps=[t_m2], inc="dx", amt=16)
        os_free[ob] = [t_o]
        out_dma_toks.append(t_o)

    if phase_limit <= 7:
        Em("sync", lambda s: None, deps=P.final_deps())
    else:
        Em("sync", lambda s: None, deps=[out_dma_toks[-1], yg_last])
    return _finish(nc, P, taps, locals())


def _finish(nc, P, taps, env):
    tap_outs = []
    for name, key in taps:
        t = env[key]
        shape = list(t.shape)
        dt = t.tensor.dtype if hasattr(t, "tensor") else t.dtype
        o = nc.dram_tensor(name, shape, dt, kind="ExternalOutput").ap()
        P.emit("sync", lambda s, o=o, t=t: s.dma_start(out=o, in_=t[:]),
               deps=P.final_deps(), inc="dx", amt=16)
        tap_outs.append(name)
    if tap_outs:
        P.emit("sync", lambda s: None, deps=P.final_deps())
    with ExitStack() as ctx:
        sems = {n: ctx.enter_context(nc.semaphore(n)) for n in P.sem_names}
        block = ctx.enter_context(nc.Block())
        P.build(block, sems)
    return nc


_NC_CACHE = {}


def _get_nc(phase_limit=99, taps=()):
    key = (phase_limit, tuple(taps))
    if key not in _NC_CACHE:
        _NC_CACHE[key] = build_nc(phase_limit, taps)
    return _NC_CACHE[key]


def kernel(L0_attn_x, L0_attn_Wk, L0_attn_Wq, L0_attn_Wv, L0_attn_Wa, L0_attn_Wb,
           L0_ffn_centroids, L0_ffn_expert_biases, L0_ffn_Ws_up, L0_ffn_Ws_down,
           L0_ffn_W_up, L0_ffn_W_down, _phase_limit=99, _taps=()):
    nc = _get_nc(_phase_limit, _taps)
    common = dict(
        wk=np.ascontiguousarray(L0_attn_Wk), wq=np.ascontiguousarray(L0_attn_Wq),
        wv=np.ascontiguousarray(L0_attn_Wv), wa=np.ascontiguousarray(L0_attn_Wa),
        wb=np.ascontiguousarray(L0_attn_Wb),
        cent=np.ascontiguousarray(L0_ffn_centroids),
        ebias=np.ascontiguousarray(np.reshape(L0_ffn_expert_biases, (1, E))),
        wsup=np.ascontiguousarray(L0_ffn_Ws_up), wsdn=np.ascontiguousarray(L0_ffn_Ws_down),
        wup=np.ascontiguousarray(L0_ffn_W_up), wdn=np.ascontiguousarray(L0_ffn_W_down),
    )
    x = np.ascontiguousarray(L0_attn_x)
    in_maps = [dict(common, x=np.ascontiguousarray(x[c])) for c in range(8)]
    res = run_bass_kernel_spmd(nc, in_maps, core_ids=list(range(8)))
    if _taps or _phase_limit < 99:
        return res
    return np.stack([r["y"] for r in res.results], axis=0)


# version 15
# speedup vs baseline: 1.7970x; 1.0108x over previous
import sys
if "/opt/trn_rl_repo" not in sys.path:
    sys.path.insert(0, "/opt/trn_rl_repo")
from collections import defaultdict
from contextlib import ExitStack

import numpy as np
import ml_dtypes
import concourse.bass as bass
import concourse.mybir as mybir
from concourse.bass_utils import run_bass_kernel_spmd
from concourse.bass_core_rust import IndirectOffsetOnAxis

F32 = mybir.dt.float32
BF16 = mybir.dt.bfloat16
AF = mybir.ActivationFunctionType
ALU = mybir.AluOpType
AX = mybir.AxisListType

ENGINES = ("sync", "gpsimd", "tensor", "vector", "scalar")

S, D, E, FF = 2048, 1024, 64, 512
TT = S // 128
DC = D // 128
FC = FF // 128
NH = 2
TH = S // NH
TTH = TH // 128
NEG_BIG = -1e30
NEG_THR = -1e29
CAP = 512
CAPH = CAP // 2
BIGK = 65536.0


class Prog:
    def __init__(self):
        self.ops = {e: [] for e in ENGINES}
        self.sem_names = []
        self.counts = defaultdict(int)

    def sem(self, name):
        if name not in self.sem_names:
            self.sem_names.append(name)
        return name

    PARANOID = False

    def __init_rr__(self):
        pass

    def emit(self, engine, fn, deps=(), inc=None, amt=1):
        tok = None
        deps = [d for d in deps if d is not None]
        if Prog.PARANOID:
            deps = [(s, v) for s, v in self.counts.items()]
        if inc == "dw":
            rr = getattr(self, "_dw_rr", 0)
            self._dw_rr = rr + 1
            inc = f"dw{rr % 4}"
        if inc is not None:
            self.sem(inc)
            if amt == 16 and self.counts[inc] > 0:
                deps.append((inc, self.counts[inc]))
            self.counts[inc] += amt
            tok = (inc, self.counts[inc])
        self.ops[engine].append((fn, deps, inc, amt))
        return tok

    def final_deps(self):
        return [(s, self.counts[s]) for s in self.sem_names]

    def build(self, block, sems):
        for e in ENGINES:
            if not self.ops[e]:
                continue

            def make(e):
                def f(eng):
                    hw = defaultdict(int)
                    for fn, deps, inc, amt in self.ops[e]:
                        for (s, v) in deps:
                            if v > hw[s]:
                                eng.wait_ge(sems[s], v)
                                hw[s] = v
                        inst = fn(eng)
                        if inc is not None:
                            if inst is None:
                                inst = eng.nop()
                            inst.then_inc(sems[inc], amt)
                return f

            getattr(block, e)(make(e))


def build_nc(phase_limit=99, taps=()):
    nc = bass.Bass("TRN2", target_bir_lowering=False)
    x_h = nc.dram_tensor("x", [S, D], F32, kind="ExternalInput").ap()
    wk_h = nc.dram_tensor("wk", [D, D], F32, kind="ExternalInput").ap()
    wq_h = nc.dram_tensor("wq", [D, D], F32, kind="ExternalInput").ap()
    wv_h = nc.dram_tensor("wv", [D, D], F32, kind="ExternalInput").ap()
    wa_h = nc.dram_tensor("wa", [D, D], F32, kind="ExternalInput").ap()
    wb_h = nc.dram_tensor("wb", [D, D], F32, kind="ExternalInput").ap()
    cent_h = nc.dram_tensor("cent", [E, D], F32, kind="ExternalInput").ap()
    ebias_h = nc.dram_tensor("ebias", [1, E], F32, kind="ExternalInput").ap()
    wsup_h = nc.dram_tensor("wsup", [D, FF], F32, kind="ExternalInput").ap()
    wsdn_h = nc.dram_tensor("wsdn", [FF, D], F32, kind="ExternalInput").ap()
    wup_h = nc.dram_tensor("wup", [E, D, FF], F32, kind="ExternalInput").ap()
    wdn_h = nc.dram_tensor("wdn", [E, FF, D], F32, kind="ExternalInput").ap()
    y_h = nc.dram_tensor("y", [S, D], F32, kind="ExternalOutput").ap()
    xnd_h = nc.dram_tensor("xnd", [S, D], BF16, kind="Internal").ap()
    sdr_h = nc.dram_tensor("sdr", [S, D], BF16, kind="Internal").ap()
    ygd_h = nc.dram_tensor("ygd", [E * CAP, D], BF16, kind="Internal").ap()

    ident_hbm = nc.inline_tensor(np.eye(128, dtype=ml_dtypes.bfloat16), name="identc").ap()
    ones_hbm = nc.inline_tensor(np.ones((1, 128), dtype=np.float32), name="onesc").ap()
    iota64_hbm = nc.inline_tensor(np.arange(E, dtype=np.float32).reshape(1, E), name="iotac").ap()
    identF_hbm = nc.inline_tensor(np.eye(128, dtype=np.float32), name="identfc").ap()

    ARENA_BYTES = 208896
    arena = nc.alloc_sbuf_tensor("ARENA", [128, ARENA_BYTES], mybir.dt.uint8)
    SBUF_BASE = nc.lookup_mloc(arena).addr

    def at(name, off, shape, dt):
        assert 0 <= off and off < ARENA_BYTES
        return nc.alloc_sbuf_tensor_at(name, shape, dt, offset=SBUF_BASE + off).ap()

    xnT = at("xnT", 0, [128, DC, S], BF16)
    cw = at("cw", 32768, [128, TT * E], F32)
    co = 36864
    ident = at("ident", co + 0, [128, 128], BF16); co += 256
    centT = at("centT", co, [128, DC, E], BF16); co += 1024
    bias_sb = at("bias_sb", co, [1, E], F32); co += 256
    bias_bc = at("bias_bc", co, [128, E], F32); co += 256
    ones_sb = at("ones_sb", co, [1, 128], F32); co += 512
    aff_sb = at("aff_sb", co, [128, E], F32); co += 256
    biased = at("biased", co, [128, E], F32); co += 256
    max8 = at("max8", co, [128, 8], F32); co += 32
    zap = at("zap", co, [128, E], F32); co += 256
    mask = at("mask", co, [128, E], F32); co += 256
    sig = at("sig", co, [128, E], F32); co += 256
    wm = at("wm", co, [128, E], F32); co += 256
    st6 = at("st6", co, [128, 12], F32); co += 64
    mv2 = at("mv2", co, [128, 2 * TT], F32); co += 128
    nmp = at("nmp", co, [128, 4], F32); co += 32
    epsv = at("epsv", co, [128, 1], F32); co += 32
    rcp16 = at("rcp16", co, [128, TT], F32); co += 64
    rstd16 = at("rstd16", co, [128, TT], F32); co += 64
    negm16 = at("negm16", co, [128, TT], F32); co += 64
    rowsum16 = at("rowsum16", co, [128, TT], F32); co += 64
    recip16 = at("recip16", co, [128, TT], F32); co += 64
    den1 = at("den1", co, [128, 1], F32); co += 64
    rd1 = at("rd1", co, [128, 1], F32); co += 64
    iota_sb = at("iota_sb", co, [1, E], F32); co += 256
    erow = at("erow", co, [128, E], F32); co += 256
    identF = at("identF", co, [128, 128], F32); co += 512
    assert co <= 45056, co
    MKb = at("MKb", 196608, [128, TT * E], BF16)
    FLU = at("FLU", 198656, [128, TT * 8], mybir.dt.uint32)
    WJ = at("WJ", 199168, [128, TT * 8], F32)
    IXT = at("IXT", 199680, [128, 4 * E], mybir.dt.uint32)
    MT = at("MT", 110592, [E, S], F32)
    CUM = at("CUM", 110592 + 8192, [E, S], F32)
    VALS = at("VALS", 110592 + 16384, [E, TH], F32)
    IDXL = at("IDXL", 110592 + 20480, [E, CAP], F32)
    RAMP = at("RAMP", 110592 + 22528, [E, TH], F32)
    EXV = at("EXV", 110592 + 26624, [E, 8], F32)
    ZROW = at("ZROW", 110592 + 26656, [E, TH], F32)
    PTM = at("PTM", 176128, [128, TT * E], F32)
    XG = [at(f"XG{i}", 0 + i * 8192, [128, 4, D], BF16) for i in range(2)]
    XGT = [at(f"XGT{i}", 16384 + i * 8192, [128, DC, CAP], BF16) for i in range(2)]
    Hg = at("Hg", 176128, [128, FC, CAP], BF16)
    gacc = [at(f"gacc{i}", 176128 + i * 2048, [128, D], BF16) for i in range(4)]
    gsil = [at(f"gsil{i}", 204800 + i * 2048, [128, D], BF16) for i in range(2)]
    Ygst = [at(f"Ygst{i}", 184320 + i * 4096, [128, 2, D], BF16) for i in range(2)]
    accG = at("accG", 143360, [128, TTH, D], F32)
    Hsh = at("Hsh", 176128, [128, FC, TH], BF16)
    u = at("u", 45056, [128, TT, D], BF16)
    Wu_b = [at(f"Wu{i}", 45056 + i * 8192, [128, DC, FF], BF16) for i in range(2)]
    Wd_b = [at(f"Wd{i}", 45056 + 16384 + i * 8192, [128, FC, D], BF16) for i in range(2)]
    vT = at("vT", 77824, [128, DC, S], BF16)
    qT = at("qT", 77824, [128, DC, S], BF16)
    kT = at("kT", 77824, [128, DC, S], BF16)
    acc = at("acc", 77824, [128, TTH, D], F32)
    WT = [at(f"WT{i}", 110592 + i * 16384, [128, DC, D], BF16) for i in range(2)]
    sstg = [at(f"sstg{i}", 200704 + i * 2048, [128, D], BF16) for i in range(2)]
    pT = at("pT", 143360, [128, DC, S], BF16)
    wstage = [at(f"wstage{i}", 192512 + i * 2048, [128, D], BF16) for i in range(2)]
    alpha_sb = at("alpha_sb", 176128, [128, S], BF16)
    alphaT_sb = at("alphaT_sb", 176128 + 4096, [128, S], BF16)
    xn_t2 = [at(f"xn_t{i}", 184320 + i * 2048, [128, D], BF16) for i in range(2)]
    sqf = at("sqf", 184320 + 4096, [128, D], F32)
    ostage = [at(f"ostage{i}", 184320 + i * 4096, [128, D], F32) for i in range(3)]

    PE = nc.alloc_psum_tensor("PE", [128, 2048], F32).ap()
    PS = nc.alloc_psum_tensor("PS", [128, 1024], F32).ap()
    PT = [nc.alloc_psum_tensor(f"PT{i}", [128, 1024], BF16).ap() for i in range(2)]

    P = Prog()
    Em = P.emit

    LN_ONLY = (phase_limit == -1)
    if not LN_ONLY:
        t_id = Em("sync", lambda s: s.dma_start(out=ident[:], in_=ident_hbm), inc="dx", amt=16)
        t_ones = Em("sync", lambda s: s.dma_start(out=ones_sb[:], in_=ones_hbm), inc="dx", amt=16)
        t_bias = Em("sync", lambda s: s.dma_start(out=bias_sb[:], in_=ebias_h), inc="dx", amt=16)
    else:
        t_id = t_ones = t_bias = None

    t_eps = Em("vector", lambda v: v.memset(epsv[:], 1e-5), inc="v")
    xs_free = [None, None]
    pt_free = [None, None]
    xn_used = [[], []]
    xnd_toks = []
    wst_free = [None, None]
    pe_free = [None] * 4
    sq_done = None
    xnT_tok = None
    xnT_tile_toks = [None] * TT
    wv_tok = [None] * DC
    vT_ev = [[None] * 4 for _ in range(DC)]
    vT_emitted = set()
    vg = [0]

    def emit_vT_group(oc, tb):
        bank = vg[0] % 4
        vg[0] += 1
        t_mm = None
        for dc in range(DC):
            t_mm = Em("tensor", lambda t, oc=oc, tb=tb, dc=dc, bank=bank: t.matmul(
                PE[:, bank * 512:(bank + 1) * 512],
                WT[0][:, dc, oc * 128:(oc + 1) * 128],
                xnT[:, dc, tb * 512:(tb + 1) * 512],
                start=(dc == 0), stop=(dc == DC - 1)),
                deps=[wv_tok[oc], xnT_tile_toks[4 * tb + 3], pe_free[bank]],
                inc=("t" if dc == DC - 1 else None))
        t_ev = Em("scalar", lambda sc, oc=oc, tb=tb, bank=bank: sc.copy(
            vT[:, oc, tb * 512:(tb + 1) * 512], PE[:, bank * 512:(bank + 1) * 512]),
            deps=[t_mm], inc="s")
        pe_free[bank] = t_ev
        vT_ev[oc][tb] = t_ev
        vT_emitted.add((oc, tb))

    def emit_ready_vT(tt):
        for tb in range(4):
            if 4 * tb + 3 > tt:
                break
            for oc in range(DC):
                if 2 * oc + 1 <= tt and (oc, tb) not in vT_emitted:
                    emit_vT_group(oc, tb)

    xp = [at(f"xp{i}", 45056 + i * 8192, [128, 2, D], F32) for i in range(2)]
    e4s = [None] * TT
    ags = [None] * TT
    xsrcs = [None] * TT

    def p0_apply(tt):
        nonlocal xnT_tok
        b = (tt // 2) % 2
        xn_t = xn_t2[tt % 2]
        xsrc = xsrcs[tt]
        t_f = Em("vector", lambda v, tt=tt, xsrc=xsrc, xn_t=xn_t: v.tensor_scalar(
            xn_t[:], xsrc[:], mv2[:, 2 * tt:2 * tt + 1], rstd16[:, tt:tt + 1],
            ALU.subtract, ALU.mult), deps=[e4s[tt], ags[tt]] + list(xn_used[tt % 2]), inc="v")
        xs_free[b] = t_f
        t_tr = None
        for dc in range(DC):
            t_tr = Em("tensor", lambda t, dc=dc, xn_t=xn_t: t.transpose(
                PT[0][:, dc * 128:(dc + 1) * 128], xn_t[:, dc * 128:(dc + 1) * 128],
                ident[:]), deps=[t_f, t_id, pt_free[0]],
                inc=("t" if dc == DC - 1 else None))
        t_xnd = Em("sync", lambda s, tt=tt, xn_t=xn_t: s.dma_start(
            out=xnd_h[tt * 128:(tt + 1) * 128, :], in_=xn_t[:]),
            deps=[t_f], inc="dn", amt=16)
        xnd_toks.append(t_xnd)
        xn_used[tt % 2] = [t_tr, t_xnd]
        for dc in range(DC):
            xnT_tok = Em("scalar", lambda sc, tt=tt, dc=dc: sc.copy(
                xnT[:, dc, tt * 128:(tt + 1) * 128],
                PT[0][:, dc * 128:(dc + 1) * 128]), deps=[t_tr], inc="s")
        pt_free[0] = xnT_tok
        xnT_tile_toks[tt] = xnT_tok
        if tt % 2 == 1:
            rc = (tt - 1) // 2
            wb_ = rc % 2
            t_w = Em("gpsimd", lambda g, rc=rc, wb_=wb_: g.dma_start(
                out=wstage[wb_][:], in_=wv_h[rc * 128:(rc + 1) * 128, :]),
                deps=[wst_free[wb_]], inc="dw", amt=16)
            t_wtr = None
            for dc in range(DC):
                t_wtr = Em("tensor", lambda t, dc=dc, wb_=wb_: t.transpose(
                    PT[1][:, dc * 128:(dc + 1) * 128],
                    wstage[wb_][:, dc * 128:(dc + 1) * 128], ident[:]),
                    deps=[t_w, t_id, pt_free[1]], inc=("t" if dc == DC - 1 else None))
            wst_free[wb_] = t_wtr
            wv_tok[rc] = Em("scalar", lambda sc, rc=rc: sc.copy(
                WT[0][:, :, rc * 128:(rc + 1) * 128],
                PT[1].rearrange("p (c q) -> p c q", c=DC)), deps=[t_wtr], inc="s")
            pt_free[1] = wv_tok[rc]
        emit_ready_vT(tt)

    t_dx = None
    for tt in range(TT + 1):
        if tt < TT:
            b = (tt // 2) % 2
            if tt % 2 == 0:
                q = tt // 2
                t_dx = Em("sync", lambda s, q=q, b=b: s.dma_start(
                    out=xp[b][:], in_=x_h[q * 256:(q + 1) * 256, :].rearrange(
                        "(c p) d -> p c d", p=128)),
                    deps=[xs_free[b]], inc="dx", amt=16)
            xsrcs[tt] = xp[b][:, tt % 2, :]
            t_b1 = Em("vector", lambda v, tt=tt: v.bn_stats(
                st6[:, 0:6], xsrcs[tt][:, 0:512]), deps=[t_dx, sq_done], inc="v")
            t_b2 = Em("vector", lambda v, tt=tt: v.bn_stats(
                st6[:, 6:12], xsrcs[tt][:, 512:1024]), deps=[t_b1], inc="v")
            t_ag = Em("vector", lambda v, tt=tt: v.bn_aggr(
                mv2[:, 2 * tt:2 * tt + 2], st6[:]), deps=[t_b2], inc="v")
            sq_done = t_ag
            ags[tt] = t_ag
            t_e3 = Em("scalar", lambda sc, tt=tt: sc.activation(
                rcp16[:, tt:tt + 1], mv2[:, 2 * tt + 1:2 * tt + 2], AF.Ln, bias=epsv[:]),
                deps=[t_ag, t_eps], inc="s")
            e4s[tt] = Em("scalar", lambda sc, tt=tt: sc.activation(
                rstd16[:, tt:tt + 1], rcp16[:, tt:tt + 1], AF.Exp, scale=-0.5),
                deps=[t_e3], inc="s")
        if not LN_ONLY and tt >= 1:
            p0_apply(tt - 1)
    if LN_ONLY:
        return _finish(nc, P, taps, locals())
    emit_ready_vT(99)
    xs_last = max(t for t in xs_free if t is not None)
    t_wvT = wv_tok[-1]
    vT_toks = [vT_ev[oc][tb] for oc in range(DC) for tb in range(4)]
    if phase_limit <= 0:
        return _finish(nc, P, taps, locals())

    def prep_wT(w_hbm, slot, deps_extra):
        nonlocal wst_free
        done = None
        for rc in range(DC):
            b = rc % 2
            t_w = Em("gpsimd", lambda g, rc=rc, b=b, w_hbm=w_hbm: g.dma_start(
                out=wstage[b][:], in_=w_hbm[rc * 128:(rc + 1) * 128, :]),
                deps=[wst_free[b]] + deps_extra, inc="dw", amt=16)
            pb = rc % 2
            t_tr = None
            for dc in range(DC):
                t_tr = Em("tensor", lambda t, rc=rc, dc=dc, b=b, pb=pb: t.transpose(
                    PT[pb][:, dc * 128:(dc + 1) * 128],
                    wstage[b][:, dc * 128:(dc + 1) * 128], ident[:]),
                    deps=[t_w, pt_free[pb]], inc=("t" if dc == DC - 1 else None))
            wst_free[b] = t_tr
            done = Em("vector", lambda v, rc=rc, slot=slot, pb=pb: v.tensor_copy(
                WT[slot][:, :, rc * 128:(rc + 1) * 128],
                PT[pb].rearrange("p (c q) -> p c q", c=DC)), deps=[t_tr], inc="v")
            pt_free[pb] = done
        return done

    def load_nat(w_hbm, slot, deps_extra):
        return Em("gpsimd", lambda g, w_hbm=w_hbm, slot=slot: g.dma_start(
            out=WT[slot][:], in_=w_hbm.rearrange("(c p) k -> p c k", p=128)),
            deps=deps_extra, inc="dw", amt=16)

    def proj(dst, lhs_of, rhs_of, dep_lhs, dep_rhs, extra_mm_deps=()):
        toks = []
        g = 0
        for oc in range(DC):
            for tb in range(4):
                bank = g % 4
                t_mm = None
                for dc in range(DC):
                    t_mm = Em("tensor", lambda t, oc=oc, tb=tb, dc=dc, bank=bank: t.matmul(
                        PE[:, bank * 512:(bank + 1) * 512], lhs_of(dc, oc), rhs_of(dc, tb),
                        start=(dc == 0), stop=(dc == DC - 1)),
                        deps=[dep_lhs, dep_rhs, pe_free[bank]] + list(extra_mm_deps),
                        inc=("t" if dc == DC - 1 else None))
                t_ev = Em("vector", lambda v, oc=oc, tb=tb, bank=bank, dst=dst: v.tensor_copy(
                    dst[:, oc, tb * 512:(tb + 1) * 512], PE[:, bank * 512:(bank + 1) * 512]),
                    deps=[t_mm], inc="v")
                pe_free[bank] = t_ev
                toks.append(t_ev)
                g += 1
        return toks

    vT_last = max((t for row in vT_ev for t in row), key=lambda x: x[1])
    t_wbT = prep_wT(wb_h, 1, [])
    u_toks = []
    g = 0
    for tt in range(TT):
        for wh in range(2):
            bank = g % 4
            t_mm = None
            for vc in range(DC):
                t_mm = Em("tensor", lambda t, tt=tt, wh=wh, vc=vc, bank=bank: t.matmul(
                    PE[:, bank * 512:(bank + 1) * 512],
                    vT[:, vc, tt * 128:(tt + 1) * 128],
                    WT[1][:, vc, wh * 512:(wh + 1) * 512],
                    start=(vc == 0), stop=(vc == DC - 1)),
                    deps=[t_wbT, pe_free[bank]] + [vT_ev[oc][tt // 4] for oc in range(DC)],
                    inc=("t" if vc == DC - 1 else None))
            t_ev = Em("vector", lambda v, tt=tt, wh=wh, bank=bank: v.tensor_copy(
                u[:, tt, wh * 512:(wh + 1) * 512], PE[:, bank * 512:(bank + 1) * 512]),
                deps=[t_mm, xs_last], inc="v")
            pe_free[bank] = t_ev
            u_toks.append(t_ev)
            g += 1
    if phase_limit <= 2:
        return _finish(nc, P, taps, locals())

    t_wqT = prep_wT(wq_h, 0, [vT_last])
    qT_toks = proj(qT, lambda dc, oc: WT[0][:, dc, oc * 128:(oc + 1) * 128],
                   lambda dc, tb: xnT[:, dc, tb * 512:(tb + 1) * 512],
                   t_wqT, xnT_tok, extra_mm_deps=[u_toks[-1]])
    t_wa = load_nat(wa_h, 1, [u_toks[-1]])
    pT_toks = proj(pT, lambda qc, kb: WT[1][:, qc, kb * 128:(kb + 1) * 128],
                   lambda qc, tb: qT[:, qc, tb * 512:(tb + 1) * 512],
                   t_wa, qT_toks[-1])
    t_wkT = prep_wT(wk_h, 0, [qT_toks[-1]])
    kT_toks = proj(kT, lambda dc, oc: WT[0][:, dc, oc * 128:(oc + 1) * 128],
                   lambda dc, tb: xnT[:, dc, tb * 512:(tb + 1) * 512],
                   t_wkT, xnT_tok, extra_mm_deps=[pT_toks[-1]])
    if phase_limit <= 4:
        return _finish(nc, P, taps, locals())

    t_cn = Em("gpsimd", lambda g: g.dma_start(out=wstage[0][0:64, :], in_=cent_h),
              deps=[wst_free[0]], inc="dw", amt=16)
    t_ct = None
    for dc in range(DC):
        t_ct = Em("tensor", lambda t, dc=dc: t.transpose(
            PT[0][:, dc * 64:(dc + 1) * 64], wstage[0][0:64, dc * 128:(dc + 1) * 128],
            ident[0:64, 0:64]), deps=[t_cn, pt_free[0]], inc=("t" if dc == DC - 1 else None))
    t_centT = Em("vector", lambda v: v.tensor_copy(
        centT[:], PT[0][:, 0:512].rearrange("p (c e) -> p c e", c=DC)),
        deps=[t_ct], inc="v")
    pt_free[0] = t_centT
    wst_free[0] = t_ct
    t_bb = Em("tensor", lambda t: t.matmul(
        PS[:, 0:E], ones_sb[:], bias_sb[:], start=True, stop=True),
        deps=[t_ones, t_bias], inc="t")
    t_bbc = Em("vector", lambda v: v.tensor_copy(bias_bc[:], PS[:, 0:E]),
               deps=[t_bb], inc="v")
    t_io = Em("sync", lambda s: s.dma_start(out=iota_sb[:], in_=iota64_hbm), inc="dx", amt=16)
    t_if = Em("sync", lambda s: s.dma_start(out=identF[:], in_=identF_hbm), inc="dx", amt=16)
    t_be = Em("tensor", lambda t: t.matmul(
        PS[:, 512:512 + E], ones_sb[:], iota_sb[:], start=True, stop=True),
        deps=[t_ones, t_io], inc="t")
    t_erow = Em("vector", lambda v: v.tensor_copy(erow[:], PS[:, 512:512 + E]),
                deps=[t_be], inc="v")
    ps_free = [t_bbc, t_erow]
    prev_chain = None
    for tt in range(TT):
        pb = tt % 2
        t_mm = None
        for dc in range(DC):
            t_mm = Em("tensor", lambda t, tt=tt, dc=dc, pb=pb: t.matmul(
                PS[:, pb * 512:pb * 512 + E], xnT[:, dc, tt * 128:(tt + 1) * 128],
                centT[:, dc, :], start=(dc == 0), stop=(dc == DC - 1)),
                deps=[xnT_tile_toks[tt], t_centT, ps_free[pb]],
                inc=("t" if dc == DC - 1 else None))
        c1 = Em("vector", lambda v, pb=pb: v.tensor_copy(aff_sb[:], PS[:, pb * 512:pb * 512 + E]),
                deps=[t_mm, prev_chain], inc="v")
        ps_free[pb] = c1
        c2 = Em("vector", lambda v: v.tensor_add(biased[:], aff_sb[:], bias_bc[:]),
                deps=[c1, t_bbc], inc="v")
        c3 = Em("vector", lambda v: v.max(out=max8[:], in_=biased[:]), deps=[c2], inc="v")
        c4 = Em("vector", lambda v: v.match_replace(
            out=zap[:], in_to_replace=max8[:], in_values=biased[:], imm_value=NEG_BIG),
            deps=[c3], inc="v")
        c5 = Em("vector", lambda v: v.tensor_scalar(mask[:], zap[:], NEG_THR, None, ALU.is_le),
                deps=[c4], inc="v")
        c6 = Em("scalar", lambda sc: sc.activation(sig[:], aff_sb[:], AF.Sigmoid),
                deps=[c1, c5], inc="s")
        c7 = Em("vector", lambda v: v.tensor_mul(wm[:], sig[:], mask[:]),
                deps=[c6, c5], inc="v")
        c8 = Em("vector", lambda v: v.tensor_reduce(den1[:], wm[:], AX.X, ALU.add),
                deps=[c7], inc="v")
        c9 = Em("vector", lambda v: v.tensor_scalar(den1[:], den1[:], 1e-8, None, ALU.add),
                deps=[c8], inc="v")
        c10 = Em("vector", lambda v: v.reciprocal(rd1[:], den1[:]), deps=[c9], inc="v")
        c11 = Em("vector", lambda v, tt=tt: v.tensor_scalar(
            cw[:, tt * E:(tt + 1) * E], wm[:], rd1[:], None, ALU.mult),
            deps=[c10], inc="v")
        c12 = Em("vector", lambda v, tt=tt: v.tensor_copy(
            MKb[:, tt * E:(tt + 1) * E], mask[:]), deps=[c5], inc="v")
        pbm = tt % 2
        t_mt = Em("tensor", lambda t, tt=tt, pbm=pbm: t.transpose(
            PT[pbm][0:64, 0:128], MKb[:, tt * E:(tt + 1) * E], ident[:]),
            deps=[c12, pt_free[pbm]], inc="t")
        t_mte = Em("vector", lambda v, tt=tt, pbm=pbm: v.tensor_copy(
            MT[:, tt * 128:(tt + 1) * 128], PT[pbm][0:64, 0:128]),
            deps=[t_mt], inc="v")
        pt_free[pbm] = t_mte
        prev_chain = t_mte
    cw_done = prev_chain

    t_zr = Em("vector", lambda v: v.memset(ZROW[:], 0.0), deps=[cw_done], inc="v")
    cum_toks = []
    for h in range(NH):
        cum_toks.append(Em("vector", lambda v, h=h: v.tensor_tensor_scan(
            CUM[:, h * TH:(h + 1) * TH], MT[:, h * TH:(h + 1) * TH], ZROW[:],
            0.0, ALU.add, ALU.add), deps=[t_zr, cw_done], inc="v"))
    t_r1 = Em("vector", lambda v: v.memset(VALS[:], 1.0), deps=[cum_toks[-1]], inc="v")
    t_r2 = Em("vector", lambda v: v.tensor_tensor_scan(
        RAMP[:], VALS[:], ZROW[:], 0.0, ALU.add, ALU.add), deps=[t_r1], inc="v")
    cumsum_done = t_r2

    ext_state = {"h": -1, "r": 32, "src": 0, "prep": None, "last": cumsum_done}
    ext_bufs = None

    def emit_extract_some(n):
        for _ in range(n):
            st = ext_state
            if st["r"] >= CAPH // 8:
                if st["h"] >= NH - 1:
                    return
                st["h"] += 1
                h = st["h"]
                t_v1 = Em("vector", lambda v: v.tensor_scalar(
                    VALS[:], RAMP[:], -1.0, float(TH + 1), ALU.mult, ALU.add),
                    deps=[st["last"], cumsum_done], inc="v")
                t_v2 = Em("vector", lambda v, h=h: v.tensor_mul(
                    VALS[:], VALS[:], MT[:, h * TH:(h + 1) * TH]),
                    deps=[t_v1], inc="v")
                st["last"] = t_v2
                st["r"] = 0
                st["src"] = 0
            h, r = st["h"], st["r"]
            srcb = VALS if st["src"] == 0 else MT[:, h * TH:(h + 1) * TH]
            dstb = MT[:, h * TH:(h + 1) * TH] if st["src"] == 0 else VALS
            t_m = Em("vector", lambda v, srcb=srcb: v.max(out=EXV[:], in_=srcb),
                     deps=[st["last"]], inc="v")
            t_z = Em("vector", lambda v, srcb=srcb, dstb=dstb: v.match_replace(
                out=dstb, in_to_replace=EXV[:], in_values=srcb, imm_value=0.0),
                deps=[t_m], inc="v")
            t_i = Em("vector", lambda v, h=h, r=r: v.tensor_scalar(
                IDXL[:, h * CAPH + 8 * r:h * CAPH + 8 * r + 8], EXV[:],
                -1.0, float(TH + h * TH), ALU.mult, ALU.add),
                deps=[t_m], inc="v")
            t_i2 = Em("vector", lambda v, h=h, r=r: v.tensor_scalar_min(
                IDXL[:, h * CAPH + 8 * r:h * CAPH + 8 * r + 8],
                IDXL[:, h * CAPH + 8 * r:h * CAPH + 8 * r + 8], float(S - 1)),
                deps=[t_i], inc="v")
            st["last"] = t_z
            st["r"] += 1
            st["src"] ^= 1

    s_toks = []
    s_wr_toks = []
    sstg_free = [None, None]
    e_free = None
    ps_av_free = max(t for t in ps_free if t is not None)
    at_free = [pt_free[0], pt_free[1]]
    alpha_used = None
    nmp_free = None
    for it in range(TT):
        pm_toks = []
        for jb in range(4):
            t_mm = None
            for kb in range(DC):
                t_mm = Em("tensor", lambda t, it=it, jb=jb, kb=kb: t.matmul(
                    PE[:, jb * 512:(jb + 1) * 512],
                    pT[:, kb, it * 128:(it + 1) * 128],
                    kT[:, kb, jb * 512:(jb + 1) * 512],
                    start=(kb == 0), stop=(kb == DC - 1)),
                    deps=[pT_toks[-1], kT_toks[-1], e_free],
                    inc=("t" if kb == DC - 1 else None))
            pm_toks.append(Em("vector", lambda v, jb=jb: v.reduce_max(
                nmp[:, jb:jb + 1], PE[:, jb * 512:(jb + 1) * 512], AX.X),
                deps=[t_mm, nmp_free], inc="v"))
        t_nm = Em("vector", lambda v, it=it: v.tensor_reduce(
            negm16[:, it:it + 1], nmp[:], AX.X, ALU.max, negate=True),
            deps=pm_toks, inc="v")
        nmp_free = t_nm
        exp_toks = []
        for c in range(4):
            exp_toks.append(Em("scalar", lambda sc, it=it, c=c: sc.activation(
                alpha_sb[:, c * 512:(c + 1) * 512], PE[:, c * 512:(c + 1) * 512],
                AF.Exp, bias=negm16[:, it:it + 1]),
                deps=[t_nm, alpha_used if c == 0 else None], inc="s"))
        e_free = exp_toks[-1]
        at_evac = []
        for gb in range(2):
            t_tr = None
            for k in range(8):
                jb8 = gb * 8 + k
                t_tr = Em("tensor", lambda t, jb8=jb8, k=k, gb=gb: t.transpose(
                    PT[gb][:, k * 128:(k + 1) * 128],
                    alpha_sb[:, jb8 * 128:(jb8 + 1) * 128], ident[:]),
                    deps=[exp_toks[jb8 // 4], at_free[gb]], inc=("t" if k == 7 else None))
            t_ev = Em("vector", lambda v, gb=gb: v.tensor_copy(
                alphaT_sb[:, gb * 1024:(gb + 1) * 1024], PT[gb][:]),
                deps=[t_tr], inc="v")
            at_free[gb] = t_ev
            at_evac.append(t_ev)
        t_rs = Em("vector", lambda v, it=it: v.reduce_sum(
            rowsum16[:, it:it + 1], alpha_sb[:], AX.X), deps=[exp_toks[-1]], inc="v")
        t_rc = Em("vector", lambda v, it=it: v.reciprocal(
            recip16[:, it:it + 1], rowsum16[:, it:it + 1]), deps=[t_rs], inc="v")
        alpha_used = t_rs
        emit_extract_some(5)
        av_groups = []
        for wh in range(2):
            t_mm = None
            for j in range(16):
                t_mm = Em("tensor", lambda t, wh=wh, j=j: t.matmul(
                    PS[:, wh * 512:(wh + 1) * 512],
                    alphaT_sb[:, j * 128:(j + 1) * 128],
                    u[:, j, wh * 512:(wh + 1) * 512],
                    start=(j == 0), stop=(j == 15)),
                    deps=[at_evac[j // 8], u_toks[-1], ps_av_free],
                    inc=("t" if j == 15 else None))
            av_groups.append(t_mm)
        t_s = Em("vector", lambda v, it=it: v.tensor_scalar(
            sstg[it % 2][:], PS[:, 0:1024], recip16[:, it:it + 1], None, ALU.mult),
            deps=av_groups + [t_rc, sstg_free[it % 2]], inc="v")
        ps_av_free = t_s
        t_sw = Em("sync", lambda s, it=it: s.dma_start(
            out=sdr_h[it * 128:(it + 1) * 128, :], in_=sstg[it % 2][:]),
            deps=[t_s], inc="dy", amt=16)
        sstg_free[it % 2] = t_sw
        s_wr_toks.append(t_sw)
        s_toks.append(t_s)
    if phase_limit <= 5:
        return _finish(nc, P, taps, locals())

    emit_extract_some(200)
    ixr = ext_state["last"]
    ixt_toks = []
    ps6_free = [s_toks[-1], s_toks[-1]]
    for c in range(4):
        pb6 = c % 2
        t_tp = Em("tensor", lambda t, c=c, pb6=pb6: t.transpose(
            PS[0:128, pb6 * 512:pb6 * 512 + 64], IDXL[:, c * 128:(c + 1) * 128],
            identF[0:64, 0:64]), deps=[ixr, ps6_free[pb6], t_if], inc="t")
        t_ix = Em("vector", lambda v, c=c, pb6=pb6: v.tensor_copy(
            IXT[:, c * E:(c + 1) * E], PS[:, pb6 * 512:pb6 * 512 + 64]),
            deps=[t_tp], inc="v")
        ps6_free[pb6] = t_ix
        ixt_toks.append(t_ix)

    def acc_ap(g):
        return acc[:, g, :] if g < TTH else accG[:, g - TTH, :]

    t_wsu = Em("gpsimd", lambda g: g.dma_start(
        out=Wu_b[0][:], in_=wsup_h.rearrange("(c p) f -> p c f", p=128)),
        deps=[s_toks[-1]], inc="dw", amt=16)
    t_wsd = Em("gpsimd", lambda g: g.dma_start(
        out=Wd_b[0][:], in_=wsdn_h.rearrange("(c p) f -> p c f", p=128)),
        deps=[s_toks[-1]], inc="dw", amt=16)
    up_bank_free = [None] * 3
    dn_bank_free = [ps6_free[0], ps6_free[1]]
    ubr = [0]
    hsh_free = [alpha_used]
    shup_last = None
    sh_acc_toks = [None] * TT
    for h in range(NH):
        sil_toks = []
        for tb in range(2):
            for fc in range(FC):
                bank = ubr[0] % 3
                ubr[0] += 1
                t_mm = None
                for dc in range(DC):
                    t_mm = Em("tensor", lambda t, h=h, tb=tb, fc=fc, dc=dc, bank=bank: t.matmul(
                        PE[:, bank * 512:(bank + 1) * 512],
                        Wu_b[0][:, dc, fc * 128:(fc + 1) * 128],
                        xnT[:, dc, h * TH + tb * 512:h * TH + (tb + 1) * 512],
                        start=(dc == 0), stop=(dc == DC - 1)),
                        deps=[t_wsu, xnT_tok, up_bank_free[bank]],
                        inc=("t" if dc == DC - 1 else None))
                shup_last = t_mm
                t_h = Em("scalar", lambda sc, tb=tb, fc=fc, bank=bank: sc.activation(
                    Hsh[:, fc, tb * 512:(tb + 1) * 512],
                    PE[:, bank * 512:(bank + 1) * 512], AF.Silu),
                    deps=[t_mm] + hsh_free, inc="s")
                up_bank_free[bank] = t_h
                sil_toks.append(t_h)
        last_dn = None
        for tl in range(TTH):
            for dh in range(2):
                bank = (tl * 2 + dh) % 2
                t_mm = None
                for fc in range(FC):
                    t_mm = Em("tensor", lambda t, tl=tl, dh=dh, fc=fc, bank=bank: t.matmul(
                        PS[:, bank * 512:(bank + 1) * 512],
                        Hsh[:, fc, tl * 128:(tl + 1) * 128],
                        Wd_b[0][:, fc, dh * 512:(dh + 1) * 512],
                        start=(fc == 0), stop=(fc == FC - 1)),
                        deps=[t_wsd, dn_bank_free[bank]] + sil_toks,
                        inc=("t" if fc == FC - 1 else None))
                last_dn = t_mm
                t_ev = Em("vector", lambda v, h=h, tl=tl, dh=dh, bank=bank: v.tensor_copy(
                    acc_ap(h * TTH + tl)[:, dh * 512:(dh + 1) * 512],
                    PS[:, bank * 512:(bank + 1) * 512]),
                    deps=[t_mm, s_toks[-1]], inc="v")
                dn_bank_free[bank] = t_ev
                sh_acc_toks[h * TTH + tl] = t_ev
        hsh_free = [last_dn]
    shdn_last = hsh_free[0]

    pe3_free = [s_toks[-1]]
    flu_toks = [None] * TT
    wj_toks = [None] * TT
    p2_next = [0]

    def emit_pass2_tile():
        tt = p2_next[0]
        if tt >= TT:
            return
        p2_next[0] += 1
        sec = float((CAPH if tt >= TTH else 0) - 1)
        t_tp = Em("tensor", lambda t, tt=tt: t.transpose(
            PE[0:128, 1536:1536 + 64], CUM[:, tt * 128:(tt + 1) * 128],
            identF[0:64, 0:64]), deps=[cumsum_done, pe3_free[0], t_if], inc="t")
        t_pe = Em("vector", lambda v: v.tensor_copy(
            aff_sb[:], PE[:, 1536:1536 + 64]), deps=[t_tp], inc="v")
        pe3_free[0] = t_pe
        s1 = Em("vector", lambda v: v.scalar_tensor_tensor(
            zap[:], erow[:], float(CAP), aff_sb[:], ALU.mult, ALU.add),
            deps=[t_pe, t_erow], inc="v")
        s2 = Em("vector", lambda v, sec=sec: v.tensor_scalar(
            zap[:], zap[:], sec, None, ALU.add), deps=[s1], inc="v")
        s3 = Em("vector", lambda v: v.tensor_scalar(
            mask[:], zap[:], -1.0, BIGK, ALU.mult, ALU.add), deps=[s2], inc="v")
        s4 = Em("vector", lambda v, tt=tt: v.tensor_mul(
            mask[:], mask[:], MKb[:, tt * E:(tt + 1) * E]), deps=[s3], inc="v")
        s5 = Em("vector", lambda v: v.max(out=max8[:], in_=mask[:]), deps=[s4], inc="v")
        s6 = Em("vector", lambda v: v.tensor_scalar(
            sig[:, 0:8], max8[:], -1.0, BIGK, ALU.mult, ALU.add), deps=[s5], inc="v")
        flu_toks[tt] = Em("vector", lambda v, tt=tt: v.tensor_copy(
            FLU[:, tt * 8:(tt + 1) * 8], sig[:, 0:8]), deps=[s6], inc="v")
        prev = flu_toks[tt]
        for j in range(8):
            o1 = Em("vector", lambda v, j=j: v.tensor_scalar(
                wm[:], mask[:], max8[:, j:j + 1], None, ALU.is_equal),
                deps=[prev], inc="v")
            o2 = Em("vector", lambda v, tt=tt: v.tensor_mul(
                wm[:], wm[:], cw[:, tt * E:(tt + 1) * E]), deps=[o1], inc="v")
            prev = Em("vector", lambda v, tt=tt, j=j: v.reduce_sum(
                WJ[:, tt * 8 + j:tt * 8 + j + 1], wm[:], AX.X), deps=[o2], inc="v")
        wj_toks[tt] = prev

    pt6_free = [pt_free[0], pt_free[1]]
    xg_free = [[shup_last], [shup_last]]
    xgt_free = [[shup_last], [shup_last]]
    hg_free = [shdn_last]
    ygst_free = [None, None]
    wu_free = [shup_last, s_toks[-1]]
    wd_free = [shdn_last, s_toks[-1]]
    yg_last = None
    def expert_dma(e):
        par = e % 2
        t_wu = Em("gpsimd", lambda g, e=e, par=par: g.dma_start(
            out=Wu_b[par][:], in_=wup_h[e].rearrange("(c p) f -> p c f", p=128)),
            deps=[wu_free[par]], inc="dw", amt=16)
        t_wd = Em("gpsimd", lambda g, e=e, par=par: g.dma_start(
            out=Wd_b[par][:], in_=wdn_h[e].rearrange("(c p) f -> p c f", p=128)),
            deps=[wd_free[par]], inc="dw", amt=16)
        gath = []
        for c in range(4):
            gath.append(Em("gpsimd", lambda g, e=e, c=c, par=par: g.indirect_dma_start(
                out=XG[par][:, c, :], out_offset=None, in_=xnd_h,
                in_offset=IndirectOffsetOnAxis(IXT[:, c * E + e:c * E + e + 1], 0)),
                deps=[ixt_toks[c], xnd_toks[-1]] + xg_free[par], inc="dw", amt=16))
        xg_free[par] = []
        return t_wu, t_wd, gath

    def expert_transpose(e, gath):
        par = e % 2
        xgt_wr = []
        for c in range(4):
            pb6 = c % 2
            t_tr = None
            for dj in range(DC):
                t_tr = Em("tensor", lambda t, c=c, dj=dj, par=par, pb6=pb6: t.transpose(
                    PT[pb6][:, dj * 128:(dj + 1) * 128],
                    XG[par][:, c, dj * 128:(dj + 1) * 128], ident[:]),
                    deps=[gath[c], pt6_free[pb6]] + xgt_free[par],
                    inc=("t" if dj == DC - 1 else None))
            xg_free[par].append(t_tr)
            t_xe = Em("vector", lambda v, c=c, par=par, pb6=pb6: v.tensor_copy(
                XGT[par][:, :, c * 128:(c + 1) * 128],
                PT[pb6].rearrange("p (dj q) -> p dj q", dj=DC)),
                deps=[t_tr], inc="v")
            pt6_free[pb6] = t_xe
            xgt_wr.append(t_xe)
        xgt_free[par] = []
        return xgt_wr

    dmas = {0: expert_dma(0)}
    xgt_ready = {0: expert_transpose(0, dmas[0][2])}
    for e in range(E):
        par = e % 2
        t_wu, t_wd, _g = dmas.pop(e)
        xgt_wr = xgt_ready.pop(e)
        if e + 1 < E:
            dmas[e + 1] = expert_dma(e + 1)
        sil_toks = []
        for fb in range(FC):
            bank = ubr[0] % 3
            ubr[0] += 1
            t_mm = None
            for dc in range(DC):
                t_mm = Em("tensor", lambda t, fb=fb, dc=dc, par=par, bank=bank: t.matmul(
                    PE[:, bank * 512:(bank + 1) * 512],
                    Wu_b[par][:, dc, fb * 128:(fb + 1) * 128],
                    XGT[par][:, dc, :],
                    start=(dc == 0), stop=(dc == DC - 1)),
                    deps=[t_wu, up_bank_free[bank]] + xgt_wr,
                    inc=("t" if dc == DC - 1 else None))
            xgt_free[par].append(t_mm)
            t_h = Em("scalar", lambda sc, fb=fb, bank=bank: sc.activation(
                Hg[:, fb, :], PE[:, bank * 512:(bank + 1) * 512], AF.Silu),
                deps=[t_mm] + hg_free, inc="s")
            up_bank_free[bank] = t_h
            sil_toks.append(t_h)
        wu_free[par] = t_mm
        if e + 1 < E:
            xgt_ready[e + 1] = expert_transpose(e + 1, dmas[e + 1][2])
        last_dn = None
        for sb in range(4):
            for dh in range(2):
                bank = (sb * 2 + dh) % 2
                t_mm = None
                for fb in range(FC):
                    t_mm = Em("tensor", lambda t, sb=sb, dh=dh, fb=fb, par=par, bank=bank: t.matmul(
                        PS[:, bank * 512:(bank + 1) * 512],
                        Hg[:, fb, sb * 128:(sb + 1) * 128],
                        Wd_b[par][:, fb, dh * 512:(dh + 1) * 512],
                        start=(fb == 0), stop=(fb == FC - 1)),
                        deps=[t_wd, dn_bank_free[bank]] + sil_toks,
                        inc=("t" if fb == FC - 1 else None))
                last_dn = t_mm
                buf = (sb // 2) % 2
                t_ev = Em("vector", lambda v, sb=sb, dh=dh, bank=bank, buf=buf: v.tensor_copy(
                    Ygst[buf][:, sb % 2, dh * 512:(dh + 1) * 512],
                    PS[:, bank * 512:(bank + 1) * 512]),
                    deps=[t_mm, ygst_free[buf]], inc="v")
                dn_bank_free[bank] = t_ev
                if sb % 2 == 1 and dh == 1:
                    pair = sb // 2
                    yg_last = Em("sync", lambda s, e=e, pair=pair, buf=buf: s.dma_start(
                        out=ygd_h[e * CAP + pair * 256:e * CAP + (pair + 1) * 256, :]
                            .rearrange("(c p) d -> p c d", p=128),
                        in_=Ygst[buf][:]),
                        deps=[dn_bank_free[0], dn_bank_free[1]], inc="dy", amt=16)
                    ygst_free[buf] = yg_last
        wd_free[par] = last_dn
        hg_free = [last_dn]
        if e % 4 == 0:
            emit_pass2_tile()
    while p2_next[0] < TT:
        emit_pass2_tile()

    if phase_limit <= 6:
        return _finish(nc, P, taps, locals())
    xstf = [at(f"xstf{i}", i * 4096, [128, D], F32) for i in range(2)]
    gacc_free = [[yg_last], [yg_last], [yg_last], [yg_last]]
    xs6_free = [list(xg_free[0]), list(xg_free[0])]
    os_free = [[yg_last], [yg_last], [yg_last]]
    gsil_free = [hg_free, hg_free]
    out_dma_toks = []
    for tt in range(TT):
        prev_stt = sh_acc_toks[tt]
        for j in range(8):
            gb6 = (tt * 8 + j) % 4
            t_ga = Em("gpsimd", lambda g, tt=tt, j=j, gb6=gb6: g.indirect_dma_start(
                out=gacc[gb6][:], out_offset=None, in_=ygd_h,
                in_offset=IndirectOffsetOnAxis(FLU[:, tt * 8 + j:tt * 8 + j + 1], 0)),
                deps=[flu_toks[tt], yg_last] + gacc_free[gb6], inc="dw", amt=16)
            prev_stt = Em("vector", lambda v, tt=tt, j=j, gb6=gb6: v.scalar_tensor_tensor(
                acc_ap(tt)[:], gacc[gb6][:], WJ[:, tt * 8 + j:tt * 8 + j + 1],
                acc_ap(tt)[:], ALU.mult, ALU.add),
                deps=[t_ga, wj_toks[tt], prev_stt], inc="v")
            gacc_free[gb6] = [prev_stt]
        if phase_limit <= 7:
            continue
        t_g = Em("scalar", lambda sc, tt=tt: sc.activation(
            gsil[tt % 2][:], acc_ap(tt)[:], AF.Silu),
            deps=[prev_stt] + gsil_free[tt % 2], inc="s")
        b = tt % 2
        t_dx = Em("sync", lambda s, tt=tt, b=b: s.dma_start(
            out=xstf[b][:], in_=x_h[tt * 128:(tt + 1) * 128, :]),
            deps=xs6_free[b] if isinstance(xs6_free[b], list) else [xs6_free[b]],
            inc="dx", amt=16)
        t_sl = Em("sync", lambda s, tt=tt: s.dma_start(
            out=sstg[tt % 2][:], in_=sdr_h[tt * 128:(tt + 1) * 128, :]),
            deps=[s_wr_toks[tt]] + sstg_free[tt % 2] if isinstance(sstg_free[tt % 2], list)
                 else [s_wr_toks[tt], sstg_free[tt % 2]],
            inc="dy", amt=16)
        ob = tt % 3
        t_m1 = Em("vector", lambda v, tt=tt, ob=ob: v.tensor_mul(
            ostage[ob][:], sstg[tt % 2][:], gsil[tt % 2][:]),
            deps=[t_g, t_sl] + os_free[ob], inc="v")
        sstg_free[tt % 2] = [t_m1]
        gsil_free[tt % 2] = [t_m1]
        t_m2 = Em("vector", lambda v, b=b, ob=ob: v.tensor_add(
            ostage[ob][:], ostage[ob][:], xstf[b][:]),
            deps=[t_m1, t_dx], inc="v")
        xs6_free[b] = [t_m2]
        t_o = Em("sync", lambda s, tt=tt, ob=ob: s.dma_start(
            out=y_h[tt * 128:(tt + 1) * 128, :], in_=ostage[ob][:]),
            deps=[t_m2], inc="dx", amt=16)
        os_free[ob] = [t_o]
        out_dma_toks.append(t_o)

    if phase_limit <= 7:
        Em("sync", lambda s: None, deps=P.final_deps())
    else:
        Em("sync", lambda s: None, deps=[out_dma_toks[-1], yg_last])
    return _finish(nc, P, taps, locals())


def _finish(nc, P, taps, env):
    tap_outs = []
    for name, key in taps:
        t = env[key]
        shape = list(t.shape)
        dt = t.tensor.dtype if hasattr(t, "tensor") else t.dtype
        o = nc.dram_tensor(name, shape, dt, kind="ExternalOutput").ap()
        P.emit("sync", lambda s, o=o, t=t: s.dma_start(out=o, in_=t[:]),
               deps=P.final_deps(), inc="dx", amt=16)
        tap_outs.append(name)
    if tap_outs:
        P.emit("sync", lambda s: None, deps=P.final_deps())
    with ExitStack() as ctx:
        sems = {n: ctx.enter_context(nc.semaphore(n)) for n in P.sem_names}
        block = ctx.enter_context(nc.Block())
        P.build(block, sems)
    return nc


_NC_CACHE = {}


def _get_nc(phase_limit=99, taps=()):
    key = (phase_limit, tuple(taps))
    if key not in _NC_CACHE:
        _NC_CACHE[key] = build_nc(phase_limit, taps)
    return _NC_CACHE[key]


def kernel(L0_attn_x, L0_attn_Wk, L0_attn_Wq, L0_attn_Wv, L0_attn_Wa, L0_attn_Wb,
           L0_ffn_centroids, L0_ffn_expert_biases, L0_ffn_Ws_up, L0_ffn_Ws_down,
           L0_ffn_W_up, L0_ffn_W_down, _phase_limit=99, _taps=()):
    nc = _get_nc(_phase_limit, _taps)
    common = dict(
        wk=np.ascontiguousarray(L0_attn_Wk), wq=np.ascontiguousarray(L0_attn_Wq),
        wv=np.ascontiguousarray(L0_attn_Wv), wa=np.ascontiguousarray(L0_attn_Wa),
        wb=np.ascontiguousarray(L0_attn_Wb),
        cent=np.ascontiguousarray(L0_ffn_centroids),
        ebias=np.ascontiguousarray(np.reshape(L0_ffn_expert_biases, (1, E))),
        wsup=np.ascontiguousarray(L0_ffn_Ws_up), wsdn=np.ascontiguousarray(L0_ffn_Ws_down),
        wup=np.ascontiguousarray(L0_ffn_W_up), wdn=np.ascontiguousarray(L0_ffn_W_down),
    )
    x = np.ascontiguousarray(L0_attn_x)
    in_maps = [dict(common, x=np.ascontiguousarray(x[c])) for c in range(8)]
    res = run_bass_kernel_spmd(nc, in_maps, core_ids=list(range(8)))
    if _taps or _phase_limit < 99:
        return res
    return np.stack([r["y"] for r in res.results], axis=0)


# version 16
# speedup vs baseline: 1.8144x; 1.0097x over previous
import sys
if "/opt/trn_rl_repo" not in sys.path:
    sys.path.insert(0, "/opt/trn_rl_repo")
from collections import defaultdict
from contextlib import ExitStack

import numpy as np
import ml_dtypes
import concourse.bass as bass
import concourse.mybir as mybir
from concourse.bass_utils import run_bass_kernel_spmd
from concourse.bass_core_rust import IndirectOffsetOnAxis

F32 = mybir.dt.float32
BF16 = mybir.dt.bfloat16
AF = mybir.ActivationFunctionType
ALU = mybir.AluOpType
AX = mybir.AxisListType

ENGINES = ("sync", "gpsimd", "tensor", "vector", "scalar")

S, D, E, FF = 2048, 1024, 64, 512
TT = S // 128
DC = D // 128
FC = FF // 128
NH = 2
TH = S // NH
TTH = TH // 128
NEG_BIG = -1e30
NEG_THR = -1e29
CAP = 512
CAPH = CAP // 2
BIGK = 65536.0


class Prog:
    def __init__(self):
        self.ops = {e: [] for e in ENGINES}
        self.sem_names = []
        self.counts = defaultdict(int)

    def sem(self, name):
        if name not in self.sem_names:
            self.sem_names.append(name)
        return name

    PARANOID = False

    def __init_rr__(self):
        pass

    def emit(self, engine, fn, deps=(), inc=None, amt=1):
        tok = None
        deps = [d for d in deps if d is not None]
        if Prog.PARANOID:
            deps = [(s, v) for s, v in self.counts.items()]
        if inc == "dw":
            rr = getattr(self, "_dw_rr", 0)
            self._dw_rr = rr + 1
            inc = f"dw{rr % 4}"
        if inc is not None:
            self.sem(inc)
            if amt == 16 and self.counts[inc] > 0:
                deps.append((inc, self.counts[inc]))
            self.counts[inc] += amt
            tok = (inc, self.counts[inc])
        self.ops[engine].append((fn, deps, inc, amt))
        return tok

    def final_deps(self):
        return [(s, self.counts[s]) for s in self.sem_names]

    def build(self, block, sems):
        for e in ENGINES:
            if not self.ops[e]:
                continue

            def make(e):
                def f(eng):
                    hw = defaultdict(int)
                    for fn, deps, inc, amt in self.ops[e]:
                        for (s, v) in deps:
                            if v > hw[s]:
                                eng.wait_ge(sems[s], v)
                                hw[s] = v
                        inst = fn(eng)
                        if inc is not None:
                            if inst is None:
                                inst = eng.nop()
                            inst.then_inc(sems[inc], amt)
                return f

            getattr(block, e)(make(e))


def build_nc(phase_limit=99, taps=()):
    nc = bass.Bass("TRN2", target_bir_lowering=False)
    x_h = nc.dram_tensor("x", [S, D], F32, kind="ExternalInput").ap()
    wk_h = nc.dram_tensor("wk", [D, D], F32, kind="ExternalInput").ap()
    wq_h = nc.dram_tensor("wq", [D, D], F32, kind="ExternalInput").ap()
    wv_h = nc.dram_tensor("wv", [D, D], F32, kind="ExternalInput").ap()
    wa_h = nc.dram_tensor("wa", [D, D], F32, kind="ExternalInput").ap()
    wb_h = nc.dram_tensor("wb", [D, D], F32, kind="ExternalInput").ap()
    cent_h = nc.dram_tensor("cent", [E, D], F32, kind="ExternalInput").ap()
    ebias_h = nc.dram_tensor("ebias", [1, E], F32, kind="ExternalInput").ap()
    wsup_h = nc.dram_tensor("wsup", [D, FF], F32, kind="ExternalInput").ap()
    wsdn_h = nc.dram_tensor("wsdn", [FF, D], F32, kind="ExternalInput").ap()
    wup_h = nc.dram_tensor("wup", [E, D, FF], F32, kind="ExternalInput").ap()
    wdn_h = nc.dram_tensor("wdn", [E, FF, D], F32, kind="ExternalInput").ap()
    y_h = nc.dram_tensor("y", [S, D], F32, kind="ExternalOutput").ap()
    xnd_h = nc.dram_tensor("xnd", [S, D], BF16, kind="Internal").ap()
    sdr_h = nc.dram_tensor("sdr", [S, D], BF16, kind="Internal").ap()
    ygd_h = nc.dram_tensor("ygd", [E * CAP, D], BF16, kind="Internal").ap()

    ident_hbm = nc.inline_tensor(np.eye(128, dtype=ml_dtypes.bfloat16), name="identc").ap()
    ones_hbm = nc.inline_tensor(np.ones((1, 128), dtype=np.float32), name="onesc").ap()
    iota64_hbm = nc.inline_tensor(np.arange(E, dtype=np.float32).reshape(1, E), name="iotac").ap()
    identF_hbm = nc.inline_tensor(np.eye(128, dtype=np.float32), name="identfc").ap()

    ARENA_BYTES = 208896
    arena = nc.alloc_sbuf_tensor("ARENA", [128, ARENA_BYTES], mybir.dt.uint8)
    SBUF_BASE = nc.lookup_mloc(arena).addr

    def at(name, off, shape, dt):
        assert 0 <= off and off < ARENA_BYTES
        return nc.alloc_sbuf_tensor_at(name, shape, dt, offset=SBUF_BASE + off).ap()

    xnT = at("xnT", 0, [128, DC, S], BF16)
    cw = at("cw", 32768, [128, TT * E], F32)
    co = 36864
    ident = at("ident", co + 0, [128, 128], BF16); co += 256
    centT = at("centT", co, [128, DC, E], BF16); co += 1024
    bias_sb = at("bias_sb", co, [1, E], F32); co += 256
    bias_bc = at("bias_bc", co, [128, E], F32); co += 256
    ones_sb = at("ones_sb", co, [1, 128], F32); co += 512
    aff_sb = at("aff_sb", co, [128, E], F32); co += 256
    biased = at("biased", co, [128, E], F32); co += 256
    max8 = at("max8", co, [128, 8], F32); co += 32
    zap = at("zap", co, [128, E], F32); co += 256
    mask = at("mask", co, [128, E], F32); co += 256
    sig = at("sig", co, [128, E], F32); co += 256
    wm = at("wm", co, [128, E], F32); co += 256
    st6 = at("st6", co, [128, 12], F32); co += 64
    mv2 = at("mv2", co, [128, 2 * TT], F32); co += 128
    nmp = at("nmp", co, [128, 4], F32); co += 32
    epsv = at("epsv", co, [128, 1], F32); co += 32
    rcp16 = at("rcp16", co, [128, TT], F32); co += 64
    rstd16 = at("rstd16", co, [128, TT], F32); co += 64
    negm16 = at("negm16", co, [128, TT], F32); co += 64
    rowsum16 = at("rowsum16", co, [128, TT], F32); co += 64
    recip16 = at("recip16", co, [128, TT], F32); co += 64
    den1 = at("den1", co, [128, 1], F32); co += 64
    rd1 = at("rd1", co, [128, 1], F32); co += 64
    iota_sb = at("iota_sb", co, [1, E], F32); co += 256
    erow = at("erow", co, [128, E], F32); co += 256
    identF = at("identF", co, [128, 128], F32); co += 512
    assert co <= 45056, co
    MKb = at("MKb", 196608, [128, TT * E], BF16)
    FLU = at("FLU", 198656, [128, TT * 8], mybir.dt.uint32)
    WJ = at("WJ", 199168, [128, TT * 8], F32)
    IXT = at("IXT", 199680, [128, 4 * E], mybir.dt.uint32)
    MT = at("MT", 110592, [E, S], F32)
    CUM = at("CUM", 110592 + 8192, [E, S], F32)
    VALS = at("VALS", 110592 + 16384, [E, TH], F32)
    IDXL = at("IDXL", 110592 + 20480, [E, CAP], F32)
    RAMP = at("RAMP", 110592 + 22528, [E, TH], F32)
    EXV = at("EXV", 110592 + 26624, [E, 8], F32)
    ZROW = at("ZROW", 110592 + 26656, [E, TH], F32)
    PTM = at("PTM", 176128, [128, TT * E], F32)
    XG = [at(f"XG{i}", 0 + i * 8192, [128, 4, D], BF16) for i in range(2)]
    XGT = [at(f"XGT{i}", 16384 + i * 8192, [128, DC, CAP], BF16) for i in range(2)]
    Hg = at("Hg", 176128, [128, FC, CAP], BF16)
    gacc = [at(f"gacc{i}", 176128 + i * 2048, [128, D], BF16) for i in range(4)]
    gsil = [at(f"gsil{i}", 204800 + i * 2048, [128, D], BF16) for i in range(2)]
    Ygst = [at(f"Ygst{i}", 184320 + i * 4096, [128, 2, D], BF16) for i in range(2)]
    accG = at("accG", 143360, [128, TTH, D], F32)
    Hsh = at("Hsh", 176128, [128, FC, TH], BF16)
    u = at("u", 45056, [128, TT, D], BF16)
    Wu_b = [at(f"Wu{i}", 45056 + i * 8192, [128, DC, FF], BF16) for i in range(2)]
    Wd_b = [at(f"Wd{i}", 45056 + 16384 + i * 8192, [128, FC, D], BF16) for i in range(2)]
    vT = at("vT", 77824, [128, DC, S], BF16)
    qT = at("qT", 77824, [128, DC, S], BF16)
    kT = at("kT", 77824, [128, DC, S], BF16)
    acc = at("acc", 77824, [128, TTH, D], F32)
    WT = [at(f"WT{i}", 110592 + i * 16384, [128, DC, D], BF16) for i in range(2)]
    sstg = [at(f"sstg{i}", 200704 + i * 2048, [128, D], BF16) for i in range(2)]
    pT = at("pT", 143360, [128, DC, S], BF16)
    wstage = [at(f"wstage{i}", 192512 + i * 2048, [128, D], BF16) for i in range(2)]
    alpha_sb = at("alpha_sb", 176128, [128, S], BF16)
    alphaT_sb = at("alphaT_sb", 176128 + 4096, [128, S], BF16)
    xn_t2 = [at(f"xn_t{i}", 184320 + i * 2048, [128, D], BF16) for i in range(2)]
    sqf = at("sqf", 184320 + 4096, [128, D], F32)
    ostage = [at(f"ostage{i}", 184320 + i * 4096, [128, D], F32) for i in range(3)]

    PE = nc.alloc_psum_tensor("PE", [128, 2048], F32).ap()
    PS = nc.alloc_psum_tensor("PS", [128, 1024], F32).ap()
    PT = [nc.alloc_psum_tensor(f"PT{i}", [128, 1024], BF16).ap() for i in range(2)]

    P = Prog()
    Em = P.emit

    LN_ONLY = (phase_limit == -1)
    if not LN_ONLY:
        t_id = Em("sync", lambda s: s.dma_start(out=ident[:], in_=ident_hbm), inc="dx", amt=16)
        t_ones = Em("sync", lambda s: s.dma_start(out=ones_sb[:], in_=ones_hbm), inc="dx", amt=16)
        t_bias = Em("sync", lambda s: s.dma_start(out=bias_sb[:], in_=ebias_h), inc="dx", amt=16)
    else:
        t_id = t_ones = t_bias = None

    t_eps = Em("vector", lambda v: v.memset(epsv[:], 1e-5), inc="v")
    xs_free = [None, None]
    pt_free = [None, None]
    xn_used = [[], []]
    xnd_toks = []
    wst_free = [None, None]
    pe_free = [None] * 4
    sq_done = None
    xnT_tok = None
    xnT_tile_toks = [None] * TT
    wv_tok = [None] * DC
    vT_ev = [[None] * 4 for _ in range(DC)]
    vT_emitted = set()
    vg = [0]

    def emit_vT_group(oc, tb):
        bank = vg[0] % 4
        vg[0] += 1
        t_mm = None
        for dc in range(DC):
            t_mm = Em("tensor", lambda t, oc=oc, tb=tb, dc=dc, bank=bank: t.matmul(
                PE[:, bank * 512:(bank + 1) * 512],
                WT[0][:, dc, oc * 128:(oc + 1) * 128],
                xnT[:, dc, tb * 512:(tb + 1) * 512],
                start=(dc == 0), stop=(dc == DC - 1)),
                deps=[wv_tok[oc], xnT_tile_toks[4 * tb + 3], pe_free[bank]],
                inc=("t" if dc == DC - 1 else None))
        t_ev = Em("scalar", lambda sc, oc=oc, tb=tb, bank=bank: sc.copy(
            vT[:, oc, tb * 512:(tb + 1) * 512], PE[:, bank * 512:(bank + 1) * 512]),
            deps=[t_mm], inc="s")
        pe_free[bank] = t_ev
        vT_ev[oc][tb] = t_ev
        vT_emitted.add((oc, tb))

    def emit_ready_vT(tt):
        for tb in range(4):
            if 4 * tb + 3 > tt:
                break
            for oc in range(DC):
                if 2 * oc + 1 <= tt and (oc, tb) not in vT_emitted:
                    emit_vT_group(oc, tb)

    xp = [at(f"xp{i}", 45056 + i * 8192, [128, 2, D], F32) for i in range(2)]
    e4s = [None] * TT
    ags = [None] * TT
    xsrcs = [None] * TT

    def p0_apply(tt):
        nonlocal xnT_tok
        b = (tt // 2) % 2
        xn_t = xn_t2[tt % 2]
        xsrc = xsrcs[tt]
        t_f = Em("vector", lambda v, tt=tt, xsrc=xsrc, xn_t=xn_t: v.tensor_scalar(
            xn_t[:], xsrc[:], mv2[:, 2 * tt:2 * tt + 1], rstd16[:, tt:tt + 1],
            ALU.subtract, ALU.mult), deps=[e4s[tt], ags[tt]] + list(xn_used[tt % 2]), inc="v")
        xs_free[b] = t_f
        t_tr = None
        for dc in range(DC):
            t_tr = Em("tensor", lambda t, dc=dc, xn_t=xn_t: t.transpose(
                PT[0][:, dc * 128:(dc + 1) * 128], xn_t[:, dc * 128:(dc + 1) * 128],
                ident[:]), deps=[t_f, t_id, pt_free[0]],
                inc=("t" if dc == DC - 1 else None))
        t_xnd = Em("sync", lambda s, tt=tt, xn_t=xn_t: s.dma_start(
            out=xnd_h[tt * 128:(tt + 1) * 128, :], in_=xn_t[:]),
            deps=[t_f], inc="dn", amt=16)
        xnd_toks.append(t_xnd)
        xn_used[tt % 2] = [t_tr, t_xnd]
        for dc in range(DC):
            xnT_tok = Em("scalar", lambda sc, tt=tt, dc=dc: sc.copy(
                xnT[:, dc, tt * 128:(tt + 1) * 128],
                PT[0][:, dc * 128:(dc + 1) * 128]), deps=[t_tr], inc="s")
        pt_free[0] = xnT_tok
        xnT_tile_toks[tt] = xnT_tok
        if tt % 2 == 1:
            rc = (tt - 1) // 2
            wb_ = rc % 2
            t_w = Em("gpsimd", lambda g, rc=rc, wb_=wb_: g.dma_start(
                out=wstage[wb_][:], in_=wv_h[rc * 128:(rc + 1) * 128, :]),
                deps=[wst_free[wb_]], inc="dw", amt=16)
            t_wtr = None
            for dc in range(DC):
                t_wtr = Em("tensor", lambda t, dc=dc, wb_=wb_: t.transpose(
                    PT[1][:, dc * 128:(dc + 1) * 128],
                    wstage[wb_][:, dc * 128:(dc + 1) * 128], ident[:]),
                    deps=[t_w, t_id, pt_free[1]], inc=("t" if dc == DC - 1 else None))
            wst_free[wb_] = t_wtr
            wv_tok[rc] = Em("scalar", lambda sc, rc=rc: sc.copy(
                WT[0][:, :, rc * 128:(rc + 1) * 128],
                PT[1].rearrange("p (c q) -> p c q", c=DC)), deps=[t_wtr], inc="s")
            pt_free[1] = wv_tok[rc]
        emit_ready_vT(tt)

    t_dx = None
    for tt in range(TT + 1):
        if tt < TT:
            b = (tt // 2) % 2
            if tt % 2 == 0:
                q = tt // 2
                t_dx = Em("sync", lambda s, q=q, b=b: s.dma_start(
                    out=xp[b][:], in_=x_h[q * 256:(q + 1) * 256, :].rearrange(
                        "(c p) d -> p c d", p=128)),
                    deps=[xs_free[b]], inc="dx", amt=16)
            xsrcs[tt] = xp[b][:, tt % 2, :]
            t_b1 = Em("vector", lambda v, tt=tt: v.bn_stats(
                st6[:, 0:6], xsrcs[tt][:, 0:512]), deps=[t_dx, sq_done], inc="v")
            t_b2 = Em("vector", lambda v, tt=tt: v.bn_stats(
                st6[:, 6:12], xsrcs[tt][:, 512:1024]), deps=[t_b1], inc="v")
            t_ag = Em("vector", lambda v, tt=tt: v.bn_aggr(
                mv2[:, 2 * tt:2 * tt + 2], st6[:]), deps=[t_b2], inc="v")
            sq_done = t_ag
            ags[tt] = t_ag
            t_e3 = Em("scalar", lambda sc, tt=tt: sc.activation(
                rcp16[:, tt:tt + 1], mv2[:, 2 * tt + 1:2 * tt + 2], AF.Ln, bias=epsv[:]),
                deps=[t_ag, t_eps], inc="s")
            e4s[tt] = Em("scalar", lambda sc, tt=tt: sc.activation(
                rstd16[:, tt:tt + 1], rcp16[:, tt:tt + 1], AF.Exp, scale=-0.5),
                deps=[t_e3], inc="s")
        if not LN_ONLY and tt >= 1:
            p0_apply(tt - 1)
    if LN_ONLY:
        return _finish(nc, P, taps, locals())
    emit_ready_vT(99)
    xs_last = max(t for t in xs_free if t is not None)
    t_wvT = wv_tok[-1]
    vT_toks = [vT_ev[oc][tb] for oc in range(DC) for tb in range(4)]
    if phase_limit <= 0:
        return _finish(nc, P, taps, locals())

    def prep_wT(w_hbm, slot, deps_extra):
        nonlocal wst_free
        done = None
        for rc in range(DC):
            b = rc % 2
            t_w = Em("gpsimd", lambda g, rc=rc, b=b, w_hbm=w_hbm: g.dma_start(
                out=wstage[b][:], in_=w_hbm[rc * 128:(rc + 1) * 128, :]),
                deps=[wst_free[b]] + deps_extra, inc="dw", amt=16)
            pb = rc % 2
            t_tr = None
            for dc in range(DC):
                t_tr = Em("tensor", lambda t, rc=rc, dc=dc, b=b, pb=pb: t.transpose(
                    PT[pb][:, dc * 128:(dc + 1) * 128],
                    wstage[b][:, dc * 128:(dc + 1) * 128], ident[:]),
                    deps=[t_w, pt_free[pb]], inc=("t" if dc == DC - 1 else None))
            wst_free[b] = t_tr
            done = Em("vector", lambda v, rc=rc, slot=slot, pb=pb: v.tensor_copy(
                WT[slot][:, :, rc * 128:(rc + 1) * 128],
                PT[pb].rearrange("p (c q) -> p c q", c=DC)), deps=[t_tr], inc="v")
            pt_free[pb] = done
        return done

    def load_nat(w_hbm, slot, deps_extra):
        return Em("gpsimd", lambda g, w_hbm=w_hbm, slot=slot: g.dma_start(
            out=WT[slot][:], in_=w_hbm.rearrange("(c p) k -> p c k", p=128)),
            deps=deps_extra, inc="dw", amt=16)

    def proj(dst, lhs_of, rhs_of, dep_lhs, dep_rhs, extra_mm_deps=()):
        toks = []
        g = 0
        for oc in range(DC):
            for tb in range(4):
                bank = g % 4
                t_mm = None
                for dc in range(DC):
                    t_mm = Em("tensor", lambda t, oc=oc, tb=tb, dc=dc, bank=bank: t.matmul(
                        PE[:, bank * 512:(bank + 1) * 512], lhs_of(dc, oc), rhs_of(dc, tb),
                        start=(dc == 0), stop=(dc == DC - 1)),
                        deps=[dep_lhs, dep_rhs, pe_free[bank]] + list(extra_mm_deps),
                        inc=("t" if dc == DC - 1 else None))
                t_ev = Em("vector", lambda v, oc=oc, tb=tb, bank=bank, dst=dst: v.tensor_copy(
                    dst[:, oc, tb * 512:(tb + 1) * 512], PE[:, bank * 512:(bank + 1) * 512]),
                    deps=[t_mm], inc="v")
                pe_free[bank] = t_ev
                toks.append(t_ev)
                g += 1
        return toks

    vT_last = max((t for row in vT_ev for t in row), key=lambda x: x[1])
    t_wbT = prep_wT(wb_h, 1, [])
    u_toks = []
    g = 0
    for tt in range(TT):
        for wh in range(2):
            bank = g % 4
            t_mm = None
            for vc in range(DC):
                t_mm = Em("tensor", lambda t, tt=tt, wh=wh, vc=vc, bank=bank: t.matmul(
                    PE[:, bank * 512:(bank + 1) * 512],
                    vT[:, vc, tt * 128:(tt + 1) * 128],
                    WT[1][:, vc, wh * 512:(wh + 1) * 512],
                    start=(vc == 0), stop=(vc == DC - 1)),
                    deps=[t_wbT, pe_free[bank]] + [vT_ev[oc][tt // 4] for oc in range(DC)],
                    inc=("t" if vc == DC - 1 else None))
            t_ev = Em("vector", lambda v, tt=tt, wh=wh, bank=bank: v.tensor_copy(
                u[:, tt, wh * 512:(wh + 1) * 512], PE[:, bank * 512:(bank + 1) * 512]),
                deps=[t_mm, xs_last], inc="v")
            pe_free[bank] = t_ev
            u_toks.append(t_ev)
            g += 1
    if phase_limit <= 2:
        return _finish(nc, P, taps, locals())

    t_wqT = prep_wT(wq_h, 0, [vT_last])
    qT_toks = proj(qT, lambda dc, oc: WT[0][:, dc, oc * 128:(oc + 1) * 128],
                   lambda dc, tb: xnT[:, dc, tb * 512:(tb + 1) * 512],
                   t_wqT, xnT_tok, extra_mm_deps=[u_toks[-1]])
    t_wa = load_nat(wa_h, 1, [u_toks[-1]])
    pT_toks = proj(pT, lambda qc, kb: WT[1][:, qc, kb * 128:(kb + 1) * 128],
                   lambda qc, tb: qT[:, qc, tb * 512:(tb + 1) * 512],
                   t_wa, qT_toks[-1])
    t_wkT = prep_wT(wk_h, 0, [qT_toks[-1]])
    kT_toks = proj(kT, lambda dc, oc: WT[0][:, dc, oc * 128:(oc + 1) * 128],
                   lambda dc, tb: xnT[:, dc, tb * 512:(tb + 1) * 512],
                   t_wkT, xnT_tok, extra_mm_deps=[pT_toks[-1]])
    if phase_limit <= 4:
        return _finish(nc, P, taps, locals())

    t_cn = Em("gpsimd", lambda g: g.dma_start(out=wstage[0][0:64, :], in_=cent_h),
              deps=[wst_free[0]], inc="dw", amt=16)
    t_ct = None
    for dc in range(DC):
        t_ct = Em("tensor", lambda t, dc=dc: t.transpose(
            PT[0][:, dc * 64:(dc + 1) * 64], wstage[0][0:64, dc * 128:(dc + 1) * 128],
            ident[0:64, 0:64]), deps=[t_cn, pt_free[0]], inc=("t" if dc == DC - 1 else None))
    t_centT = Em("vector", lambda v: v.tensor_copy(
        centT[:], PT[0][:, 0:512].rearrange("p (c e) -> p c e", c=DC)),
        deps=[t_ct], inc="v")
    pt_free[0] = t_centT
    wst_free[0] = t_ct
    t_bb = Em("tensor", lambda t: t.matmul(
        PS[:, 0:E], ones_sb[:], bias_sb[:], start=True, stop=True),
        deps=[t_ones, t_bias], inc="t")
    t_bbc = Em("vector", lambda v: v.tensor_copy(bias_bc[:], PS[:, 0:E]),
               deps=[t_bb], inc="v")
    t_io = Em("sync", lambda s: s.dma_start(out=iota_sb[:], in_=iota64_hbm), inc="dx", amt=16)
    t_if = Em("sync", lambda s: s.dma_start(out=identF[:], in_=identF_hbm), inc="dx", amt=16)
    t_be = Em("tensor", lambda t: t.matmul(
        PS[:, 512:512 + E], ones_sb[:], iota_sb[:], start=True, stop=True),
        deps=[t_ones, t_io], inc="t")
    t_erow = Em("vector", lambda v: v.tensor_copy(erow[:], PS[:, 512:512 + E]),
                deps=[t_be], inc="v")
    ps_free = [t_bbc, t_erow]
    prev_chain = None
    for tt in range(TT):
        pb = tt % 2
        t_mm = None
        for dc in range(DC):
            t_mm = Em("tensor", lambda t, tt=tt, dc=dc, pb=pb: t.matmul(
                PS[:, pb * 512:pb * 512 + E], xnT[:, dc, tt * 128:(tt + 1) * 128],
                centT[:, dc, :], start=(dc == 0), stop=(dc == DC - 1)),
                deps=[xnT_tile_toks[tt], t_centT, ps_free[pb]],
                inc=("t" if dc == DC - 1 else None))
        c1 = Em("vector", lambda v, pb=pb: v.tensor_copy(aff_sb[:], PS[:, pb * 512:pb * 512 + E]),
                deps=[t_mm, prev_chain], inc="v")
        ps_free[pb] = c1
        c2 = Em("vector", lambda v: v.tensor_add(biased[:], aff_sb[:], bias_bc[:]),
                deps=[c1, t_bbc], inc="v")
        c3 = Em("vector", lambda v: v.max(out=max8[:], in_=biased[:]), deps=[c2], inc="v")
        c4 = Em("vector", lambda v: v.match_replace(
            out=zap[:], in_to_replace=max8[:], in_values=biased[:], imm_value=NEG_BIG),
            deps=[c3], inc="v")
        c5 = Em("vector", lambda v: v.tensor_scalar(mask[:], zap[:], NEG_THR, None, ALU.is_le),
                deps=[c4], inc="v")
        c6 = Em("scalar", lambda sc: sc.activation(sig[:], aff_sb[:], AF.Sigmoid),
                deps=[c1, c5], inc="s")
        c7 = Em("vector", lambda v: v.tensor_mul(wm[:], sig[:], mask[:]),
                deps=[c6, c5], inc="v")
        c8 = Em("vector", lambda v: v.tensor_reduce(den1[:], wm[:], AX.X, ALU.add),
                deps=[c7], inc="v")
        c9 = Em("vector", lambda v: v.tensor_scalar(den1[:], den1[:], 1e-8, None, ALU.add),
                deps=[c8], inc="v")
        c10 = Em("vector", lambda v: v.reciprocal(rd1[:], den1[:]), deps=[c9], inc="v")
        c11 = Em("vector", lambda v, tt=tt: v.tensor_scalar(
            cw[:, tt * E:(tt + 1) * E], wm[:], rd1[:], None, ALU.mult),
            deps=[c10], inc="v")
        c12 = Em("vector", lambda v, tt=tt: v.tensor_copy(
            MKb[:, tt * E:(tt + 1) * E], mask[:]), deps=[c5], inc="v")
        pbm = tt % 2
        t_mt = Em("tensor", lambda t, tt=tt, pbm=pbm: t.transpose(
            PT[pbm][0:64, 0:128], MKb[:, tt * E:(tt + 1) * E], ident[:]),
            deps=[c12, pt_free[pbm]], inc="t")
        t_mte = Em("vector", lambda v, tt=tt, pbm=pbm: v.tensor_copy(
            MT[:, tt * 128:(tt + 1) * 128], PT[pbm][0:64, 0:128]),
            deps=[t_mt], inc="v")
        pt_free[pbm] = t_mte
        prev_chain = t_mte
    cw_done = prev_chain

    t_zr = Em("vector", lambda v: v.memset(ZROW[:], 0.0), deps=[cw_done], inc="v")
    cum_toks = []
    for h in range(NH):
        cum_toks.append(Em("vector", lambda v, h=h: v.tensor_tensor_scan(
            CUM[:, h * TH:(h + 1) * TH], MT[:, h * TH:(h + 1) * TH], ZROW[:],
            0.0, ALU.add, ALU.add), deps=[t_zr, cw_done], inc="v"))
    t_r1 = Em("vector", lambda v: v.memset(VALS[:], 1.0), deps=[cum_toks[-1]], inc="v")
    t_r2 = Em("vector", lambda v: v.tensor_tensor_scan(
        RAMP[:], VALS[:], ZROW[:], 0.0, ALU.add, ALU.add), deps=[t_r1], inc="v")
    cumsum_done = t_r2

    ext_state = {"h": -1, "r": 32, "src": 0, "prep": None, "last": cumsum_done}
    ext_bufs = None

    def emit_extract_some(n):
        for _ in range(n):
            st = ext_state
            if st["r"] >= CAPH // 8:
                if st["h"] >= NH - 1:
                    return
                st["h"] += 1
                h = st["h"]
                t_v1 = Em("vector", lambda v: v.tensor_scalar(
                    VALS[:], RAMP[:], -1.0, float(TH + 1), ALU.mult, ALU.add),
                    deps=[st["last"], cumsum_done], inc="v")
                t_v2 = Em("vector", lambda v, h=h: v.tensor_mul(
                    VALS[:], VALS[:], MT[:, h * TH:(h + 1) * TH]),
                    deps=[t_v1], inc="v")
                st["last"] = t_v2
                st["r"] = 0
                st["src"] = 0
            h, r = st["h"], st["r"]
            srcb = VALS if st["src"] == 0 else MT[:, h * TH:(h + 1) * TH]
            dstb = MT[:, h * TH:(h + 1) * TH] if st["src"] == 0 else VALS
            t_m = Em("vector", lambda v, srcb=srcb: v.max(out=EXV[:], in_=srcb),
                     deps=[st["last"]], inc="v")
            t_z = Em("vector", lambda v, srcb=srcb, dstb=dstb: v.match_replace(
                out=dstb, in_to_replace=EXV[:], in_values=srcb, imm_value=0.0),
                deps=[t_m], inc="v")
            t_i = Em("vector", lambda v, h=h, r=r: v.tensor_scalar(
                IDXL[:, h * CAPH + 8 * r:h * CAPH + 8 * r + 8], EXV[:],
                -1.0, float(TH + h * TH), ALU.mult, ALU.add),
                deps=[t_m], inc="v")
            t_i2 = Em("vector", lambda v, h=h, r=r: v.tensor_scalar_min(
                IDXL[:, h * CAPH + 8 * r:h * CAPH + 8 * r + 8],
                IDXL[:, h * CAPH + 8 * r:h * CAPH + 8 * r + 8], float(S - 1)),
                deps=[t_i], inc="v")
            st["last"] = t_z
            st["r"] += 1
            st["src"] ^= 1

    s_toks = []
    s_wr_toks = []
    sstg_free = [None, None]
    e_free = None
    ps_av_free = max(t for t in ps_free if t is not None)
    at_free = [pt_free[0], pt_free[1]]
    alpha_used = None
    nmp_free = None
    for it in range(TT):
        pm_toks = []
        for jb in range(4):
            t_mm = None
            for kb in range(DC):
                t_mm = Em("tensor", lambda t, it=it, jb=jb, kb=kb: t.matmul(
                    PE[:, jb * 512:(jb + 1) * 512],
                    pT[:, kb, it * 128:(it + 1) * 128],
                    kT[:, kb, jb * 512:(jb + 1) * 512],
                    start=(kb == 0), stop=(kb == DC - 1)),
                    deps=[pT_toks[-1], kT_toks[-1], e_free],
                    inc=("t" if kb == DC - 1 else None))
            pm_toks.append(Em("vector", lambda v, jb=jb: v.reduce_max(
                nmp[:, jb:jb + 1], PE[:, jb * 512:(jb + 1) * 512], AX.X),
                deps=[t_mm, nmp_free], inc="v"))
        t_nm = Em("vector", lambda v, it=it: v.tensor_reduce(
            negm16[:, it:it + 1], nmp[:], AX.X, ALU.max, negate=True),
            deps=pm_toks, inc="v")
        nmp_free = t_nm
        exp_toks = []
        for c in range(4):
            exp_toks.append(Em("scalar", lambda sc, it=it, c=c: sc.activation(
                alpha_sb[:, c * 512:(c + 1) * 512], PE[:, c * 512:(c + 1) * 512],
                AF.Exp, bias=negm16[:, it:it + 1]),
                deps=[t_nm, alpha_used if c == 0 else None], inc="s"))
        e_free = exp_toks[-1]
        at_evac = []
        for gb in range(2):
            t_tr = None
            for k in range(8):
                jb8 = gb * 8 + k
                t_tr = Em("tensor", lambda t, jb8=jb8, k=k, gb=gb: t.transpose(
                    PT[gb][:, k * 128:(k + 1) * 128],
                    alpha_sb[:, jb8 * 128:(jb8 + 1) * 128], ident[:]),
                    deps=[exp_toks[jb8 // 4], at_free[gb]], inc=("t" if k == 7 else None))
            t_ev = Em("vector", lambda v, gb=gb: v.tensor_copy(
                alphaT_sb[:, gb * 1024:(gb + 1) * 1024], PT[gb][:]),
                deps=[t_tr], inc="v")
            at_free[gb] = t_ev
            at_evac.append(t_ev)
        t_rs = Em("vector", lambda v, it=it: v.reduce_sum(
            rowsum16[:, it:it + 1], alpha_sb[:], AX.X), deps=[exp_toks[-1]], inc="v")
        t_rc = Em("vector", lambda v, it=it: v.reciprocal(
            recip16[:, it:it + 1], rowsum16[:, it:it + 1]), deps=[t_rs], inc="v")
        alpha_used = t_rs
        emit_extract_some(5)
        av_groups = []
        for wh in range(2):
            t_mm = None
            for j in range(16):
                t_mm = Em("tensor", lambda t, wh=wh, j=j: t.matmul(
                    PS[:, wh * 512:(wh + 1) * 512],
                    alphaT_sb[:, j * 128:(j + 1) * 128],
                    u[:, j, wh * 512:(wh + 1) * 512],
                    start=(j == 0), stop=(j == 15)),
                    deps=[at_evac[j // 8], u_toks[-1], ps_av_free],
                    inc=("t" if j == 15 else None))
            av_groups.append(t_mm)
        t_s = Em("vector", lambda v, it=it: v.tensor_scalar(
            sstg[it % 2][:], PS[:, 0:1024], recip16[:, it:it + 1], None, ALU.mult),
            deps=av_groups + [t_rc, sstg_free[it % 2]], inc="v")
        ps_av_free = t_s
        t_sw = Em("sync", lambda s, it=it: s.dma_start(
            out=sdr_h[it * 128:(it + 1) * 128, :], in_=sstg[it % 2][:]),
            deps=[t_s], inc="dy", amt=16)
        sstg_free[it % 2] = t_sw
        s_wr_toks.append(t_sw)
        s_toks.append(t_s)
    if phase_limit <= 5:
        return _finish(nc, P, taps, locals())

    emit_extract_some(200)
    ixr = ext_state["last"]
    ixt_toks = []
    ps6_free = [s_toks[-1], s_toks[-1]]
    for c in range(4):
        pb6 = c % 2
        t_tp = Em("tensor", lambda t, c=c, pb6=pb6: t.transpose(
            PS[0:128, pb6 * 512:pb6 * 512 + 64], IDXL[:, c * 128:(c + 1) * 128],
            identF[0:64, 0:64]), deps=[ixr, ps6_free[pb6], t_if], inc="t")
        t_ix = Em("vector", lambda v, c=c, pb6=pb6: v.tensor_copy(
            IXT[:, c * E:(c + 1) * E], PS[:, pb6 * 512:pb6 * 512 + 64]),
            deps=[t_tp], inc="v")
        ps6_free[pb6] = t_ix
        ixt_toks.append(t_ix)

    def acc_ap(g):
        return acc[:, g, :] if g < TTH else accG[:, g - TTH, :]

    t_wsu = Em("gpsimd", lambda g: g.dma_start(
        out=Wu_b[0][:], in_=wsup_h.rearrange("(c p) f -> p c f", p=128)),
        deps=[s_toks[-1]], inc="dw", amt=16)
    t_wsd = Em("gpsimd", lambda g: g.dma_start(
        out=Wd_b[0][:], in_=wsdn_h.rearrange("(c p) f -> p c f", p=128)),
        deps=[s_toks[-1]], inc="dw", amt=16)
    bank_aps = [PE[:, 0:512], PE[:, 512:1024], PE[:, 1024:1536],
                PS[:, 0:512], PS[:, 512:1024]]
    bank_free = [None, None, None, ps6_free[0], ps6_free[1]]
    ubr = [0]

    def next_bank():
        i = ubr[0] % 5
        ubr[0] += 1
        return i
    hsh_free = [alpha_used]
    shup_last = None
    sh_acc_toks = [None] * TT
    for h in range(NH):
        sil_toks = []
        for tb in range(2):
            for fc in range(FC):
                bank = next_bank()
                t_mm = None
                for dc in range(DC):
                    t_mm = Em("tensor", lambda t, h=h, tb=tb, fc=fc, dc=dc, bank=bank: t.matmul(
                        bank_aps[bank],
                        Wu_b[0][:, dc, fc * 128:(fc + 1) * 128],
                        xnT[:, dc, h * TH + tb * 512:h * TH + (tb + 1) * 512],
                        start=(dc == 0), stop=(dc == DC - 1)),
                        deps=[t_wsu, xnT_tok, bank_free[bank]],
                        inc=("t" if dc == DC - 1 else None))
                shup_last = t_mm
                t_h = Em("scalar", lambda sc, tb=tb, fc=fc, bank=bank: sc.activation(
                    Hsh[:, fc, tb * 512:(tb + 1) * 512],
                    bank_aps[bank], AF.Silu),
                    deps=[t_mm] + hsh_free, inc="s")
                bank_free[bank] = t_h
                sil_toks.append(t_h)
        last_dn = None
        for tl in range(TTH):
            for dh in range(2):
                bank = next_bank()
                t_mm = None
                for fc in range(FC):
                    t_mm = Em("tensor", lambda t, tl=tl, dh=dh, fc=fc, bank=bank: t.matmul(
                        bank_aps[bank],
                        Hsh[:, fc, tl * 128:(tl + 1) * 128],
                        Wd_b[0][:, fc, dh * 512:(dh + 1) * 512],
                        start=(fc == 0), stop=(fc == FC - 1)),
                        deps=[t_wsd, bank_free[bank]] + sil_toks,
                        inc=("t" if fc == FC - 1 else None))
                last_dn = t_mm
                t_ev = Em("vector", lambda v, h=h, tl=tl, dh=dh, bank=bank: v.tensor_copy(
                    acc_ap(h * TTH + tl)[:, dh * 512:(dh + 1) * 512],
                    bank_aps[bank]),
                    deps=[t_mm, s_toks[-1]], inc="v")
                bank_free[bank] = t_ev
                sh_acc_toks[h * TTH + tl] = t_ev
        hsh_free = [last_dn]
    shdn_last = hsh_free[0]

    pe3_free = [s_toks[-1]]
    flu_toks = [None] * TT
    wj_toks = [None] * TT
    p2_next = [0]

    def emit_pass2_tile():
        tt = p2_next[0]
        if tt >= TT:
            return
        p2_next[0] += 1
        sec = float((CAPH if tt >= TTH else 0) - 1)
        t_tp = Em("tensor", lambda t, tt=tt: t.transpose(
            PE[0:128, 1536:1536 + 64], CUM[:, tt * 128:(tt + 1) * 128],
            identF[0:64, 0:64]), deps=[cumsum_done, pe3_free[0], t_if], inc="t")
        t_pe = Em("vector", lambda v: v.tensor_copy(
            aff_sb[:], PE[:, 1536:1536 + 64]), deps=[t_tp], inc="v")
        pe3_free[0] = t_pe
        s1 = Em("vector", lambda v: v.scalar_tensor_tensor(
            zap[:], erow[:], float(CAP), aff_sb[:], ALU.mult, ALU.add),
            deps=[t_pe, t_erow], inc="v")
        s2 = Em("vector", lambda v, sec=sec: v.tensor_scalar(
            zap[:], zap[:], sec, None, ALU.add), deps=[s1], inc="v")
        s3 = Em("vector", lambda v: v.tensor_scalar(
            mask[:], zap[:], -1.0, BIGK, ALU.mult, ALU.add), deps=[s2], inc="v")
        s4 = Em("vector", lambda v, tt=tt: v.tensor_mul(
            mask[:], mask[:], MKb[:, tt * E:(tt + 1) * E]), deps=[s3], inc="v")
        s5 = Em("vector", lambda v: v.max(out=max8[:], in_=mask[:]), deps=[s4], inc="v")
        s6 = Em("vector", lambda v: v.tensor_scalar(
            sig[:, 0:8], max8[:], -1.0, BIGK, ALU.mult, ALU.add), deps=[s5], inc="v")
        flu_toks[tt] = Em("vector", lambda v, tt=tt: v.tensor_copy(
            FLU[:, tt * 8:(tt + 1) * 8], sig[:, 0:8]), deps=[s6], inc="v")
        prev = flu_toks[tt]
        for j in range(8):
            o1 = Em("vector", lambda v, j=j: v.tensor_scalar(
                wm[:], mask[:], max8[:, j:j + 1], None, ALU.is_equal),
                deps=[prev], inc="v")
            o2 = Em("vector", lambda v, tt=tt: v.tensor_mul(
                wm[:], wm[:], cw[:, tt * E:(tt + 1) * E]), deps=[o1], inc="v")
            prev = Em("vector", lambda v, tt=tt, j=j: v.reduce_sum(
                WJ[:, tt * 8 + j:tt * 8 + j + 1], wm[:], AX.X), deps=[o2], inc="v")
        wj_toks[tt] = prev

    pt6_free = [pt_free[0], pt_free[1]]
    xg_free = [[shup_last], [shup_last]]
    xgt_free = [[shup_last], [shup_last]]
    hg_free = [shdn_last]
    ygst_free = [None, None]
    wu_free = [shup_last, s_toks[-1]]
    wd_free = [shdn_last, s_toks[-1]]
    yg_last = None
    def expert_dma(e):
        par = e % 2
        t_wu = Em("gpsimd", lambda g, e=e, par=par: g.dma_start(
            out=Wu_b[par][:], in_=wup_h[e].rearrange("(c p) f -> p c f", p=128)),
            deps=[wu_free[par]], inc="dw", amt=16)
        t_wd = Em("gpsimd", lambda g, e=e, par=par: g.dma_start(
            out=Wd_b[par][:], in_=wdn_h[e].rearrange("(c p) f -> p c f", p=128)),
            deps=[wd_free[par]], inc="dw", amt=16)
        gath = []
        for c in range(4):
            gath.append(Em("gpsimd", lambda g, e=e, c=c, par=par: g.indirect_dma_start(
                out=XG[par][:, c, :], out_offset=None, in_=xnd_h,
                in_offset=IndirectOffsetOnAxis(IXT[:, c * E + e:c * E + e + 1], 0)),
                deps=[ixt_toks[c], xnd_toks[-1]] + xg_free[par], inc="dw", amt=16))
        xg_free[par] = []
        return t_wu, t_wd, gath

    def expert_transpose(e, gath):
        par = e % 2
        xgt_wr = []
        for c in range(4):
            pb6 = c % 2
            t_tr = None
            for dj in range(DC):
                t_tr = Em("tensor", lambda t, c=c, dj=dj, par=par, pb6=pb6: t.transpose(
                    PT[pb6][:, dj * 128:(dj + 1) * 128],
                    XG[par][:, c, dj * 128:(dj + 1) * 128], ident[:]),
                    deps=[gath[c], pt6_free[pb6]] + xgt_free[par],
                    inc=("t" if dj == DC - 1 else None))
            xg_free[par].append(t_tr)
            t_xe = Em("vector", lambda v, c=c, par=par, pb6=pb6: v.tensor_copy(
                XGT[par][:, :, c * 128:(c + 1) * 128],
                PT[pb6].rearrange("p (dj q) -> p dj q", dj=DC)),
                deps=[t_tr], inc="v")
            pt6_free[pb6] = t_xe
            xgt_wr.append(t_xe)
        xgt_free[par] = []
        return xgt_wr

    dmas = {0: expert_dma(0)}
    xgt_ready = {0: expert_transpose(0, dmas[0][2])}
    for e in range(E):
        par = e % 2
        t_wu, t_wd, _g = dmas.pop(e)
        xgt_wr = xgt_ready.pop(e)
        if e + 1 < E:
            dmas[e + 1] = expert_dma(e + 1)
        sil_toks = []
        for fb in range(FC):
            bank = next_bank()
            t_mm = None
            for dc in range(DC):
                t_mm = Em("tensor", lambda t, fb=fb, dc=dc, par=par, bank=bank: t.matmul(
                    bank_aps[bank],
                    Wu_b[par][:, dc, fb * 128:(fb + 1) * 128],
                    XGT[par][:, dc, :],
                    start=(dc == 0), stop=(dc == DC - 1)),
                    deps=[t_wu, bank_free[bank]] + xgt_wr,
                    inc=("t" if dc == DC - 1 else None))
            xgt_free[par].append(t_mm)
            t_h = Em("scalar", lambda sc, fb=fb, bank=bank: sc.activation(
                Hg[:, fb, :], bank_aps[bank], AF.Silu),
                deps=[t_mm] + hg_free, inc="s")
            bank_free[bank] = t_h
            sil_toks.append(t_h)
        wu_free[par] = t_mm
        if e + 1 < E:
            xgt_ready[e + 1] = expert_transpose(e + 1, dmas[e + 1][2])
        last_dn = None
        pair_evs = []
        for sb in range(4):
            for dh in range(2):
                bank = next_bank()
                t_mm = None
                for fb in range(FC):
                    t_mm = Em("tensor", lambda t, sb=sb, dh=dh, fb=fb, par=par, bank=bank: t.matmul(
                        bank_aps[bank],
                        Hg[:, fb, sb * 128:(sb + 1) * 128],
                        Wd_b[par][:, fb, dh * 512:(dh + 1) * 512],
                        start=(fb == 0), stop=(fb == FC - 1)),
                        deps=[t_wd, bank_free[bank]] + sil_toks,
                        inc=("t" if fb == FC - 1 else None))
                last_dn = t_mm
                buf = (sb // 2) % 2
                t_ev = Em("vector", lambda v, sb=sb, dh=dh, bank=bank, buf=buf: v.tensor_copy(
                    Ygst[buf][:, sb % 2, dh * 512:(dh + 1) * 512],
                    bank_aps[bank]),
                    deps=[t_mm, ygst_free[buf]], inc="v")
                bank_free[bank] = t_ev
                pair_evs.append(t_ev)
                if sb % 2 == 1 and dh == 1:
                    pair = sb // 2
                    yg_last = Em("sync", lambda s, e=e, pair=pair, buf=buf: s.dma_start(
                        out=ygd_h[e * CAP + pair * 256:e * CAP + (pair + 1) * 256, :]
                            .rearrange("(c p) d -> p c d", p=128),
                        in_=Ygst[buf][:]),
                        deps=list(pair_evs[-4:]), inc="dy", amt=16)
                    ygst_free[buf] = yg_last
        wd_free[par] = last_dn
        hg_free = [last_dn]
        if e % 4 == 0:
            emit_pass2_tile()
    while p2_next[0] < TT:
        emit_pass2_tile()

    if phase_limit <= 6:
        return _finish(nc, P, taps, locals())
    xstf = [at(f"xstf{i}", i * 4096, [128, D], F32) for i in range(2)]
    gacc_free = [[yg_last], [yg_last], [yg_last], [yg_last]]
    xs6_free = [list(xg_free[0]), list(xg_free[0])]
    os_free = [[yg_last], [yg_last], [yg_last]]
    gsil_free = [hg_free, hg_free]
    out_dma_toks = []
    for tt in range(TT):
        prev_stt = sh_acc_toks[tt]
        for j in range(8):
            gb6 = (tt * 8 + j) % 4
            t_ga = Em("gpsimd", lambda g, tt=tt, j=j, gb6=gb6: g.indirect_dma_start(
                out=gacc[gb6][:], out_offset=None, in_=ygd_h,
                in_offset=IndirectOffsetOnAxis(FLU[:, tt * 8 + j:tt * 8 + j + 1], 0)),
                deps=[flu_toks[tt], yg_last] + gacc_free[gb6], inc="dw", amt=16)
            prev_stt = Em("vector", lambda v, tt=tt, j=j, gb6=gb6: v.scalar_tensor_tensor(
                acc_ap(tt)[:], gacc[gb6][:], WJ[:, tt * 8 + j:tt * 8 + j + 1],
                acc_ap(tt)[:], ALU.mult, ALU.add),
                deps=[t_ga, wj_toks[tt], prev_stt], inc="v")
            gacc_free[gb6] = [prev_stt]
        if phase_limit <= 7:
            continue
        t_g = Em("scalar", lambda sc, tt=tt: sc.activation(
            gsil[tt % 2][:], acc_ap(tt)[:], AF.Silu),
            deps=[prev_stt] + gsil_free[tt % 2], inc="s")
        b = tt % 2
        t_dx = Em("sync", lambda s, tt=tt, b=b: s.dma_start(
            out=xstf[b][:], in_=x_h[tt * 128:(tt + 1) * 128, :]),
            deps=xs6_free[b] if isinstance(xs6_free[b], list) else [xs6_free[b]],
            inc="dx", amt=16)
        t_sl = Em("sync", lambda s, tt=tt: s.dma_start(
            out=sstg[tt % 2][:], in_=sdr_h[tt * 128:(tt + 1) * 128, :]),
            deps=[s_wr_toks[tt]] + sstg_free[tt % 2] if isinstance(sstg_free[tt % 2], list)
                 else [s_wr_toks[tt], sstg_free[tt % 2]],
            inc="dy", amt=16)
        ob = tt % 3
        t_m1 = Em("vector", lambda v, tt=tt, ob=ob: v.tensor_mul(
            ostage[ob][:], sstg[tt % 2][:], gsil[tt % 2][:]),
            deps=[t_g, t_sl] + os_free[ob], inc="v")
        sstg_free[tt % 2] = [t_m1]
        gsil_free[tt % 2] = [t_m1]
        t_m2 = Em("vector", lambda v, b=b, ob=ob: v.tensor_add(
            ostage[ob][:], ostage[ob][:], xstf[b][:]),
            deps=[t_m1, t_dx], inc="v")
        xs6_free[b] = [t_m2]
        t_o = Em("sync", lambda s, tt=tt, ob=ob: s.dma_start(
            out=y_h[tt * 128:(tt + 1) * 128, :], in_=ostage[ob][:]),
            deps=[t_m2], inc="dx", amt=16)
        os_free[ob] = [t_o]
        out_dma_toks.append(t_o)

    if phase_limit <= 7:
        Em("sync", lambda s: None, deps=P.final_deps())
    else:
        Em("sync", lambda s: None, deps=[out_dma_toks[-1], yg_last])
    return _finish(nc, P, taps, locals())


def _finish(nc, P, taps, env):
    tap_outs = []
    for name, key in taps:
        t = env[key]
        shape = list(t.shape)
        dt = t.tensor.dtype if hasattr(t, "tensor") else t.dtype
        o = nc.dram_tensor(name, shape, dt, kind="ExternalOutput").ap()
        P.emit("sync", lambda s, o=o, t=t: s.dma_start(out=o, in_=t[:]),
               deps=P.final_deps(), inc="dx", amt=16)
        tap_outs.append(name)
    if tap_outs:
        P.emit("sync", lambda s: None, deps=P.final_deps())
    with ExitStack() as ctx:
        sems = {n: ctx.enter_context(nc.semaphore(n)) for n in P.sem_names}
        block = ctx.enter_context(nc.Block())
        P.build(block, sems)
    return nc


_NC_CACHE = {}


def _get_nc(phase_limit=99, taps=()):
    key = (phase_limit, tuple(taps))
    if key not in _NC_CACHE:
        _NC_CACHE[key] = build_nc(phase_limit, taps)
    return _NC_CACHE[key]


def kernel(L0_attn_x, L0_attn_Wk, L0_attn_Wq, L0_attn_Wv, L0_attn_Wa, L0_attn_Wb,
           L0_ffn_centroids, L0_ffn_expert_biases, L0_ffn_Ws_up, L0_ffn_Ws_down,
           L0_ffn_W_up, L0_ffn_W_down, _phase_limit=99, _taps=()):
    nc = _get_nc(_phase_limit, _taps)
    common = dict(
        wk=np.ascontiguousarray(L0_attn_Wk), wq=np.ascontiguousarray(L0_attn_Wq),
        wv=np.ascontiguousarray(L0_attn_Wv), wa=np.ascontiguousarray(L0_attn_Wa),
        wb=np.ascontiguousarray(L0_attn_Wb),
        cent=np.ascontiguousarray(L0_ffn_centroids),
        ebias=np.ascontiguousarray(np.reshape(L0_ffn_expert_biases, (1, E))),
        wsup=np.ascontiguousarray(L0_ffn_Ws_up), wsdn=np.ascontiguousarray(L0_ffn_Ws_down),
        wup=np.ascontiguousarray(L0_ffn_W_up), wdn=np.ascontiguousarray(L0_ffn_W_down),
    )
    x = np.ascontiguousarray(L0_attn_x)
    in_maps = [dict(common, x=np.ascontiguousarray(x[c])) for c in range(8)]
    res = run_bass_kernel_spmd(nc, in_maps, core_ids=list(range(8)))
    if _taps or _phase_limit < 99:
        return res
    return np.stack([r["y"] for r in res.results], axis=0)
